# version 02
# baseline (speedup 1.0000x reference)
import contextlib
import numpy as np
import ml_dtypes
import concourse.bass as bass
import concourse.mybir as mybir
from concourse.bass_utils import run_bass_kernel_spmd

F32 = mybir.dt.float32
BF16 = mybir.dt.bfloat16
E4 = mybir.dt.float8e4
AF = mybir.ActivationFunctionType
ALU = mybir.AluOpType

T, D, H = 2048, 2048, 8192
NS, RNK = 16, 128
TC = 256
NCH = T // TC
DCN = D // 128
HCN = H // 128
NSUP = 32
NDJ = 16
NDBLK = 32
NP = 8
SEPW = 257
EPS = 1e-5
C1 = float(np.float32(ml_dtypes.bfloat16(0.044715)))
C2 = float(np.float32(ml_dtypes.bfloat16(np.sqrt(2.0 / np.pi))))
INVD = 1.0 / D
NG = 16


class Sem:
    def __init__(self, es, nc, name):
        self.h = es.enter_context(nc.semaphore(name))
        self.n = 0

    def inc(self, inst, amt=1):
        inst.then_inc(self.h, amt)
        self.n += amt
        return inst


class Bank:
    def __init__(self, es, nc, name):
        self.ps = es.enter_context(nc.psum_tensor(name, [128, 512], F32))
        self.free_tok = None

    def freed_by(self, sem):
        self.free_tok = (sem, sem.n)

    def wait_free(self, eng):
        if self.free_tok is not None:
            sem, n = self.free_tok
            eng.wait_ge(sem.h, n)
            self.free_tok = None


class Builder:
    def __init__(self):
        self.nc = bass.Bass("TRN2", target_bir_lowering=False,
                            detect_race_conditions=False)
        self.es = contextlib.ExitStack()
        self._out_done = {}
        self._x1done = {}
        self._ffn_pe_end = {}
        self._w01_done = {}
        self._x3q_tok = {}

    def sect(self, eng, fn):
        getattr(self.block, eng)(fn)

    def run_items(self, *lists):
        lists = [l for l in lists if l]
        if not lists:
            return
        main = max(lists, key=len)
        rest = [l for l in lists if l is not main]
        n = len(main)
        idxs = [0] * len(rest)
        for k, (eng, fn) in enumerate(main):
            self.sect(eng, fn)
            for li, l in enumerate(rest):
                want = (k + 1) * len(l) // n
                while idxs[li] < want:
                    e2, f2 = l[idxs[li]]
                    self.sect(e2, f2)
                    idxs[li] += 1
        for li, l in enumerate(rest):
            while idxs[li] < len(l):
                e2, f2 = l[idxs[li]]
                self.sect(e2, f2)
                idxs[li] += 1

    def build(self):
        nc, es = self.nc, self.es
        self.x_in = nc.dram_tensor("x", [T, D], F32, kind="ExternalInput").ap()
        self.conv_w = nc.dram_tensor("conv_w", [D, 1, 4], F32, kind="ExternalInput").ap()
        self.conv_b = nc.dram_tensor("conv_b", [D], F32, kind="ExternalInput").ap()
        self.xproj_w = nc.dram_tensor("xproj_w", [160, D], F32, kind="ExternalInput").ap()
        self.dtproj_w = nc.dram_tensor("dtproj_w", [D, RNK], F32, kind="ExternalInput").ap()
        self.dtproj_b = nc.dram_tensor("dtproj_b", [D], F32, kind="ExternalInput").ap()
        self.A_permin = nc.dram_tensor("A_perm_log", [128, 256], F32, kind="ExternalInput").ap()
        self.D_param = nc.dram_tensor("D_param", [D], F32, kind="ExternalInput").ap()
        self.wg = nc.dram_tensor("w_gating", [2, D, H], F32, kind="ExternalInput").ap()
        self.wl = nc.dram_tensor("w_linear", [H, D], F32, kind="ExternalInput").ap()
        self.ident = nc.dram_tensor("ident", [128, 128], F32, kind="ExternalInput").ap()
        self.ones16 = nc.dram_tensor("ones16", [128, 128], BF16, kind="ExternalInput").ap()
        self.r_even = nc.dram_tensor("r_even", [128, 128], BF16, kind="ExternalInput").ap()
        self.r_odd = nc.dram_tensor("r_odd", [128, 128], BF16, kind="ExternalInput").ap()
        self.fconst = nc.dram_tensor("fconst", [128, 8], F32, kind="ExternalInput").ap()
        self.out = nc.dram_tensor("out", [T, D], F32, kind="ExternalOutput").ap()
        self.w01s = nc.dram_tensor("w01s", [2, NSUP, 16, 128, 256], E4).ap()
        self.w2s = nc.dram_tensor("w2s", [NDJ, HCN, 128, 128], E4).ap()
        self.xproj16 = nc.dram_tensor("xproj16", [160, D], BF16).ap()
        self.dtproj16 = nc.dram_tensor("dtproj16", [D, RNK], BF16).ap()

        sb = lambda name, shape, dt: es.enter_context(nc.sbuf_tensor(name, shape, dt))
        self.xbuf = [sb(f"xbuf{i}", [128, DCN, TC], F32) for i in range(3)]
        self.xio = sb("xio", [128, 2048], F32)
        self.x1q = sb("x1q", [128, DCN, TC], E4)
        self.x3q = [sb(f"x3q{i}", [128, DCN, TC], E4) for i in range(2)]
        _gf = sb("gf16", [128, 2, 512], BF16)
        self.gf16 = [_gf, _gf]
        self.ch_x2 = sb("ch_x2", [128, 2, 256], BF16)
        self.ch_t = sb("ch_t", [128, 2, 256], BF16)
        self.ch_cdf = sb("ch_cdf", [128, 2, 256], BF16)
        self.ch_gv = sb("ch_gv", [128, 2, 256], BF16)
        self.ch_a16 = self.ch_t
        self.ch_x3 = self.ch_cdf
        self.ch_u = self.ch_gv
        self.act = [sb(f"act{c}", [128, HCN, TC], E4) for c in range(2)]
        self.w01b = sb("w01b", [128, 2, 2, 16, 256], E4)
        self.w2b = sb("w2b", [128, HCN // 2, 128], E4)
        self.out16 = sb("out16", [128, TC], BF16)
        self.lnscr = {}
        for fam in ("F", "M"):
            d = dict(
                q=sb(f"q_{fam}", [128, 256], F32),
                qh=sb(f"qh_{fam}", [128, 256], BF16),
                ql=sb(f"ql_{fam}", [128, 256], BF16),
                xh=sb(f"xh_{fam}", [128, 256], BF16),
                xl=sb(f"xl_{fam}", [128, 256], BF16),
                m=sb(f"m_{fam}", [128, TC], F32),
                msq=sb(f"msq_{fam}", [128, TC], F32),
                r=sb(f"r_{fam}", [128, TC], F32),
            )
            d["var"] = d["msq"]
            d["xm"] = d["q"]
            self.lnscr[fam] = d
        self.m2_t = sb("m2_t", [128, TC], F32)
        self.r2_t = sb("r2_t", [128, TC], F32)
        self.xh2 = sb("xh2", [128, DCN, TC + 4], BF16)
        self.xcb = sb("xcb", [128, TC], F32)
        self.th_s = sb("th_s", [128, TC], F32)
        self.u_s = self.lnscr["M"]["q"]
        self.xconv = sb("xconv", [128, DCN, TC], BF16)
        self.Bbc = self.xconv[:, 0:NP, :]
        self.Cbc = self.xconv[:, NP:2 * NP, :]
        self.dt16 = sb("dt16", [128, TC], BF16)
        self.bc16 = sb("bc16", [128, TC], BF16)
        self.delta = sb("delta", [128, DCN, TC], F32)
        self.wtd = sb("wtd", [128, DCN, TC], BF16)
        self.z_s = self.xcb
        self.az_s = self.u_s
        self.e_s = self.th_s
        self.ytmp = self.xcb
        self.ytmp2 = self.th_s
        self.ddn = [sb(f"ddn{i}", [128, TC], F32) for i in range(2)]
        self.wdn = [sb(f"wdn{i}", [128, TC], BF16) for i in range(2)]
        self.a_g = sb("a_g", [128, NP, SEPW], BF16)
        self.b_g = sb("b_g", [128, NP, SEPW], BF16)
        self.h_g = self.b_g
        self.h_state = sb("h_state", [128, 256], F32)
        self.A_perm = sb("A_perm", [128, 256], F32)
        self.fconst_s = sb("fconst_s", [128, 8], F32)
        self.id_s = sb("id_s", [128, 128], F32)
        self.ones_s = sb("ones_s", [128, 128], BF16)
        self.re_s = sb("re_s", [128, 128], BF16)
        self.ro_s = sb("ro_s", [128, 128], BF16)
        self.convw_s = sb("convw_s", [128, DCN, 4], F32)
        self.convb_s = sb("convb_s", [128, DCN], F32)
        self.dtb_s = sb("dtb_s", [128, DCN], F32)
        self.Dp_s = sb("Dp_s", [128, DCN], F32)
        self.xproj_s = sb("xproj_s", [128, DCN, 160], BF16)
        self.dtproj_s = sb("dtproj_s", [128, DCN, 128], BF16)

        for j, val in enumerate([EPS, 1.0, -0.5, -1.0, 0.5, INVD]):
            nc.const_aps.register(F32, val, self.fconst_s[:, j:j + 1])

        self.bk = [Bank(es, nc, f"bk{i}") for i in range(8)]

        self.dg0 = Sem(es, nc, "dg0")
        self.dgwA = Sem(es, nc, "dgwA")
        self.dgwB = Sem(es, nc, "dgwB")
        self.dgw2 = Sem(es, nc, "dgw2")
        self.dsbc = Sem(es, nc, "dsbc")
        self.dse = Sem(es, nc, "dse")
        self.dso = Sem(es, nc, "dso")
        self.dsx = Sem(es, nc, "dsx")
        self.dsoo = Sem(es, nc, "dsoo")
        self.dsw = Sem(es, nc, "dsw")
        self.pef = Sem(es, nc, "pef")
        self.acf = Sem(es, nc, "acf")
        self.dvf = Sem(es, nc, "dvf")
        self.pem = Sem(es, nc, "pem")
        self.acm = Sem(es, nc, "acm")
        self.dvm = Sem(es, nc, "dvm")
        self.F = (self.pef, self.acf, self.dvf)
        self.M = (self.pem, self.acm, self.dvm)

        self.block = es.enter_context(nc.Block())
        self.emit()
        self.finalize()
        return nc

    def emit(self):
        self.phase0()
        for s in range(NCH + 2):
            if s < NCH:
                self.phase_xload(s)
                self.phase_ln1(s)
            ffn_it = self.ffn_items(s)
            mam_it = self.mamba_items(s - 1) if 1 <= s <= NCH else []
            if s >= NCH - 1:
                for it in mam_it:
                    self.sect(*it)
                for it in ffn_it:
                    self.sect(*it)
            else:
                self.run_items(ffn_it, mam_it)
            if mam_it:
                got = (self.pem.n, self.acm.n, self.dvm.n)
                assert got == self._expect_m, (s, got, self._expect_m)
            if s >= 2:
                self.phase_out(s - 2)

    def phase0(self):
        dg0, dsw, acs, dv = self.dg0, self.dsw, self.acf, self.dvf

        def g_loads(g):
            for W in range(2):
                for ms in range(NSUP):
                    src = self.wg[W, :, ms * 256:(ms + 1) * 256] \
                        .rearrange("(kt kp) m -> kt kp m", kp=128)
                    dg0.inc(g.dma_start(self.w01s[W, ms], src), 16)
            for dj in range(NDJ):
                src = self.wl[:, dj * 128:(dj + 1) * 128] \
                    .rearrange("(kh hp) d -> kh hp d", hp=128)
                dg0.inc(g.dma_start(self.w2s[dj], src), 16)
            dg0.inc(g.dma_start(self.xproj16, self.xproj_w), 16)
            dg0.inc(g.dma_start(self.dtproj16, self.dtproj_w), 16)
            dg0.inc(g.dma_start(self.convw_s[:],
                                self.conv_w.squeeze(1).rearrange("(dc p) j -> p dc j", p=128)), 16)
            dg0.inc(g.dma_start(self.convb_s[:],
                                self.conv_b.rearrange("(dc p) -> p dc", p=128),
                                allow_slow_non_contiguous=True), 16)
            dg0.inc(g.dma_start(self.dtb_s[:],
                                self.dtproj_b.rearrange("(dc p) -> p dc", p=128),
                                allow_slow_non_contiguous=True), 16)
            dg0.inc(g.dma_start(self.Dp_s[:],
                                self.D_param.rearrange("(dc p) -> p dc", p=128),
                                allow_slow_non_contiguous=True), 16)
            dg0.inc(g.dma_start(self.A_perm[:], self.A_permin), 16)
            dg0.inc(g.dma_start(self.fconst_s[:], self.fconst), 16)
            dg0.inc(g.dma_start(self.id_s[:], self.ident), 16)
            dg0.inc(g.dma_start(self.ones_s[:], self.ones16), 16)
            dg0.inc(g.dma_start(self.re_s[:], self.r_even), 16)
            dg0.inc(g.dma_start(self.ro_s[:], self.r_odd), 16)

        self.sect("gpsimd", g_loads)
        self._dg0_all = dg0.n

        def s_loads(sy):
            sy.wait_ge(dg0.h, self._dg0_all)
            for kt in range(DCN):
                dsw.inc(sy.dma_start(self.xproj_s[:, kt, :],
                                     self.xproj16[:, kt * 128:(kt + 1) * 128],
                                     transpose=True), 16)
            for dc in range(DCN):
                dsw.inc(sy.dma_start(self.dtproj_s[:, dc, :],
                                     self.dtproj16[dc * 128:(dc + 1) * 128, :],
                                     transpose=True), 16)

        self.sect("sync", s_loads)
        self._projw_tok = dsw.n

        def s_negexp(sc):
            sc.wait_ge(dg0.h, self._dg0_all)
            acs.inc(sc.activation(self.A_perm[:], self.A_perm[:], AF.Exp))

        self.sect("scalar", s_negexp)
        acs_tok = acs.n

        def v_init(v):
            v.wait_ge(acs.h, acs_tok)
            v.tensor_scalar_mul(self.A_perm[:], self.A_perm[:], -1.0)
            v.memset(self.h_state[:], 0.0)
            v.memset(self.a_g[:].rearrange("p a t -> p (a t)"), 0.0)
            dv.inc(v.memset(self.xh2[:].rearrange("p a t -> p (a t)"), 0.0))

        self.sect("vector", v_init)
        self._init_dv = dv.n

    def phase_xload(self, s):
        ds, pe, dv = self.dsx, self.pef, self.dvf
        xb = self.xbuf[s % 3]
        ds_base = ds.n
        pe_base = pe.n
        dv_all = dv.n

        def s_load(sy):
            if s >= 3:
                sy.wait_ge(self.dsoo.h, self._out_done[s - 3])
            if s >= 1:
                sy.wait_ge(pe.h, pe_base)
            ds.inc(sy.dma_start(self.xio[:],
                                self.x_in[s * TC: s * TC + 128, :]), 16)
            sy.wait_ge(pe.h, pe_base + 4)
            ds.inc(sy.dma_start(self.xio[:],
                                self.x_in[s * TC + 128: s * TC + 256, :]), 16)

        self.sect("sync", s_load)

        def t_tr(t):
            t.wait_ge(dv.h, dv_all)
            if s == 0:
                t.wait_ge(dv.h, self._init_dv)
            k = 0
            for th in range(2):
                t.wait_ge(ds.h, ds_base + 16 * (th + 1))
                for dcg in range(4):
                    k += 1
                    b = self.bk[4 + (th * 4 + dcg) % 2]
                    if k <= 2:
                        b.wait_free(t)
                    else:
                        t.wait_ge(dv.h, dv_all + (k - 2))
                    for j in range(4):
                        dc = dcg * 4 + j
                        ins = t.transpose(b.ps[:, j * 128:(j + 1) * 128],
                                          self.xio[:, dc * 128:(dc + 1) * 128],
                                          self.id_s[:])
                    pe.inc(ins)

        self.sect("tensor", t_tr)

        def v_evac(v):
            k = 0
            for th in range(2):
                for dcg in range(4):
                    k += 1
                    b = self.bk[4 + (th * 4 + dcg) % 2]
                    v.wait_ge(pe.h, pe_base + k)
                    ins = v.tensor_copy(
                        xb[:, dcg * 4:(dcg + 1) * 4, th * 128:(th + 1) * 128],
                        b.ps[:].rearrange("p (a t) -> p a t", a=4))
                    dv.inc(ins)
                    b.freed_by(dv)

        self.sect("vector", v_evac)

    def _ln_stats_items(self, xb, fam, banks, pe_b, acs_b, dv_b, x_tok=None,
                        with_m2=False):
        pe, acs, dv = fam
        bm, bq = banks
        sc_ = self.lnscr["F" if fam is self.F else "M"]
        items = []
        for g in range(NG):
            def s_sq(sc, g=g):
                if g == 0 and x_tok is not None:
                    sem, n = x_tok
                    sc.wait_ge(sem.h, n)
                if g >= 1:
                    sc.wait_ge(pe.h, pe_b + g)
                    sc.wait_ge(dv.h, dv_b + g)
                xv = xb[:, g, :]
                sc.activation(sc_["q"][:], xv, AF.Square)
                acs.inc(sc.activation(sc_["qh"][:], sc_["q"][:], AF.Copy))
            items.append(("scalar", s_sq))

            def v_hl(v, g=g):
                if g >= 1:
                    v.wait_ge(pe.h, pe_b + g)
                xv = xb[:, g, :]
                v.wait_ge(acs.h, acs_b + g + 1)
                v.scalar_tensor_tensor(sc_["ql"][:], sc_["qh"][:], -1.0, sc_["q"][:],
                                       ALU.mult, ALU.add)
                v.tensor_copy(sc_["xh"][:], xv)
                dv.inc(v.scalar_tensor_tensor(sc_["xl"][:], sc_["xh"][:], -1.0, xv,
                                              ALU.mult, ALU.add))
            items.append(("vector", v_hl))

            def t_mm(t, g=g):
                if g == 0:
                    bm.wait_free(t)
                    bq.wait_free(t)
                t.wait_ge(dv.h, dv_b + g + 1)
                t.matmul(bm.ps[:, 0:256], self.ones_s[:], sc_["xh"][:],
                         start=(g == 0), stop=False)
                t.matmul(bm.ps[:, 0:256], self.ones_s[:], sc_["xl"][:],
                         start=False, stop=(g == NG - 1))
                t.matmul(bq.ps[:, 0:256], self.ones_s[:], sc_["qh"][:],
                         start=(g == 0), stop=False)
                ins = t.matmul(bq.ps[:, 0:256], self.ones_s[:], sc_["ql"][:],
                               start=False, stop=(g == NG - 1))
                pe.inc(ins)
            items.append(("tensor", t_mm))

        def s_stats(sc):
            sc.wait_ge(pe.h, pe_b + NG)
            sc.activation(sc_["m"][:], bm.ps[:, 0:256], AF.Copy, scale=INVD)
            acs.inc(sc.activation(sc_["msq"][:], sc_["m"][:], AF.Square))
        items.append(("scalar", s_stats))

        def v_var(v):
            v.wait_ge(acs.h, acs_b + NG + 1)
            dv.inc(v.scalar_tensor_tensor(sc_["var"][:], bq.ps[:, 0:256], INVD,
                                          sc_["msq"][:], ALU.mult, ALU.subtract))
        items.append(("vector", v_var))

        def s_r(sc):
            sc.wait_ge(dv.h, dv_b + NG + 1)
            sc.activation(sc_["r"][:], sc_["var"][:], AF.Ln, bias=EPS)
            ins = sc.activation(sc_["r"][:], sc_["r"][:], AF.Exp, scale=-0.5)
            if with_m2:
                sc.activation(self.m2_t[:], sc_["m"][:], AF.Copy)
                ins = sc.activation(self.r2_t[:], sc_["r"][:], AF.Copy)
            acs.inc(ins)
            bm.freed_by(acs)
            bq.freed_by(acs)
        items.append(("scalar", s_r))
        return items

    def _ln_apply_items(self, xb, dst, fam, r_tok, out_slice=None, pre_wait=None,
                        out_dv=None):
        pe, acs, dv = fam
        sc_ = self.lnscr["F" if fam is self.F else "M"]

        def v_xq(v):
            v.wait_ge(acs.h, r_tok)
            if pre_wait is not None:
                sem, n = pre_wait
                v.wait_ge(sem.h, n)
            ins = None
            for g in range(NG):
                xv = xb[:, g, :]
                v.tensor_tensor(sc_["xm"][:], xv, sc_["m"][:], ALU.subtract)
                if out_slice is None:
                    o = dst[:, g, :]
                else:
                    o = dst[:, g, out_slice]
                ins = v.tensor_tensor(o, sc_["xm"][:], sc_["r"][:], ALU.mult)
            dv.inc(ins)
        return [("vector", v_xq)]

    def phase_ln1(self, s):
        pe, acs, dv = self.F
        xb = self.xbuf[s % 3]
        items = self._ln_stats_items(xb, self.F, (self.bk[4], self.bk[5]),
                                     pe_b=pe.n, acs_b=acs.n, dv_b=dv.n,
                                     x_tok=(dv, dv.n))
        r_tok = acs.n + NG + 2
        pw = (self.pef, self._w01_done.get(s - 1, 0))
        items += self._ln_apply_items(xb, self.x1q, self.F, r_tok, pre_wait=pw)
        x1q_tok = dv.n + (NG + 1) + 1
        for it in items:
            self.sect(*it)
        assert dv.n == x1q_tok, (dv.n, x1q_tok)
        self._x1q_tok = x1q_tok

    def ffn_items(self, s):
        pe, acs, dv = self.F
        dgw = [self.dgwA, self.dgwB]
        dgw2 = self.dgw2
        calls = []
        if s < NCH:
            calls.append((0, self.x1q, self.act[0], s, ("F", self._x1q_tok)))
        if s >= 2:
            calls.append((1, self.x3q[s % 2], self.act[1], s - 2,
                          ("M", self._x3q_tok[s - 2])))
        ncal = len(calls)
        if ncal == 0:
            return []

        dgw_base = [dgw[0].n, dgw[1].n]
        dgw2_base = dgw2.n
        pe_base = pe.n
        acs_base = acs.n
        dv_base = dv.n

        def gidx(i, ci):
            return i * ncal + ci

        G = NSUP * ncal
        items = []

        def g_w(g):
            g.wait_ge(pe.h, pe_base)
            if s == 0:
                g.wait_ge(self.dg0.h, self._dg0_all)
            for i in range(NSUP):
                if i >= 2:
                    g.wait_ge(pe.h, pe_base + (i - 1) * ncal)
                for W in range(2):
                    dgw[i % 2].inc(g.dma_start(
                        self.w01b[:, i % 2, W],
                        self.w01s[W, i].rearrange("kt kp m -> kp kt m")), 16)
            for dj in range(NDJ):
                for hh in range(2):
                    u = 2 * dj + hh
                    if u >= 1:
                        g.wait_ge(pe.h, pe_base + G + u)
                    dgw2.inc(g.dma_start(
                        self.w2b[:],
                        self.w2s[dj, hh * 32:(hh + 1) * 32]
                        .rearrange("kh hp d -> hp kh d")), 16)
        items.append(("gpsimd", g_w))

        entry_waits = []
        for c, _, _, _, _ in calls:
            for mt in range(2):
                b = self.bk[c * 2 + mt]
                if b.free_tok is not None:
                    entry_waits.append(b.free_tok)
                    b.free_tok = None

        for i in range(NSUP):
            for ci, (c, xq, actb, cs, rhs_tok) in enumerate(calls):
                g = gidx(i, ci)

                def t_mm(t, i=i, ci=ci, c=c, xq=xq, g=g, rhs_tok=rhs_tok):
                    if i == 0 and ci == 0:
                        for sem, n in entry_waits:
                            t.wait_ge(sem.h, n)
                    if i == 0:
                        fam, n = rhs_tok
                        t.wait_ge((self.dvf if fam == "F" else self.dvm).h, n)
                    t.wait_ge(dgw[i % 2].h, dgw_base[i % 2] + 32 * (i // 2 + 1))
                    if i >= 1:
                        t.wait_ge(acs.h, acs_base + 2 * gidx(i - 1, ci) + 1)
                    ins = None
                    for mt in range(2):
                        b = self.bk[c * 2 + mt]
                        for kt in range(DCN):
                            t.matmul(b.ps[:, 0:256],
                                     self.w01b[:, i % 2, 0, kt, mt * 128:(mt + 1) * 128],
                                     xq[:, kt, :],
                                     start=(kt == 0), stop=False)
                        for kt in range(DCN):
                            ins = t.matmul(b.ps[:, 256:512],
                                           self.w01b[:, i % 2, 1, kt, mt * 128:(mt + 1) * 128],
                                           xq[:, kt, :],
                                           start=False, stop=(kt == DCN - 1))
                    pe.inc(ins)
                items.append(("tensor", t_mm))

                def s_evx2(sc, ci=ci, c=c, g=g):
                    sc.wait_ge(pe.h, pe_base + g + 1)
                    if g >= 1:
                        sc.wait_ge(dv.h, dv_base + 2 * (g - 1) + 2)
                    for mt in range(2):
                        b = self.bk[c * 2 + mt]
                        sc.activation(self.gf16[ci][:, mt, :], b.ps[:], AF.Copy)
                    acs.inc(sc.activation(self.ch_x2[:],
                                          self.gf16[ci][:, :, 0:256], AF.Square))
                items.append(("scalar", s_evx2))

                def v_chA(v, ci=ci, g=g):
                    gv_ = self.gf16[ci][:, :, 0:256]
                    v.wait_ge(acs.h, acs_base + 2 * g + 1)
                    v.tensor_tensor(self.ch_x3[:], self.ch_x2[:], gv_, ALU.mult)
                    v.tensor_scalar_mul(self.ch_t[:], self.ch_x3[:], C1)
                    v.tensor_tensor(self.ch_t[:], self.ch_t[:], gv_, ALU.add)
                    dv.inc(v.tensor_scalar_mul(self.ch_u[:], self.ch_t[:], C2))
                items.append(("vector", v_chA))

                def s_tanh(sc, g=g):
                    sc.wait_ge(dv.h, dv_base + 2 * g + 1)
                    acs.inc(sc.activation(self.ch_t[:], self.ch_u[:], AF.Tanh))
                items.append(("scalar", s_tanh))

                def v_chB(v, i=i, ci=ci, g=g, actb=actb):
                    gv_ = self.gf16[ci][:, :, 0:256]
                    fv_ = self.gf16[ci][:, :, 256:512]
                    if i == 0:
                        v.wait_ge(pe.h, pe_base)
                    v.wait_ge(acs.h, acs_base + 2 * g + 2)
                    v.tensor_scalar(self.ch_cdf[:], self.ch_t[:], 1.0, 0.5,
                                    ALU.add, ALU.mult)
                    v.tensor_tensor(self.ch_gv[:], self.ch_cdf[:], gv_, ALU.mult)
                    v.tensor_tensor(self.ch_a16[:], self.ch_gv[:], fv_, ALU.mult)
                    dv.inc(v.tensor_copy(actb[:, 2 * i:2 * i + 2, :], self.ch_a16[:]))
                items.append(("vector", v_chB))

        chain_done = dv_base + 2 * G
        self._w01_done[s] = pe_base + G
        for c, _, _, _, _ in calls:
            for mt in range(2):
                self.bk[c * 2 + mt].free_tok = (acs, acs_base + 2 * (G - 1) + 1)

        pe2_base = pe_base + G
        acs2_base = acs_base + 2 * G

        for dj in range(NDJ):
            def t_w2(t, dj=dj):
                if dj == 0:
                    t.wait_ge(dv.h, chain_done)
                if dj >= 2:
                    t.wait_ge(dv.h, chain_done + (dj - 1) * ncal)
                for hh in range(2):
                    u = 2 * dj + hh
                    t.wait_ge(dgw2.h, dgw2_base + 16 * (u + 1))
                    ins = None
                    for ci, (c, xq, actb, cs, _) in enumerate(calls):
                        b = self.bk[c * 2 + dj % 2]
                        for kh in range(32):
                            ins = t.matmul(b.ps[:, 0:256],
                                           self.w2b[:, kh, :],
                                           actb[:, hh * 32 + kh, :],
                                           start=(hh == 0 and kh == 0),
                                           stop=(hh == 1 and kh == 31))
                    pe.inc(ins)
            items.append(("tensor", t_w2))

            for ci in range(ncal):
                def s_w2(sc, dj=dj, ci=ci):
                    sc.wait_ge(pe.h, pe2_base + 2 * dj + 2)
                    sc.wait_ge(dv.h, chain_done + dj * ncal + ci)
                    c = calls[ci][0]
                    acs.inc(sc.activation(self.out16[:],
                                          self.bk[c * 2 + dj % 2].ps[:, 0:256],
                                          AF.Copy))
                items.append(("scalar", s_w2))

                def v_w2(v, dj=dj, ci=ci):
                    c, xq, actb, cs, _ = calls[ci]
                    v.wait_ge(acs.h, acs2_base + dj * ncal + ci + 1)
                    xbv = self.xbuf[cs % 3]
                    dv.inc(v.scalar_tensor_tensor(xbv[:, dj, :], self.out16[:], 0.5,
                                                  xbv[:, dj, :], ALU.mult, ALU.add))
                items.append(("vector", v_w2))

        for ci, (c, xq, actb, cs, _) in enumerate(calls):
            for p in range(2):
                last_dj = NDJ - 2 + p
                self.bk[c * 2 + p].free_tok = (acs, acs2_base + last_dj * ncal + ci + 1)

        self._ffn_pe_end[s] = pe2_base + 2 * NDJ
        end_dv = chain_done + NDJ * ncal
        for ci, (c, xq, actb, cs, _) in enumerate(calls):
            self._x1done[cs] = end_dv
        if s >= 2:
            self._x2done = end_dv
        return items

    def phase_out(self, c):
        pe, dv, ds = self.pef, self.dvf, self.dsoo
        xb = self.xbuf[c % 3]
        done_tok = self._x2done
        pe_base = pe.n
        dv_base = dv.n
        ds_base = ds.n

        def t_tr(t):
            t.wait_ge(dv.h, done_tok)
            k = 0
            for th in range(2):
                for dcg in range(4):
                    k += 1
                    b = self.bk[4 + (th * 4 + dcg) % 2]
                    if k <= 2:
                        b.wait_free(t)
                    else:
                        t.wait_ge(dv.h, dv_base + (k - 2))
                    for j in range(4):
                        dc = dcg * 4 + j
                        ins = t.transpose(b.ps[:, j * 128:(j + 1) * 128],
                                          xb[:, dc, th * 128:(th + 1) * 128],
                                          self.id_s[:])
                    pe.inc(ins)

        self.sect("tensor", t_tr)

        def v_evac(v):
            k = 0
            for th in range(2):
                for dcg in range(4):
                    k += 1
                    if k == 5:
                        v.wait_ge(ds.h, ds_base + 16)
                    b = self.bk[4 + (th * 4 + dcg) % 2]
                    v.wait_ge(pe.h, pe_base + k)
                    ins = v.tensor_copy(
                        self.xio[:, dcg * 512:(dcg + 1) * 512]
                        .rearrange("p (a t) -> p a t", a=4),
                        b.ps[:].rearrange("p (a t) -> p a t", a=4))
                    dv.inc(ins)
                    b.freed_by(dv)

        self.sect("vector", v_evac)

        def s_out(sy):
            for th in range(2):
                sy.wait_ge(dv.h, dv_base + 4 * (th + 1))
                ds.inc(sy.dma_start(
                    self.out[c * TC + th * 128: c * TC + (th + 1) * 128, :],
                    self.xio[:]), 16)

        self.sect("sync", s_out)
        self._out_done[c] = ds.n

    def mamba_items(self, m):
        pe, acs, dv = self.M
        xb = self.xbuf[m % 3]
        items = []

        pe0, ac0, dv0 = pe.n, acs.n, dv.n

        x_tok = (self.dvf, self._x1done[m])
        items += self._ln_stats_items(xb, self.M, (self.bk[6], self.bk[7]),
                                      pe_b=pe0, acs_b=ac0, dv_b=dv0,
                                      x_tok=x_tok, with_m2=True)
        pe1 = pe0 + NG
        ac1 = ac0 + NG + 2
        dv1 = dv0 + NG + 1
        r_tok = ac1

        def v_halo(v):
            v.wait_ge(acs.h, r_tok)
            dv.inc(v.tensor_copy(self.xh2[:, :, 1:4],
                                 self.xh2[:, :, TC + 1:TC + 4]))
        items.append(("vector", v_halo))
        dv2 = dv1 + 1

        items += self._ln_apply_items(xb, self.xh2, self.M, r_tok,
                                      out_slice=slice(4, 4 + TC))
        xh2_tok = dv2 + 1

        acs_cv = ac1
        for dc in range(DCN):
            def v_taps(v, dc=dc):
                if dc == 0:
                    v.wait_ge(dv.h, xh2_tok)
                else:
                    v.wait_ge(acs.h, acs_cv + dc)
                v.tensor_scalar_mul(self.xcb[:], self.xh2[:, dc, 1:1 + TC],
                                    self.convw_s[:, dc, 0:1])
                for j in range(1, 4):
                    v.scalar_tensor_tensor(self.xcb[:],
                                           self.xh2[:, dc, 1 + j:1 + j + TC],
                                           self.convw_s[:, dc, j:j + 1],
                                           self.xcb[:], ALU.mult, ALU.add)
                dv.inc(v.tensor_scalar_add(self.xcb[:], self.xcb[:],
                                           self.convb_s[:, dc:dc + 1]))
            items.append(("vector", v_taps))

            def s_tanh(sc, dc=dc):
                sc.wait_ge(dv.h, xh2_tok + 2 * dc + 1)
                acs.inc(sc.activation(self.th_s[:], self.xcb[:], AF.Tanh, scale=0.5))
            items.append(("scalar", s_tanh))

            def v_silu(v, dc=dc):
                v.wait_ge(acs.h, acs_cv + dc + 1)
                v.scalar_tensor_tensor(self.u_s[:], self.th_s[:], 1.0, self.xcb[:],
                                       ALU.add, ALU.mult)
                dv.inc(v.tensor_scalar_mul(self.xconv[:, dc, :], self.u_s[:], 0.5))
            items.append(("vector", v_silu))

        dv3 = xh2_tok + 32
        ac2 = acs_cv + 16

        bdt, bbc = self.bk[6], self.bk[7]
        pe_pj = pe1

        def t_proj(t):
            t.wait_ge(dv.h, dv3)
            if m == 0:
                t.wait_ge(self.dsw.h, self._projw_tok)
            bdt.wait_free(t)
            bbc.wait_free(t)
            ins = None
            for kt in range(DCN):
                t.matmul(bdt.ps[:, 0:256], self.xproj_s[:, kt, 0:128], self.xconv[:, kt, :],
                         start=(kt == 0), stop=(kt == DCN - 1))
                ins = t.matmul(bbc.ps[0:32, 0:256], self.xproj_s[:, kt, 128:160],
                               self.xconv[:, kt, :],
                               start=(kt == 0), stop=(kt == DCN - 1))
            pe.inc(ins)
        items.append(("tensor", t_proj))

        def v_pj(v):
            v.wait_ge(pe.h, pe_pj + 1)
            v.tensor_copy(self.dt16[:], bdt.ps[:, 0:256])
            dv.inc(v.tensor_copy(self.bc16[0:32, :], bbc.ps[0:32, 0:256]))
            bdt.freed_by(dv)
            bbc.freed_by(dv)
        items.append(("vector", v_pj))
        dv_pj = dv3 + 1

        pe_d = pe_pj + 1
        acs_d = ac2
        dv_d = dv_pj
        for dc in range(DCN):
            def t_delta(t, dc=dc):
                b = self.bk[6 + dc % 2]
                if dc < 2:
                    t.wait_ge(dv.h, dv_pj)
                else:
                    t.wait_ge(dv.h, dv_d + 3 * (dc - 2) + 2)
                pe.inc(t.matmul(b.ps[:, 0:256], self.dtproj_s[:, dc, :], self.dt16[:],
                                start=True, stop=True))
            items.append(("tensor", t_delta))

            def s_z(sc, dc=dc):
                b = self.bk[6 + dc % 2]
                sc.wait_ge(pe.h, pe_d + dc + 1)
                if dc >= 1:
                    sc.wait_ge(dv.h, dv_d + 3 * (dc - 1) + 2)
                acs.inc(sc.activation(self.z_s[:], b.ps[:, 0:256], AF.Identity,
                                      bias=self.dtb_s[:, dc:dc + 1]))
            items.append(("scalar", s_z))

            def v_az(v, dc=dc):
                v.wait_ge(acs.h, acs_d + 2 * dc + 1)
                dv.inc(v.scalar_tensor_tensor(self.az_s[:], self.z_s[:], -1.0, self.z_s[:],
                                              ALU.mult, ALU.max))
            items.append(("vector", v_az))

            def s_expln(sc, dc=dc):
                sc.wait_ge(dv.h, dv_d + 3 * dc + 1)
                sc.activation(self.e_s[:], self.az_s[:], AF.Exp, scale=-1.0)
                acs.inc(sc.activation(self.e_s[:], self.e_s[:], AF.Ln, bias=1.0))
            items.append(("scalar", s_expln))

            def v_dl(v, dc=dc):
                v.wait_ge(acs.h, acs_d + 2 * dc + 2)
                dv.inc(v.scalar_tensor_tensor(self.delta[:, dc, :], self.z_s[:], 0.0,
                                              self.e_s[:], ALU.max, ALU.add))
                dv.inc(v.tensor_tensor(self.wtd[:, dc, :], self.delta[:, dc, :],
                                       self.xconv[:, dc, :], ALU.mult))
            items.append(("vector", v_dl))

        delta_tok = dv_d + 3 * DCN
        ac3 = acs_d + 2 * DCN
        pe3 = pe_d + DCN
        tok_y = [(dv, dv_d + 3 * (DCN - 2) + 2), (dv, dv_d + 3 * (DCN - 1) + 2)]

        dsbc = self.dsbc
        bc_base = dsbc.n

        def s_bc(sy):
            sy.wait_ge(dv.h, delta_tok)
            sy.wait_ge(pe.h, pe_pj + 1)
            for np_ in range(NP):
                for n2 in range(2):
                    dsbc.inc(sy.dma_start(
                        self.Bbc[n2 * 64:(n2 + 1) * 64, np_, :],
                        self.bc16[2 * np_ + n2:2 * np_ + n2 + 1, :].unsqueeze(1)
                        .broadcast_to([1, 64, TC])), 16)
                    dsbc.inc(sy.dma_start(
                        self.Cbc[n2 * 64:(n2 + 1) * 64, np_, :],
                        self.bc16[16 + 2 * np_ + n2:16 + 2 * np_ + n2 + 1, :].unsqueeze(1)
                        .broadcast_to([1, 64, TC])), 16)
        items.append(("sync", s_bc))
        bc_tok = bc_base + 16 * 32

        dsp = [self.dse, self.dso]
        dsp_base = [self.dse.n, self.dso.n]
        acs_sc = ac3
        scan_dv = delta_tok
        pe_y = pe3

        def sdv(nscan, ny):
            return scan_dv + 5 * nscan + ny

        def after_scan(dblk):
            return sdv(dblk + 1, dblk // 2)

        def after_y(dc):
            return sdv(2 * dc + 2, dc + 1)

        def s_ddn(sy):
            for dblk in range(NDBLK):
                par = dblk % 2
                if dblk >= 2:
                    sy.wait_ge(dv.h, after_scan(dblk - 2))
                    sy.wait_ge(acs.h, acs_sc + (dblk - 1))
                p0 = 64 * (dblk % 2)
                dc = dblk // 2
                for n2 in range(2):
                    dsp[par].inc(sy.dma_start(
                        self.ddn[par][n2 * 64:(n2 + 1) * 64, :],
                        self.delta[p0:p0 + 64, dc, :]), 16)
                    dsp[par].inc(sy.dma_start(
                        self.wdn[par][n2 * 64:(n2 + 1) * 64, :],
                        self.wtd[p0:p0 + 64, dc, :]), 16)
        items.append(("sync", s_ddn))

        for dblk in range(NDBLK):
            par = dblk % 2

            def s_exp(sc, dblk=dblk, par=par):
                sc.wait_ge(dsp[par].h, dsp_base[par] + 64 * (dblk // 2 + 1))
                if dblk >= 1:
                    sc.wait_ge(dv.h, after_scan(dblk - 1))
                    sc.wait_ge(pe.h, pe_y + dblk)
                ins = None
                for np_ in range(NP):
                    ins = sc.activation(self.a_g[:, np_, 1:SEPW], self.ddn[par][:],
                                        AF.Exp,
                                        scale=self.A_perm[:, dblk * 8 + np_:dblk * 8 + np_ + 1])
                acs.inc(ins)
            items.append(("scalar", s_exp))

            def v_scan(v, dblk=dblk, par=par):
                if dblk == 0:
                    v.wait_ge(dsbc.h, bc_tok)
                v.wait_ge(acs.h, acs_sc + dblk + 1)
                if dblk >= 1:
                    v.wait_ge(pe.h, pe_y + dblk)
                v.tensor_copy(self.b_g[:, :, 0:1],
                              self.h_state[:, dblk * 8:dblk * 8 + 8].unsqueeze(2))
                v.tensor_tensor(self.b_g[:, :, 1:SEPW],
                                self.wdn[par][:].unsqueeze(1).broadcast_to([128, NP, TC]),
                                self.Bbc[:], ALU.mult)
                v.tensor_tensor_scan(self.h_g[:].rearrange("p a t -> p (a t)"),
                                     self.a_g[:].rearrange("p a t -> p (a t)"),
                                     self.b_g[:].rearrange("p a t -> p (a t)"),
                                     0.0, ALU.mult, ALU.add)
                v.tensor_copy(self.h_state[:, dblk * 8:dblk * 8 + 8],
                              self.h_g[:, :, SEPW - 1:SEPW].squeeze(2))
                ins = v.tensor_tensor(self.a_g[:, :, 1:SEPW], self.h_g[:, :, 1:SEPW],
                                      self.Cbc[:], ALU.mult)
                dv.inc(ins, 5)
            items.append(("vector", v_scan))

            def t_red(t, dblk=dblk):
                t.wait_ge(dv.h, after_scan(dblk))
                dc = dblk // 2
                b = self.bk[6 + dc % 2]
                if dblk % 2 == 0:
                    if dc < 2:
                        if tok_y[dc % 2] is not None:
                            sem, n = tok_y[dc % 2]
                            t.wait_ge(sem.h, n)
                            tok_y[dc % 2] = None
                    else:
                        t.wait_ge(dv.h, after_y(dc - 2))
                R = self.re_s if dblk % 2 == 0 else self.ro_s
                ins = None
                for np_ in range(NP):
                    ins = t.matmul(b.ps[:, 0:256], R[:], self.a_g[:, np_, 1:SEPW],
                                   start=(dblk % 2 == 0 and np_ == 0),
                                   stop=(dblk % 2 == 1 and np_ == NP - 1))
                pe.inc(ins)
            items.append(("tensor", t_red))

            if dblk % 2 == 1:
                def v_y(v, dblk=dblk):
                    dc = dblk // 2
                    b = self.bk[6 + dc % 2]
                    v.wait_ge(pe.h, pe_y + 2 * dc + 2)
                    v.tensor_tensor(self.ytmp[:], xb[:, dc, :], self.m2_t[:],
                                    ALU.subtract)
                    v.scalar_tensor_tensor(self.ytmp2[:], self.ytmp[:],
                                           self.Dp_s[:, dc:dc + 1], self.r2_t[:],
                                           ALU.mult, ALU.mult)
                    v.tensor_tensor(self.ytmp[:], self.ytmp2[:], b.ps[:, 0:256],
                                    ALU.add)
                    dv.inc(v.tensor_tensor(xb[:, dc, :], xb[:, dc, :], self.ytmp[:],
                                           ALU.add))
                    b.freed_by(dv)
                items.append(("vector", v_y))

        dv4 = scan_dv + 5 * NDBLK + DCN
        pe4 = pe_y + NDBLK
        ac4 = acs_sc + NDBLK

        items += self._ln_stats_items(xb, self.M, (self.bk[6], self.bk[7]),
                                      pe_b=pe4, acs_b=ac4, dv_b=dv4)
        r3_tok = ac4 + NG + 2
        pwait = (self.pef, self._ffn_pe_end.get(m, 0))
        items += self._ln_apply_items(xb, self.x3q[m % 2], self.M, r3_tok,
                                      pre_wait=pwait)
        self._x3q_tok[m] = dv4 + NG + 1 + 1
        self._expect_m = (pe4 + NG, ac4 + NG + 2, dv4 + NG + 1 + 1)
        return items

    def finalize(self):
        sems = [self.dg0, self.dgwA, self.dgwB, self.dgw2, self.dsbc,
                self.dse, self.dso, self.dsx, self.dsoo, self.dsw]
        tgts = [(s.h, s.n) for s in sems]

        def s_fin(sy):
            for h, n in tgts:
                sy.wait_ge(h, n)

        self.sect("sync", s_fin)


def build_program():
    b = Builder()
    with b.es:
        return b.build()


_NC_CACHE = None


def kernel(L0_attn_x, L0_attn_conv_w, L0_attn_conv_b, L0_attn_xproj_w,
           L0_attn_dtproj_w, L0_attn_dtproj_b, L0_attn_A_log, L0_attn_D_param,
           L0_ffn_w_gating, L0_ffn_w_linear):
    global _NC_CACHE
    X = np.ascontiguousarray(np.asarray(L0_attn_x, np.float32))
    consts = {
        "conv_w": np.ascontiguousarray(np.asarray(L0_attn_conv_w, np.float32)),
        "conv_b": np.ascontiguousarray(np.asarray(L0_attn_conv_b, np.float32)),
        "xproj_w": np.ascontiguousarray(np.asarray(L0_attn_xproj_w, np.float32)),
        "dtproj_w": np.ascontiguousarray(np.asarray(L0_attn_dtproj_w, np.float32)),
        "dtproj_b": np.ascontiguousarray(np.asarray(L0_attn_dtproj_b, np.float32)),
        "A_perm_log": np.ascontiguousarray(
            np.asarray(L0_attn_A_log, np.float32)
            .reshape(32, 64, 8, 2).transpose(3, 1, 0, 2).reshape(128, 256)),
        "D_param": np.ascontiguousarray(np.asarray(L0_attn_D_param, np.float32)),
        "w_gating": np.ascontiguousarray(np.asarray(L0_ffn_w_gating, np.float32)),
        "w_linear": np.ascontiguousarray(np.asarray(L0_ffn_w_linear, np.float32)),
        "ident": np.eye(128, dtype=np.float32),
        "fconst": np.tile(np.array([[EPS, 1.0, -0.5, -1.0, 0.5, INVD, 0.0, 0.0]],
                                   np.float32), (128, 1)),
        "ones16": np.ones((128, 128), dtype=ml_dtypes.bfloat16),
    }
    re_ = np.zeros((128, 128), np.float32)
    ro_ = np.zeros((128, 128), np.float32)
    for p in range(128):
        re_[p, p % 64] = 1.0
        ro_[p, 64 + (p % 64)] = 1.0
    consts["r_even"] = re_.astype(ml_dtypes.bfloat16)
    consts["r_odd"] = ro_.astype(ml_dtypes.bfloat16)

    if _NC_CACHE is None:
        _NC_CACHE = build_program()
    nc = _NC_CACHE
    in_maps = [dict(consts, x=np.ascontiguousarray(X[b])) for b in range(8)]
    res = run_bass_kernel_spmd(nc, in_maps, core_ids=list(range(8)))
    return np.stack([np.asarray(res.results[b]["out"], np.float32) for b in range(8)],
                    axis=0)


# version 03
# speedup vs baseline: 1.1744x; 1.1744x over previous
import contextlib
import numpy as np
import ml_dtypes
import concourse.bass as bass
import concourse.mybir as mybir
from concourse.bass_utils import run_bass_kernel_spmd

F32 = mybir.dt.float32
BF16 = mybir.dt.bfloat16
E4 = mybir.dt.float8e4
AF = mybir.ActivationFunctionType
ALU = mybir.AluOpType

T, D, H = 2048, 2048, 8192
NS, RNK = 16, 128
TC = 256
NCH = T // TC
DCN = D // 128
HCN = H // 128
NSUP = 32
NDJ = 16
NDBLK = 32
NP = 8
SEPW = 257
EPS = 1e-5
C1 = float(np.float32(ml_dtypes.bfloat16(0.044715)))
C2 = float(np.float32(ml_dtypes.bfloat16(np.sqrt(2.0 / np.pi))))
INVD = 1.0 / D
NG = 16


class Sem:
    def __init__(self, es, nc, name):
        self.h = es.enter_context(nc.semaphore(name))
        self.n = 0

    def inc(self, inst, amt=1):
        inst.then_inc(self.h, amt)
        self.n += amt
        return inst


class Bank:
    def __init__(self, es, nc, name):
        self.ps = es.enter_context(nc.psum_tensor(name, [128, 512], F32))
        self.free_tok = None

    def freed_by(self, sem):
        self.free_tok = (sem, sem.n)

    def wait_free(self, eng):
        if self.free_tok is not None:
            sem, n = self.free_tok
            eng.wait_ge(sem.h, n)
            self.free_tok = None


class Builder:
    def __init__(self):
        self.nc = bass.Bass("TRN2", target_bir_lowering=False,
                            detect_race_conditions=False)
        self.es = contextlib.ExitStack()
        self._out_done = {}
        self._x1done = {}
        self._ffn_pe_end = {}
        self._w01_done = {}
        self._x3q_tok = {}

    def sect(self, eng, fn):
        getattr(self.block, eng)(fn)

    def _overlay(self, name, shape, dt, base_handle, byte_off):
        nc = self.nc
        h = nc.virtual_sbuf_tensor(name, shape, dt)
        nc._tensor(h.name, h.shape, h.dtype, type="SB")
        mloc = nc.lookup_mloc(h)
        mloc.allocated = True
        mloc.addr = nc.lookup_mloc(base_handle).addr + byte_off
        return h

    def run_items(self, *lists):
        lists = [l for l in lists if l]
        if not lists:
            return
        main = max(lists, key=len)
        rest = [l for l in lists if l is not main]
        n = len(main)
        idxs = [0] * len(rest)
        for k, (eng, fn) in enumerate(main):
            self.sect(eng, fn)
            for li, l in enumerate(rest):
                want = (k + 1) * len(l) // n
                while idxs[li] < want:
                    e2, f2 = l[idxs[li]]
                    self.sect(e2, f2)
                    idxs[li] += 1
        for li, l in enumerate(rest):
            while idxs[li] < len(l):
                e2, f2 = l[idxs[li]]
                self.sect(e2, f2)
                idxs[li] += 1

    def build(self):
        nc, es = self.nc, self.es
        self.x_in = nc.dram_tensor("x", [T, D], F32, kind="ExternalInput").ap()
        self.conv_w = nc.dram_tensor("conv_w", [D, 1, 4], F32, kind="ExternalInput").ap()
        self.conv_b = nc.dram_tensor("conv_b", [D], F32, kind="ExternalInput").ap()
        self.xproj_w = nc.dram_tensor("xproj_w", [160, D], F32, kind="ExternalInput").ap()
        self.dtproj_w = nc.dram_tensor("dtproj_w", [D, RNK], F32, kind="ExternalInput").ap()
        self.dtproj_b = nc.dram_tensor("dtproj_b", [D], F32, kind="ExternalInput").ap()
        self.A_permin = nc.dram_tensor("A_perm_log", [128, 256], F32, kind="ExternalInput").ap()
        self.D_param = nc.dram_tensor("D_param", [D], F32, kind="ExternalInput").ap()
        self.wg = nc.dram_tensor("w_gating", [2, D, H], F32, kind="ExternalInput").ap()
        self.wl = nc.dram_tensor("w_linear", [H, D], F32, kind="ExternalInput").ap()
        self.ident = nc.dram_tensor("ident", [128, 128], F32, kind="ExternalInput").ap()
        self.ones16 = nc.dram_tensor("ones16", [128, 128], BF16, kind="ExternalInput").ap()
        self.r_even = nc.dram_tensor("r_even", [128, 128], BF16, kind="ExternalInput").ap()
        self.r_odd = nc.dram_tensor("r_odd", [128, 128], BF16, kind="ExternalInput").ap()
        self.fconst = nc.dram_tensor("fconst", [128, 8], F32, kind="ExternalInput").ap()
        self.out = nc.dram_tensor("out", [T, D], F32, kind="ExternalOutput").ap()
        self.w01s = nc.dram_tensor("w01s", [2, NSUP, 16, 128, 256], E4).ap()
        self.w2s = nc.dram_tensor("w2s", [NDJ, HCN, 128, 128], E4).ap()
        self.xproj16 = nc.dram_tensor("xproj16", [160, D], BF16).ap()
        self.dtproj16 = nc.dram_tensor("dtproj16", [D, RNK], BF16).ap()

        sb = lambda name, shape, dt: es.enter_context(nc.sbuf_tensor(name, shape, dt))
        self.xbuf = [sb(f"xbuf{i}", [128, DCN, TC], F32) for i in range(3)]
        self.xio = sb("xio", [128, 2048], F32)
        self.x1q = sb("x1q", [128, DCN, TC], E4)
        self.x3q = [sb(f"x3q{i}", [128, DCN, TC], E4) for i in range(2)]
        _gf = sb("gf16", [128, 2, 512], BF16)
        self.gf16 = [_gf, _gf]
        self.ch_x2 = sb("ch_x2", [128, 2, 256], BF16)
        self.ch_t = sb("ch_t", [128, 2, 256], BF16)
        self.ch_cdf = sb("ch_cdf", [128, 2, 256], BF16)
        self.ch_gv = sb("ch_gv", [128, 2, 256], BF16)
        self.ch_a16 = self.ch_t
        self.ch_x3 = self.ch_cdf
        self.ch_u = self.ch_gv
        self.act = [sb(f"act{c}", [128, HCN, TC], E4) for c in range(2)]
        self.w01b = sb("w01b", [128, 2, 2, 16, 256], E4)
        self.w2b = [sb("w2b0", [128, HCN // 2, 128], E4),
                    self._overlay("w2b1", [128, HCN // 2, 128], E4,
                                  self.xio, 2048)]
        self.out16 = [self._overlay(f"out16_{i}", [128, TC], BF16,
                                    self.xio, 512 * i) for i in range(4)]
        self.lnscr = {}
        for fam in ("F", "M"):
            d = dict(
                q=sb(f"q_{fam}", [128, 256], F32),
                qh=sb(f"qh_{fam}", [128, 256], BF16),
                ql=sb(f"ql_{fam}", [128, 256], BF16),
                xh=sb(f"xh_{fam}", [128, 256], BF16),
                xl=sb(f"xl_{fam}", [128, 256], BF16),
                m=sb(f"m_{fam}", [128, TC], F32),
                msq=sb(f"msq_{fam}", [128, TC], F32),
                r=sb(f"r_{fam}", [128, TC], F32),
            )
            d["var"] = d["msq"]
            d["xm"] = d["q"]
            self.lnscr[fam] = d
        self.m2_t = sb("m2_t", [128, TC], F32)
        self.r2_t = sb("r2_t", [128, TC], F32)
        self.xh2 = sb("xh2", [128, DCN, TC + 4], BF16)
        self.xcb = sb("xcb", [128, TC], F32)
        self.th_s = sb("th_s", [128, TC], F32)
        self.u_s = self.lnscr["M"]["q"]
        self.xconv = sb("xconv", [128, DCN, TC], BF16)
        self.Bbc = self.xconv[:, 0:NP, :]
        self.Cbc = self.xconv[:, NP:2 * NP, :]
        self.dt16 = sb("dt16", [128, TC], BF16)
        self.bc16 = sb("bc16", [128, TC], BF16)
        self.delta = sb("delta", [128, DCN, TC], F32)
        self.wtd = sb("wtd", [128, DCN, TC], BF16)
        self.z_s = self.xcb
        self.az_s = self.u_s
        self.e_s = self.th_s
        self.ytmp = self.xcb
        self.ytmp2 = self.th_s
        self.ddn = [sb(f"ddn{i}", [128, TC], F32) for i in range(2)]
        self.wdn = [sb(f"wdn{i}", [128, TC], BF16) for i in range(2)]
        self.a_g = sb("a_g", [128, NP, SEPW], BF16)
        self.b_g = sb("b_g", [128, NP, SEPW], BF16)
        self.h_g = self.b_g
        self.h_state = sb("h_state", [128, 256], F32)
        self.A_perm = sb("A_perm", [128, 256], F32)
        self.fconst_s = sb("fconst_s", [128, 8], F32)
        self.id_s = sb("id_s", [128, 128], F32)
        self.ones_s = sb("ones_s", [128, 128], BF16)
        self.re_s = sb("re_s", [128, 128], BF16)
        self.ro_s = sb("ro_s", [128, 128], BF16)
        self.convw_s = sb("convw_s", [128, DCN, 4], F32)
        self.convb_s = sb("convb_s", [128, DCN], F32)
        self.dtb_s = sb("dtb_s", [128, DCN], F32)
        self.Dp_s = sb("Dp_s", [128, DCN], F32)
        self.xproj_s = sb("xproj_s", [128, DCN, 160], BF16)
        self.dtproj_s = sb("dtproj_s", [128, DCN, 128], BF16)

        for j, val in enumerate([EPS, 1.0, -0.5, -1.0, 0.5, INVD]):
            nc.const_aps.register(F32, val, self.fconst_s[:, j:j + 1])

        self.bk = [Bank(es, nc, f"bk{i}") for i in range(8)]

        self.dg0 = Sem(es, nc, "dg0")
        self.dgwA = Sem(es, nc, "dgwA")
        self.dgwB = Sem(es, nc, "dgwB")
        self.dgw2 = Sem(es, nc, "dgw2")
        self.dsbc = Sem(es, nc, "dsbc")
        self.dse = Sem(es, nc, "dse")
        self.dso = Sem(es, nc, "dso")
        self.dsx = Sem(es, nc, "dsx")
        self.dsoo = Sem(es, nc, "dsoo")
        self.dsw = Sem(es, nc, "dsw")
        self.pef = Sem(es, nc, "pef")
        self.acf = Sem(es, nc, "acf")
        self.dvf = Sem(es, nc, "dvf")
        self.pem = Sem(es, nc, "pem")
        self.acm = Sem(es, nc, "acm")
        self.dvm = Sem(es, nc, "dvm")
        self.F = (self.pef, self.acf, self.dvf)
        self.M = (self.pem, self.acm, self.dvm)

        self.block = es.enter_context(nc.Block())
        self.emit()
        self.finalize()
        return nc

    def emit(self):
        self.phase0()
        for s in range(NCH + 2):
            if s < NCH:
                self.phase_xload(s)
                self.phase_ln1(s)
            ffn_it = self.ffn_items(s)
            mam_it = self.mamba_items(s - 1) if 1 <= s <= NCH else []
            if s >= NCH - 1:
                for it in mam_it:
                    self.sect(*it)
                for it in ffn_it:
                    self.sect(*it)
            else:
                self.run_items(ffn_it, mam_it)
            if mam_it:
                got = (self.pem.n, self.acm.n, self.dvm.n)
                assert got == self._expect_m, (s, got, self._expect_m)
            if s >= 2:
                self.phase_out(s - 2)

    def phase0(self):
        dg0, dsw, acs, dv = self.dg0, self.dsw, self.acf, self.dvf

        def g_loads(g):
            for W in range(2):
                for ms in range(NSUP):
                    src = self.wg[W, :, ms * 256:(ms + 1) * 256] \
                        .rearrange("(kt kp) m -> kt kp m", kp=128)
                    dg0.inc(g.dma_start(self.w01s[W, ms], src), 16)
            for dj in range(NDJ):
                src = self.wl[:, dj * 128:(dj + 1) * 128] \
                    .rearrange("(kh hp) d -> kh hp d", hp=128)
                dg0.inc(g.dma_start(self.w2s[dj], src), 16)
            dg0.inc(g.dma_start(self.xproj16, self.xproj_w), 16)
            dg0.inc(g.dma_start(self.dtproj16, self.dtproj_w), 16)
            dg0.inc(g.dma_start(self.convw_s[:],
                                self.conv_w.squeeze(1).rearrange("(dc p) j -> p dc j", p=128)), 16)
            dg0.inc(g.dma_start(self.convb_s[:],
                                self.conv_b.rearrange("(dc p) -> p dc", p=128),
                                allow_slow_non_contiguous=True), 16)
            dg0.inc(g.dma_start(self.dtb_s[:],
                                self.dtproj_b.rearrange("(dc p) -> p dc", p=128),
                                allow_slow_non_contiguous=True), 16)
            dg0.inc(g.dma_start(self.Dp_s[:],
                                self.D_param.rearrange("(dc p) -> p dc", p=128),
                                allow_slow_non_contiguous=True), 16)
            dg0.inc(g.dma_start(self.A_perm[:], self.A_permin), 16)
            dg0.inc(g.dma_start(self.fconst_s[:], self.fconst), 16)
            dg0.inc(g.dma_start(self.id_s[:], self.ident), 16)
            dg0.inc(g.dma_start(self.ones_s[:], self.ones16), 16)
            dg0.inc(g.dma_start(self.re_s[:], self.r_even), 16)
            dg0.inc(g.dma_start(self.ro_s[:], self.r_odd), 16)

        self.sect("gpsimd", g_loads)
        self._dg0_all = dg0.n

        def s_loads(sy):
            sy.wait_ge(dg0.h, self._dg0_all)
            for kt in range(DCN):
                dsw.inc(sy.dma_start(self.xproj_s[:, kt, :],
                                     self.xproj16[:, kt * 128:(kt + 1) * 128],
                                     transpose=True), 16)
            for dc in range(DCN):
                dsw.inc(sy.dma_start(self.dtproj_s[:, dc, :],
                                     self.dtproj16[dc * 128:(dc + 1) * 128, :],
                                     transpose=True), 16)

        self.sect("sync", s_loads)
        self._projw_tok = dsw.n

        def s_negexp(sc):
            sc.wait_ge(dg0.h, self._dg0_all)
            acs.inc(sc.activation(self.A_perm[:], self.A_perm[:], AF.Exp))

        self.sect("scalar", s_negexp)
        acs_tok = acs.n

        def v_init(v):
            v.wait_ge(acs.h, acs_tok)
            v.tensor_scalar_mul(self.A_perm[:], self.A_perm[:], -1.0)
            v.memset(self.h_state[:], 0.0)
            v.memset(self.a_g[:].rearrange("p a t -> p (a t)"), 0.0)
            dv.inc(v.memset(self.xh2[:].rearrange("p a t -> p (a t)"), 0.0))

        self.sect("vector", v_init)
        self._init_dv = dv.n

    def phase_xload(self, s):
        ds, pe, dv = self.dsx, self.pef, self.dvf
        xb = self.xbuf[s % 3]
        ds_base = ds.n
        pe_base = pe.n
        dv_all = dv.n

        def s_load(sy):
            if s >= 3:
                sy.wait_ge(self.dsoo.h, self._out_done[s - 3])
            if s >= 1:
                sy.wait_ge(pe.h, pe_base)
                sy.wait_ge(dv.h, dv_all)
            ds.inc(sy.dma_start(self.xio[:],
                                self.x_in[s * TC: s * TC + 128, :]), 16)
            sy.wait_ge(pe.h, pe_base + 4)
            ds.inc(sy.dma_start(self.xio[:],
                                self.x_in[s * TC + 128: s * TC + 256, :]), 16)

        self.sect("sync", s_load)

        def t_tr(t):
            t.wait_ge(dv.h, dv_all)
            if s == 0:
                t.wait_ge(dv.h, self._init_dv)
            k = 0
            for th in range(2):
                t.wait_ge(ds.h, ds_base + 16 * (th + 1))
                for dcg in range(4):
                    k += 1
                    b = self.bk[4 + (th * 4 + dcg) % 2]
                    if k <= 2:
                        b.wait_free(t)
                    else:
                        t.wait_ge(dv.h, dv_all + (k - 2))
                    for j in range(4):
                        dc = dcg * 4 + j
                        ins = t.transpose(b.ps[:, j * 128:(j + 1) * 128],
                                          self.xio[:, dc * 128:(dc + 1) * 128],
                                          self.id_s[:])
                    pe.inc(ins)

        self.sect("tensor", t_tr)

        def v_evac(v):
            k = 0
            for th in range(2):
                for dcg in range(4):
                    k += 1
                    b = self.bk[4 + (th * 4 + dcg) % 2]
                    v.wait_ge(pe.h, pe_base + k)
                    ins = v.tensor_copy(
                        xb[:, dcg * 4:(dcg + 1) * 4, th * 128:(th + 1) * 128],
                        b.ps[:].rearrange("p (a t) -> p a t", a=4))
                    dv.inc(ins)
                    b.freed_by(dv)

        self.sect("vector", v_evac)

    def _ln_stats_items(self, xb, fam, banks, pe_b, acs_b, dv_b, x_tok=None,
                        with_m2=False):
        pe, acs, dv = fam
        bm, bq = banks
        sc_ = self.lnscr["F" if fam is self.F else "M"]
        items = []
        for g in range(NG):
            def s_sq(sc, g=g):
                if g == 0 and x_tok is not None:
                    sem, n = x_tok
                    sc.wait_ge(sem.h, n)
                if g >= 1:
                    sc.wait_ge(pe.h, pe_b + g)
                    sc.wait_ge(dv.h, dv_b + g)
                xv = xb[:, g, :]
                sc.activation(sc_["q"][:], xv, AF.Square)
                acs.inc(sc.activation(sc_["qh"][:], sc_["q"][:], AF.Copy))
            items.append(("scalar", s_sq))

            def v_hl(v, g=g):
                if g >= 1:
                    v.wait_ge(pe.h, pe_b + g)
                xv = xb[:, g, :]
                v.wait_ge(acs.h, acs_b + g + 1)
                v.scalar_tensor_tensor(sc_["ql"][:], sc_["qh"][:], -1.0, sc_["q"][:],
                                       ALU.mult, ALU.add)
                v.tensor_copy(sc_["xh"][:], xv)
                dv.inc(v.scalar_tensor_tensor(sc_["xl"][:], sc_["xh"][:], -1.0, xv,
                                              ALU.mult, ALU.add))
            items.append(("vector", v_hl))

            def t_mm(t, g=g):
                if g == 0:
                    bm.wait_free(t)
                    bq.wait_free(t)
                t.wait_ge(dv.h, dv_b + g + 1)
                t.matmul(bm.ps[:, 0:256], self.ones_s[:], sc_["xh"][:],
                         start=(g == 0), stop=False)
                t.matmul(bm.ps[:, 0:256], self.ones_s[:], sc_["xl"][:],
                         start=False, stop=(g == NG - 1))
                t.matmul(bq.ps[:, 0:256], self.ones_s[:], sc_["qh"][:],
                         start=(g == 0), stop=False)
                ins = t.matmul(bq.ps[:, 0:256], self.ones_s[:], sc_["ql"][:],
                               start=False, stop=(g == NG - 1))
                pe.inc(ins)
            items.append(("tensor", t_mm))

        def s_stats(sc):
            sc.wait_ge(pe.h, pe_b + NG)
            sc.activation(sc_["m"][:], bm.ps[:, 0:256], AF.Copy, scale=INVD)
            acs.inc(sc.activation(sc_["msq"][:], sc_["m"][:], AF.Square))
        items.append(("scalar", s_stats))

        def v_var(v):
            v.wait_ge(acs.h, acs_b + NG + 1)
            dv.inc(v.scalar_tensor_tensor(sc_["var"][:], bq.ps[:, 0:256], INVD,
                                          sc_["msq"][:], ALU.mult, ALU.subtract))
        items.append(("vector", v_var))

        def s_r(sc):
            sc.wait_ge(dv.h, dv_b + NG + 1)
            sc.activation(sc_["r"][:], sc_["var"][:], AF.Ln, bias=EPS)
            ins = sc.activation(sc_["r"][:], sc_["r"][:], AF.Exp, scale=-0.5)
            if with_m2:
                sc.activation(self.m2_t[:], sc_["m"][:], AF.Copy)
                ins = sc.activation(self.r2_t[:], sc_["r"][:], AF.Copy)
            acs.inc(ins)
            bm.freed_by(acs)
            bq.freed_by(acs)
        items.append(("scalar", s_r))
        return items

    def _ln_apply_items(self, xb, dst, fam, r_tok, out_slice=None, pre_wait=None,
                        out_dv=None):
        pe, acs, dv = fam
        sc_ = self.lnscr["F" if fam is self.F else "M"]

        def v_xq(v):
            v.wait_ge(acs.h, r_tok)
            if pre_wait is not None:
                sem, n = pre_wait
                v.wait_ge(sem.h, n)
            ins = None
            for g in range(NG):
                xv = xb[:, g, :]
                v.tensor_tensor(sc_["xm"][:], xv, sc_["m"][:], ALU.subtract)
                if out_slice is None:
                    o = dst[:, g, :]
                else:
                    o = dst[:, g, out_slice]
                ins = v.tensor_tensor(o, sc_["xm"][:], sc_["r"][:], ALU.mult)
            dv.inc(ins)
        return [("vector", v_xq)]

    def phase_ln1(self, s):
        pe, acs, dv = self.F
        xb = self.xbuf[s % 3]
        items = self._ln_stats_items(xb, self.F, (self.bk[4], self.bk[5]),
                                     pe_b=pe.n, acs_b=acs.n, dv_b=dv.n,
                                     x_tok=(dv, dv.n))
        r_tok = acs.n + NG + 2
        pw = (self.pef, self._w01_done.get(s - 1, 0))
        items += self._ln_apply_items(xb, self.x1q, self.F, r_tok, pre_wait=pw)
        x1q_tok = dv.n + (NG + 1) + 1
        for it in items:
            self.sect(*it)
        assert dv.n == x1q_tok, (dv.n, x1q_tok)
        self._x1q_tok = x1q_tok

    def ffn_items(self, s):
        pe, acs, dv = self.F
        dgw = [self.dgwA, self.dgwB]
        dgw2 = self.dgw2
        calls = []
        if s < NCH:
            calls.append((0, self.x1q, self.act[0], s, ("F", self._x1q_tok)))
        if s >= 2:
            calls.append((1, self.x3q[s % 2], self.act[1], s - 2,
                          ("M", self._x3q_tok[s - 2])))
        ncal = len(calls)
        if ncal == 0:
            return []

        dgw_base = [dgw[0].n, dgw[1].n]
        dgw2_base = dgw2.n
        pe_base = pe.n
        acs_base = acs.n
        dv_base = dv.n

        def gidx(i, ci):
            return i * ncal + ci

        G = NSUP * ncal
        items = []

        def g_w(g):
            g.wait_ge(pe.h, pe_base)
            if s == 0:
                g.wait_ge(self.dg0.h, self._dg0_all)
            for i in range(NSUP):
                if i >= 2:
                    g.wait_ge(pe.h, pe_base + (i - 1) * ncal)
                for W in range(2):
                    dgw[i % 2].inc(g.dma_start(
                        self.w01b[:, i % 2, W],
                        self.w01s[W, i].rearrange("kt kp m -> kp kt m")), 16)
            for dj in range(NDJ):
                for hh in range(2):
                    u = 2 * dj + hh
                    if u >= 2:
                        g.wait_ge(pe.h, pe_base + G + u - 1)
                    dgw2.inc(g.dma_start(
                        self.w2b[u % 2][:],
                        self.w2s[dj, hh * 32:(hh + 1) * 32]
                        .rearrange("kh hp d -> hp kh d")), 16)
        items.append(("gpsimd", g_w))

        entry_waits = []
        for c, _, _, _, _ in calls:
            for mt in range(2):
                b = self.bk[c * 2 + mt]
                if b.free_tok is not None:
                    entry_waits.append(b.free_tok)
                    b.free_tok = None

        for i in range(NSUP):
            for ci, (c, xq, actb, cs, rhs_tok) in enumerate(calls):
                g = gidx(i, ci)

                def t_mm(t, i=i, ci=ci, c=c, xq=xq, g=g, rhs_tok=rhs_tok):
                    if i == 0 and ci == 0:
                        for sem, n in entry_waits:
                            t.wait_ge(sem.h, n)
                    if i == 0:
                        fam, n = rhs_tok
                        t.wait_ge((self.dvf if fam == "F" else self.dvm).h, n)
                    t.wait_ge(dgw[i % 2].h, dgw_base[i % 2] + 32 * (i // 2 + 1))
                    if i >= 1:
                        t.wait_ge(acs.h, acs_base + 2 * gidx(i - 1, ci) + 1)
                    ins = None
                    for mt in range(2):
                        b = self.bk[c * 2 + mt]
                        for kt in range(DCN):
                            t.matmul(b.ps[:, 0:256],
                                     self.w01b[:, i % 2, 0, kt, mt * 128:(mt + 1) * 128],
                                     xq[:, kt, :],
                                     start=(kt == 0), stop=False)
                        for kt in range(DCN):
                            ins = t.matmul(b.ps[:, 256:512],
                                           self.w01b[:, i % 2, 1, kt, mt * 128:(mt + 1) * 128],
                                           xq[:, kt, :],
                                           start=False, stop=(kt == DCN - 1))
                    pe.inc(ins)
                items.append(("tensor", t_mm))

                def s_evx2(sc, ci=ci, c=c, g=g):
                    sc.wait_ge(pe.h, pe_base + g + 1)
                    if g >= 1:
                        sc.wait_ge(dv.h, dv_base + 2 * (g - 1) + 2)
                    for mt in range(2):
                        b = self.bk[c * 2 + mt]
                        sc.activation(self.gf16[ci][:, mt, :], b.ps[:], AF.Copy)
                    acs.inc(sc.activation(self.ch_x2[:],
                                          self.gf16[ci][:, :, 0:256], AF.Square))
                items.append(("scalar", s_evx2))

                def v_chA(v, ci=ci, g=g):
                    gv_ = self.gf16[ci][:, :, 0:256]
                    v.wait_ge(acs.h, acs_base + 2 * g + 1)
                    v.tensor_tensor(self.ch_x3[:], self.ch_x2[:], gv_, ALU.mult)
                    v.tensor_scalar_mul(self.ch_t[:], self.ch_x3[:], C1)
                    v.tensor_tensor(self.ch_t[:], self.ch_t[:], gv_, ALU.add)
                    dv.inc(v.tensor_scalar_mul(self.ch_u[:], self.ch_t[:], C2))
                items.append(("vector", v_chA))

                def s_tanh(sc, g=g):
                    sc.wait_ge(dv.h, dv_base + 2 * g + 1)
                    acs.inc(sc.activation(self.ch_t[:], self.ch_u[:], AF.Tanh))
                items.append(("scalar", s_tanh))

                def v_chB(v, i=i, ci=ci, g=g, actb=actb):
                    gv_ = self.gf16[ci][:, :, 0:256]
                    fv_ = self.gf16[ci][:, :, 256:512]
                    if i == 0:
                        v.wait_ge(pe.h, pe_base)
                    v.wait_ge(acs.h, acs_base + 2 * g + 2)
                    v.tensor_scalar(self.ch_cdf[:], self.ch_t[:], 1.0, 0.5,
                                    ALU.add, ALU.mult)
                    v.tensor_tensor(self.ch_gv[:], self.ch_cdf[:], gv_, ALU.mult)
                    v.tensor_tensor(self.ch_a16[:], self.ch_gv[:], fv_, ALU.mult)
                    dv.inc(v.tensor_copy(actb[:, 2 * i:2 * i + 2, :], self.ch_a16[:]))
                items.append(("vector", v_chB))

        chain_done = dv_base + 2 * G
        self._w01_done[s] = pe_base + G
        for c, _, _, _, _ in calls:
            for mt in range(2):
                self.bk[c * 2 + mt].free_tok = (acs, acs_base + 2 * (G - 1) + 1)

        pe2_base = pe_base + G
        acs2_base = acs_base + 2 * G
        w1_evac_done = acs_base + 2 * (G - 1) + 1
        hh0_done = dv_base + 2 * (NSUP // 2) * ncal

        for dj in range(NDJ):
            def t_w2(t, dj=dj):
                if dj <= 1:
                    t.wait_ge(acs.h, w1_evac_done)
                    if dj == 0:
                        t.wait_ge(dv.h, hh0_done)
                if dj >= 2:
                    t.wait_ge(acs.h, acs2_base + (dj - 1) * ncal)
                for hh in range(2):
                    u = 2 * dj + hh
                    t.wait_ge(dgw2.h, dgw2_base + 16 * (u + 1))
                    if dj == 0 and hh == 1:
                        t.wait_ge(dv.h, chain_done)
                    ins = None
                    for ci, (c, xq, actb, cs, _) in enumerate(calls):
                        b = self.bk[c * 2 + dj % 2]
                        for kh in range(32):
                            ins = t.matmul(b.ps[:, 0:256],
                                           self.w2b[u % 2][:, kh, :],
                                           actb[:, hh * 32 + kh, :],
                                           start=(hh == 0 and kh == 0),
                                           stop=(hh == 1 and kh == 31))
                    pe.inc(ins)
            items.append(("tensor", t_w2))

            for ci in range(ncal):
                def s_w2(sc, dj=dj, ci=ci):
                    sc.wait_ge(pe.h, pe2_base + 2 * dj + 2)
                    if dj >= 2:
                        sc.wait_ge(dv.h, chain_done + (dj - 2) * ncal + ci + 1)
                    c = calls[ci][0]
                    acs.inc(sc.activation(self.out16[2 * (dj % 2) + ci][:],
                                          self.bk[c * 2 + dj % 2].ps[:, 0:256],
                                          AF.Copy))
                items.append(("scalar", s_w2))

                def v_w2(v, dj=dj, ci=ci):
                    c, xq, actb, cs, _ = calls[ci]
                    v.wait_ge(acs.h, acs2_base + dj * ncal + ci + 1)
                    xbv = self.xbuf[cs % 3]
                    dv.inc(v.scalar_tensor_tensor(xbv[:, dj, :],
                                                  self.out16[2 * (dj % 2) + ci][:], 0.5,
                                                  xbv[:, dj, :], ALU.mult, ALU.add))
                items.append(("vector", v_w2))

        for ci, (c, xq, actb, cs, _) in enumerate(calls):
            for p in range(2):
                last_dj = NDJ - 2 + p
                self.bk[c * 2 + p].free_tok = (acs, acs2_base + last_dj * ncal + ci + 1)

        self._ffn_pe_end[s] = pe2_base + 2 * NDJ
        end_dv = chain_done + NDJ * ncal
        for ci, (c, xq, actb, cs, _) in enumerate(calls):
            self._x1done[cs] = end_dv
        if s >= 2:
            self._x2done = end_dv
        return items

    def phase_out(self, c):
        pe, dv, ds = self.pef, self.dvf, self.dsoo
        xb = self.xbuf[c % 3]
        done_tok = self._x2done
        pe_base = pe.n
        dv_base = dv.n
        ds_base = ds.n

        def t_tr(t):
            t.wait_ge(dv.h, done_tok)
            k = 0
            for th in range(2):
                for dcg in range(4):
                    k += 1
                    b = self.bk[4 + (th * 4 + dcg) % 2]
                    if k <= 2:
                        b.wait_free(t)
                    else:
                        t.wait_ge(dv.h, dv_base + (k - 2))
                    for j in range(4):
                        dc = dcg * 4 + j
                        ins = t.transpose(b.ps[:, j * 128:(j + 1) * 128],
                                          xb[:, dc, th * 128:(th + 1) * 128],
                                          self.id_s[:])
                    pe.inc(ins)

        self.sect("tensor", t_tr)

        def v_evac(v):
            k = 0
            for th in range(2):
                for dcg in range(4):
                    k += 1
                    if k == 5:
                        v.wait_ge(ds.h, ds_base + 16)
                    b = self.bk[4 + (th * 4 + dcg) % 2]
                    v.wait_ge(pe.h, pe_base + k)
                    ins = v.tensor_copy(
                        self.xio[:, dcg * 512:(dcg + 1) * 512]
                        .rearrange("p (a t) -> p a t", a=4),
                        b.ps[:].rearrange("p (a t) -> p a t", a=4))
                    dv.inc(ins)
                    b.freed_by(dv)

        self.sect("vector", v_evac)

        def s_out(sy):
            for th in range(2):
                sy.wait_ge(dv.h, dv_base + 4 * (th + 1))
                ds.inc(sy.dma_start(
                    self.out[c * TC + th * 128: c * TC + (th + 1) * 128, :],
                    self.xio[:]), 16)

        self.sect("sync", s_out)
        self._out_done[c] = ds.n

    def mamba_items(self, m):
        pe, acs, dv = self.M
        xb = self.xbuf[m % 3]
        items = []

        pe0, ac0, dv0 = pe.n, acs.n, dv.n

        x_tok = (self.dvf, self._x1done[m])
        items += self._ln_stats_items(xb, self.M, (self.bk[6], self.bk[7]),
                                      pe_b=pe0, acs_b=ac0, dv_b=dv0,
                                      x_tok=x_tok, with_m2=True)
        pe1 = pe0 + NG
        ac1 = ac0 + NG + 2
        dv1 = dv0 + NG + 1
        r_tok = ac1

        def v_halo(v):
            v.wait_ge(acs.h, r_tok)
            dv.inc(v.tensor_copy(self.xh2[:, :, 1:4],
                                 self.xh2[:, :, TC + 1:TC + 4]))
        items.append(("vector", v_halo))
        dv2 = dv1 + 1

        items += self._ln_apply_items(xb, self.xh2, self.M, r_tok,
                                      out_slice=slice(4, 4 + TC))
        xh2_tok = dv2 + 1

        acs_cv = ac1
        for dc in range(DCN):
            def v_taps(v, dc=dc):
                if dc == 0:
                    v.wait_ge(dv.h, xh2_tok)
                else:
                    v.wait_ge(acs.h, acs_cv + dc)
                v.tensor_scalar_mul(self.xcb[:], self.xh2[:, dc, 1:1 + TC],
                                    self.convw_s[:, dc, 0:1])
                for j in range(1, 4):
                    v.scalar_tensor_tensor(self.xcb[:],
                                           self.xh2[:, dc, 1 + j:1 + j + TC],
                                           self.convw_s[:, dc, j:j + 1],
                                           self.xcb[:], ALU.mult, ALU.add)
                dv.inc(v.tensor_scalar_add(self.xcb[:], self.xcb[:],
                                           self.convb_s[:, dc:dc + 1]))
            items.append(("vector", v_taps))

            def s_tanh(sc, dc=dc):
                sc.wait_ge(dv.h, xh2_tok + 2 * dc + 1)
                acs.inc(sc.activation(self.th_s[:], self.xcb[:], AF.Tanh, scale=0.5))
            items.append(("scalar", s_tanh))

            def v_silu(v, dc=dc):
                v.wait_ge(acs.h, acs_cv + dc + 1)
                v.scalar_tensor_tensor(self.u_s[:], self.th_s[:], 1.0, self.xcb[:],
                                       ALU.add, ALU.mult)
                dv.inc(v.tensor_scalar_mul(self.xconv[:, dc, :], self.u_s[:], 0.5))
            items.append(("vector", v_silu))

        dv3 = xh2_tok + 32
        ac2 = acs_cv + 16

        bdt, bbc = self.bk[6], self.bk[7]
        pe_pj = pe1

        def t_proj(t):
            t.wait_ge(dv.h, dv3)
            if m == 0:
                t.wait_ge(self.dsw.h, self._projw_tok)
            bdt.wait_free(t)
            bbc.wait_free(t)
            ins = None
            for kt in range(DCN):
                t.matmul(bdt.ps[:, 0:256], self.xproj_s[:, kt, 0:128], self.xconv[:, kt, :],
                         start=(kt == 0), stop=(kt == DCN - 1))
                ins = t.matmul(bbc.ps[0:32, 0:256], self.xproj_s[:, kt, 128:160],
                               self.xconv[:, kt, :],
                               start=(kt == 0), stop=(kt == DCN - 1))
            pe.inc(ins)
        items.append(("tensor", t_proj))

        def v_pj(v):
            v.wait_ge(pe.h, pe_pj + 1)
            v.tensor_copy(self.dt16[:], bdt.ps[:, 0:256])
            dv.inc(v.tensor_copy(self.bc16[0:32, :], bbc.ps[0:32, 0:256]))
            bdt.freed_by(dv)
            bbc.freed_by(dv)
        items.append(("vector", v_pj))
        dv_pj = dv3 + 1

        pe_d = pe_pj + 1
        acs_d = ac2
        dv_d = dv_pj
        for dc in range(DCN):
            def t_delta(t, dc=dc):
                b = self.bk[6 + dc % 2]
                if dc < 2:
                    t.wait_ge(dv.h, dv_pj)
                else:
                    t.wait_ge(dv.h, dv_d + 3 * (dc - 2) + 2)
                pe.inc(t.matmul(b.ps[:, 0:256], self.dtproj_s[:, dc, :], self.dt16[:],
                                start=True, stop=True))
            items.append(("tensor", t_delta))

            def s_z(sc, dc=dc):
                b = self.bk[6 + dc % 2]
                sc.wait_ge(pe.h, pe_d + dc + 1)
                if dc >= 1:
                    sc.wait_ge(dv.h, dv_d + 3 * (dc - 1) + 2)
                acs.inc(sc.activation(self.z_s[:], b.ps[:, 0:256], AF.Identity,
                                      bias=self.dtb_s[:, dc:dc + 1]))
            items.append(("scalar", s_z))

            def v_az(v, dc=dc):
                v.wait_ge(acs.h, acs_d + 2 * dc + 1)
                dv.inc(v.scalar_tensor_tensor(self.az_s[:], self.z_s[:], -1.0, self.z_s[:],
                                              ALU.mult, ALU.max))
            items.append(("vector", v_az))

            def s_expln(sc, dc=dc):
                sc.wait_ge(dv.h, dv_d + 3 * dc + 1)
                sc.activation(self.e_s[:], self.az_s[:], AF.Exp, scale=-1.0)
                acs.inc(sc.activation(self.e_s[:], self.e_s[:], AF.Ln, bias=1.0))
            items.append(("scalar", s_expln))

            def v_dl(v, dc=dc):
                v.wait_ge(acs.h, acs_d + 2 * dc + 2)
                dv.inc(v.scalar_tensor_tensor(self.delta[:, dc, :], self.z_s[:], 0.0,
                                              self.e_s[:], ALU.max, ALU.add))
                dv.inc(v.tensor_tensor(self.wtd[:, dc, :], self.delta[:, dc, :],
                                       self.xconv[:, dc, :], ALU.mult))
            items.append(("vector", v_dl))

        delta_tok = dv_d + 3 * DCN
        ac3 = acs_d + 2 * DCN
        pe3 = pe_d + DCN
        tok_y = [(dv, dv_d + 3 * (DCN - 2) + 2), (dv, dv_d + 3 * (DCN - 1) + 2)]

        dsbc = self.dsbc
        bc_base = dsbc.n

        def s_bc(sy):
            sy.wait_ge(dv.h, delta_tok)
            sy.wait_ge(pe.h, pe_pj + 1)
            for np_ in range(NP):
                for n2 in range(2):
                    dsbc.inc(sy.dma_start(
                        self.Bbc[n2 * 64:(n2 + 1) * 64, np_, :],
                        self.bc16[2 * np_ + n2:2 * np_ + n2 + 1, :].unsqueeze(1)
                        .broadcast_to([1, 64, TC])), 16)
                    dsbc.inc(sy.dma_start(
                        self.Cbc[n2 * 64:(n2 + 1) * 64, np_, :],
                        self.bc16[16 + 2 * np_ + n2:16 + 2 * np_ + n2 + 1, :].unsqueeze(1)
                        .broadcast_to([1, 64, TC])), 16)
        items.append(("sync", s_bc))
        bc_tok = bc_base + 16 * 32

        dsp = [self.dse, self.dso]
        dsp_base = [self.dse.n, self.dso.n]
        acs_sc = ac3
        scan_dv = delta_tok
        pe_y = pe3

        def sdv(nscan, ny):
            return scan_dv + 5 * nscan + ny

        def after_scan(dblk):
            return sdv(dblk + 1, dblk // 2)

        def after_y(dc):
            return sdv(2 * dc + 2, dc + 1)

        def s_ddn(sy):
            for dblk in range(NDBLK):
                par = dblk % 2
                if dblk >= 2:
                    sy.wait_ge(dv.h, after_scan(dblk - 2))
                    sy.wait_ge(acs.h, acs_sc + (dblk - 1))
                p0 = 64 * (dblk % 2)
                dc = dblk // 2
                for n2 in range(2):
                    dsp[par].inc(sy.dma_start(
                        self.ddn[par][n2 * 64:(n2 + 1) * 64, :],
                        self.delta[p0:p0 + 64, dc, :]), 16)
                    dsp[par].inc(sy.dma_start(
                        self.wdn[par][n2 * 64:(n2 + 1) * 64, :],
                        self.wtd[p0:p0 + 64, dc, :]), 16)
        items.append(("sync", s_ddn))

        for dblk in range(NDBLK):
            par = dblk % 2

            def s_exp(sc, dblk=dblk, par=par):
                sc.wait_ge(dsp[par].h, dsp_base[par] + 64 * (dblk // 2 + 1))
                if dblk >= 1:
                    sc.wait_ge(dv.h, after_scan(dblk - 1))
                    sc.wait_ge(pe.h, pe_y + dblk)
                ins = None
                for np_ in range(NP):
                    ins = sc.activation(self.a_g[:, np_, 1:SEPW], self.ddn[par][:],
                                        AF.Exp,
                                        scale=self.A_perm[:, dblk * 8 + np_:dblk * 8 + np_ + 1])
                acs.inc(ins)
            items.append(("scalar", s_exp))

            def v_scan(v, dblk=dblk, par=par):
                if dblk == 0:
                    v.wait_ge(dsbc.h, bc_tok)
                v.wait_ge(acs.h, acs_sc + dblk + 1)
                if dblk >= 1:
                    v.wait_ge(pe.h, pe_y + dblk)
                v.tensor_copy(self.b_g[:, :, 0:1],
                              self.h_state[:, dblk * 8:dblk * 8 + 8].unsqueeze(2))
                v.tensor_tensor(self.b_g[:, :, 1:SEPW],
                                self.wdn[par][:].unsqueeze(1).broadcast_to([128, NP, TC]),
                                self.Bbc[:], ALU.mult)
                v.tensor_tensor_scan(self.h_g[:].rearrange("p a t -> p (a t)"),
                                     self.a_g[:].rearrange("p a t -> p (a t)"),
                                     self.b_g[:].rearrange("p a t -> p (a t)"),
                                     0.0, ALU.mult, ALU.add)
                v.tensor_copy(self.h_state[:, dblk * 8:dblk * 8 + 8],
                              self.h_g[:, :, SEPW - 1:SEPW].squeeze(2))
                ins = v.tensor_tensor(self.a_g[:, :, 1:SEPW], self.h_g[:, :, 1:SEPW],
                                      self.Cbc[:], ALU.mult)
                dv.inc(ins, 5)
            items.append(("vector", v_scan))

            def t_red(t, dblk=dblk):
                t.wait_ge(dv.h, after_scan(dblk))
                dc = dblk // 2
                b = self.bk[6 + dc % 2]
                if dblk % 2 == 0:
                    if dc < 2:
                        if tok_y[dc % 2] is not None:
                            sem, n = tok_y[dc % 2]
                            t.wait_ge(sem.h, n)
                            tok_y[dc % 2] = None
                    else:
                        t.wait_ge(dv.h, after_y(dc - 2))
                R = self.re_s if dblk % 2 == 0 else self.ro_s
                ins = None
                for np_ in range(NP):
                    ins = t.matmul(b.ps[:, 0:256], R[:], self.a_g[:, np_, 1:SEPW],
                                   start=(dblk % 2 == 0 and np_ == 0),
                                   stop=(dblk % 2 == 1 and np_ == NP - 1))
                pe.inc(ins)
            items.append(("tensor", t_red))

            if dblk % 2 == 1:
                def v_y(v, dblk=dblk):
                    dc = dblk // 2
                    b = self.bk[6 + dc % 2]
                    v.wait_ge(pe.h, pe_y + 2 * dc + 2)
                    v.tensor_tensor(self.ytmp[:], xb[:, dc, :], self.m2_t[:],
                                    ALU.subtract)
                    v.scalar_tensor_tensor(self.ytmp2[:], self.ytmp[:],
                                           self.Dp_s[:, dc:dc + 1], self.r2_t[:],
                                           ALU.mult, ALU.mult)
                    v.tensor_tensor(self.ytmp[:], self.ytmp2[:], b.ps[:, 0:256],
                                    ALU.add)
                    dv.inc(v.tensor_tensor(xb[:, dc, :], xb[:, dc, :], self.ytmp[:],
                                           ALU.add))
                    b.freed_by(dv)
                items.append(("vector", v_y))

        dv4 = scan_dv + 5 * NDBLK + DCN
        pe4 = pe_y + NDBLK
        ac4 = acs_sc + NDBLK

        items += self._ln_stats_items(xb, self.M, (self.bk[6], self.bk[7]),
                                      pe_b=pe4, acs_b=ac4, dv_b=dv4)
        r3_tok = ac4 + NG + 2
        pwait = (self.pef, self._ffn_pe_end.get(m, 0))
        items += self._ln_apply_items(xb, self.x3q[m % 2], self.M, r3_tok,
                                      pre_wait=pwait)
        self._x3q_tok[m] = dv4 + NG + 1 + 1
        self._expect_m = (pe4 + NG, ac4 + NG + 2, dv4 + NG + 1 + 1)
        return items

    def finalize(self):
        sems = [self.dg0, self.dgwA, self.dgwB, self.dgw2, self.dsbc,
                self.dse, self.dso, self.dsx, self.dsoo, self.dsw]
        tgts = [(s.h, s.n) for s in sems]

        def s_fin(sy):
            for h, n in tgts:
                sy.wait_ge(h, n)

        self.sect("sync", s_fin)


def build_program():
    b = Builder()
    with b.es:
        return b.build()


_NC_CACHE = None


def kernel(L0_attn_x, L0_attn_conv_w, L0_attn_conv_b, L0_attn_xproj_w,
           L0_attn_dtproj_w, L0_attn_dtproj_b, L0_attn_A_log, L0_attn_D_param,
           L0_ffn_w_gating, L0_ffn_w_linear):
    global _NC_CACHE
    X = np.ascontiguousarray(np.asarray(L0_attn_x, np.float32))
    consts = {
        "conv_w": np.ascontiguousarray(np.asarray(L0_attn_conv_w, np.float32)),
        "conv_b": np.ascontiguousarray(np.asarray(L0_attn_conv_b, np.float32)),
        "xproj_w": np.ascontiguousarray(np.asarray(L0_attn_xproj_w, np.float32)),
        "dtproj_w": np.ascontiguousarray(np.asarray(L0_attn_dtproj_w, np.float32)),
        "dtproj_b": np.ascontiguousarray(np.asarray(L0_attn_dtproj_b, np.float32)),
        "A_perm_log": np.ascontiguousarray(
            np.asarray(L0_attn_A_log, np.float32)
            .reshape(32, 64, 8, 2).transpose(3, 1, 0, 2).reshape(128, 256)),
        "D_param": np.ascontiguousarray(np.asarray(L0_attn_D_param, np.float32)),
        "w_gating": np.ascontiguousarray(np.asarray(L0_ffn_w_gating, np.float32)),
        "w_linear": np.ascontiguousarray(np.asarray(L0_ffn_w_linear, np.float32)),
        "ident": np.eye(128, dtype=np.float32),
        "fconst": np.tile(np.array([[EPS, 1.0, -0.5, -1.0, 0.5, INVD, 0.0, 0.0]],
                                   np.float32), (128, 1)),
        "ones16": np.ones((128, 128), dtype=ml_dtypes.bfloat16),
    }
    re_ = np.zeros((128, 128), np.float32)
    ro_ = np.zeros((128, 128), np.float32)
    for p in range(128):
        re_[p, p % 64] = 1.0
        ro_[p, 64 + (p % 64)] = 1.0
    consts["r_even"] = re_.astype(ml_dtypes.bfloat16)
    consts["r_odd"] = ro_.astype(ml_dtypes.bfloat16)

    if _NC_CACHE is None:
        _NC_CACHE = build_program()
    nc = _NC_CACHE
    in_maps = [dict(consts, x=np.ascontiguousarray(X[b])) for b in range(8)]
    res = run_bass_kernel_spmd(nc, in_maps, core_ids=list(range(8)))
    return np.stack([np.asarray(res.results[b]["out"], np.float32) for b in range(8)],
                    axis=0)


# version 05
# speedup vs baseline: 1.1784x; 1.0034x over previous
import contextlib
import numpy as np
import ml_dtypes
import concourse.bass as bass
import concourse.mybir as mybir
from concourse.bass_utils import run_bass_kernel_spmd

F32 = mybir.dt.float32
BF16 = mybir.dt.bfloat16
E4 = mybir.dt.float8e4
AF = mybir.ActivationFunctionType
ALU = mybir.AluOpType

T, D, H = 2048, 2048, 8192
NS, RNK = 16, 128
TC = 256
NCH = T // TC
DCN = D // 128
HCN = H // 128
NSUP = 32
NDJ = 16
NDBLK = 32
NP = 8
SEPW = 257
EPS = 1e-5
C1 = float(np.float32(ml_dtypes.bfloat16(0.044715)))
C2 = float(np.float32(ml_dtypes.bfloat16(np.sqrt(2.0 / np.pi))))
INVD = 1.0 / D
NG = 16


class Sem:
    def __init__(self, es, nc, name):
        self.h = es.enter_context(nc.semaphore(name))
        self.n = 0

    def inc(self, inst, amt=1):
        inst.then_inc(self.h, amt)
        self.n += amt
        return inst


class Bank:
    def __init__(self, es, nc, name):
        self.ps = es.enter_context(nc.psum_tensor(name, [128, 512], F32))
        self.free_tok = None

    def freed_by(self, sem):
        self.free_tok = (sem, sem.n)

    def wait_free(self, eng):
        if self.free_tok is not None:
            sem, n = self.free_tok
            eng.wait_ge(sem.h, n)
            self.free_tok = None


class Builder:
    def __init__(self):
        self.nc = bass.Bass("TRN2", target_bir_lowering=False,
                            detect_race_conditions=False)
        self.es = contextlib.ExitStack()
        self._out_done = {}
        self._x1done = {}
        self._ffn_pe_end = {}
        self._w01_done = {}
        self._x3q_tok = {}

    def sect(self, eng, fn):
        getattr(self.block, eng)(fn)

    def _tbl(self, sc):
        sc.add_instruction(mybir.InstLoadActFuncSet(
            name=self.nc.get_next_instruction_name(),
            act_func_set_id=6, ins=[], outs=[]))

    def _overlay(self, name, shape, dt, base_handle, byte_off):
        nc = self.nc
        h = nc.virtual_sbuf_tensor(name, shape, dt)
        nc._tensor(h.name, h.shape, h.dtype, type="SB")
        mloc = nc.lookup_mloc(h)
        mloc.allocated = True
        mloc.addr = nc.lookup_mloc(base_handle).addr + byte_off
        return h

    def run_items(self, *lists):
        lists = [l for l in lists if l]
        if not lists:
            return
        main = max(lists, key=len)
        rest = [l for l in lists if l is not main]
        n = len(main)
        idxs = [0] * len(rest)
        for k, (eng, fn) in enumerate(main):
            self.sect(eng, fn)
            for li, l in enumerate(rest):
                want = (k + 1) * len(l) // n
                while idxs[li] < want:
                    e2, f2 = l[idxs[li]]
                    self.sect(e2, f2)
                    idxs[li] += 1
        for li, l in enumerate(rest):
            while idxs[li] < len(l):
                e2, f2 = l[idxs[li]]
                self.sect(e2, f2)
                idxs[li] += 1

    def build(self):
        nc, es = self.nc, self.es
        self.x_in = nc.dram_tensor("x", [T, D], F32, kind="ExternalInput").ap()
        self.conv_w = nc.dram_tensor("conv_w", [D, 1, 4], F32, kind="ExternalInput").ap()
        self.conv_b = nc.dram_tensor("conv_b", [D], F32, kind="ExternalInput").ap()
        self.xproj_w = nc.dram_tensor("xproj_w", [160, D], F32, kind="ExternalInput").ap()
        self.dtproj_w = nc.dram_tensor("dtproj_w", [D, RNK], F32, kind="ExternalInput").ap()
        self.dtproj_b = nc.dram_tensor("dtproj_b", [D], F32, kind="ExternalInput").ap()
        self.A_permin = nc.dram_tensor("A_perm_log", [128, 256], F32, kind="ExternalInput").ap()
        self.D_param = nc.dram_tensor("D_param", [D], F32, kind="ExternalInput").ap()
        self.wg = nc.dram_tensor("w_gating", [2, D, H], F32, kind="ExternalInput").ap()
        self.wl = nc.dram_tensor("w_linear", [H, D], F32, kind="ExternalInput").ap()
        self.ident = nc.dram_tensor("ident", [128, 128], F32, kind="ExternalInput").ap()
        self.ones16 = nc.dram_tensor("ones16", [128, 128], BF16, kind="ExternalInput").ap()
        self.r_even = nc.dram_tensor("r_even", [128, 128], BF16, kind="ExternalInput").ap()
        self.r_odd = nc.dram_tensor("r_odd", [128, 128], BF16, kind="ExternalInput").ap()
        self.fconst = nc.dram_tensor("fconst", [128, 8], F32, kind="ExternalInput").ap()
        self.out = nc.dram_tensor("out", [T, D], F32, kind="ExternalOutput").ap()
        self.w01s = nc.dram_tensor("w01s", [2, NSUP, 16, 128, 256], E4).ap()
        self.w2s = nc.dram_tensor("w2s", [NDJ, HCN, 128, 128], E4).ap()
        self.xproj16 = nc.dram_tensor("xproj16", [160, D], BF16).ap()
        self.dtproj16 = nc.dram_tensor("dtproj16", [D, RNK], BF16).ap()

        sb = lambda name, shape, dt: es.enter_context(nc.sbuf_tensor(name, shape, dt))
        self.xbuf = [sb(f"xbuf{i}", [128, DCN, TC], F32) for i in range(3)]
        self.xio = sb("xio", [128, 2048], F32)
        self.x1q = sb("x1q", [128, DCN, TC], E4)
        self.x3q = [sb(f"x3q{i}", [128, DCN, TC], E4) for i in range(2)]
        _gf = sb("gf16", [128, 2, 512], BF16)
        self.gf16 = [_gf, _gf]
        self.ch_x2 = sb("ch_x2", [128, 2, 256], BF16)
        self.ch_t = sb("ch_t", [128, 2, 256], BF16)
        self.ch_cdf = sb("ch_cdf", [128, 2, 256], BF16)
        self.ch_gv = sb("ch_gv", [128, 2, 256], BF16)
        self.ch_a16 = self.ch_t
        self.ch_x3 = self.ch_cdf
        self.ch_u = self.ch_gv
        self.act = [sb(f"act{c}", [128, HCN, TC], E4) for c in range(2)]
        self.w01b = sb("w01b", [128, 2, 2, 16, 256], E4)
        self.w2b = [sb("w2b0", [128, HCN // 2, 128], E4),
                    self._overlay("w2b1", [128, HCN // 2, 128], E4,
                                  self.xio, 2048)]
        self.out16 = [self._overlay(f"out16_{i}", [128, TC], BF16,
                                    self.xio, 512 * i) for i in range(4)]
        self.lnscr = {}
        for fam in ("F", "M"):
            d = dict(
                q=sb(f"q_{fam}", [128, 256], F32),
                qh=sb(f"qh_{fam}", [128, 256], BF16),
                ql=sb(f"ql_{fam}", [128, 256], BF16),
                xh=sb(f"xh_{fam}", [128, 256], BF16),
                xl=sb(f"xl_{fam}", [128, 256], BF16),
                m=sb(f"m_{fam}", [128, TC], F32),
                msq=sb(f"msq_{fam}", [128, TC], F32),
                r=sb(f"r_{fam}", [128, TC], F32),
            )
            d["var"] = d["msq"]
            d["xm"] = d["q"]
            self.lnscr[fam] = d
        self.m2_t = sb("m2_t", [128, TC], F32)
        self.r2_t = sb("r2_t", [128, TC], F32)
        self.xh2 = sb("xh2", [128, DCN, TC + 4], BF16)
        self.xcb = sb("xcb", [128, TC], F32)
        self.th_s = sb("th_s", [128, TC], F32)
        self.u_s = self.lnscr["M"]["q"]
        self.xconv = sb("xconv", [128, DCN, TC], BF16)
        self.Bbc = self.xconv[:, 0:NP, :]
        self.Cbc = self.xconv[:, NP:2 * NP, :]
        self.dt16 = sb("dt16", [128, TC], BF16)
        self.bc16 = sb("bc16", [128, TC], BF16)
        self.delta = sb("delta", [128, DCN, TC], F32)
        self.wtd = sb("wtd", [128, DCN, TC], BF16)
        self.z_s = self.xcb
        self.az_s = self.u_s
        self.e_s = self.th_s
        self.ytmp = self.xcb
        self.ytmp2 = self.th_s
        self.ddn = [sb(f"ddn{i}", [128, TC], F32) for i in range(2)]
        self.wdn = [sb(f"wdn{i}", [128, TC], BF16) for i in range(2)]
        self.a_g = sb("a_g", [128, NP, SEPW], BF16)
        self.b_g = sb("b_g", [128, NP, SEPW], BF16)
        self.h_g = self.b_g
        self.h_state = sb("h_state", [128, 256], F32)
        self.A_perm = sb("A_perm", [128, 256], F32)
        self.fconst_s = sb("fconst_s", [128, 8], F32)
        self.id_s = sb("id_s", [128, 128], F32)
        self.ones_s = sb("ones_s", [128, 128], BF16)
        self.re_s = sb("re_s", [128, 128], BF16)
        self.ro_s = sb("ro_s", [128, 128], BF16)
        self.convw_s = sb("convw_s", [128, DCN, 4], F32)
        self.convb_s = sb("convb_s", [128, DCN], F32)
        self.dtb_s = sb("dtb_s", [128, DCN], F32)
        self.Dp_s = sb("Dp_s", [128, DCN], F32)
        self.xproj_s = sb("xproj_s", [128, DCN, 160], BF16)
        self.dtproj_s = sb("dtproj_s", [128, DCN, 128], BF16)

        for j, val in enumerate([EPS, 1.0, -0.5, -1.0, 0.5, INVD]):
            nc.const_aps.register(F32, val, self.fconst_s[:, j:j + 1])

        self.bk = [Bank(es, nc, f"bk{i}") for i in range(8)]

        self.dg0 = Sem(es, nc, "dg0")
        self.dgwA = Sem(es, nc, "dgwA")
        self.dgwB = Sem(es, nc, "dgwB")
        self.dgw2 = Sem(es, nc, "dgw2")
        self.dsbc = Sem(es, nc, "dsbc")
        self.dse = Sem(es, nc, "dse")
        self.dso = Sem(es, nc, "dso")
        self.dsx = Sem(es, nc, "dsx")
        self.dsoo = Sem(es, nc, "dsoo")
        self.dsw = Sem(es, nc, "dsw")
        self.pef = Sem(es, nc, "pef")
        self.acf = Sem(es, nc, "acf")
        self.dvf = Sem(es, nc, "dvf")
        self.pem = Sem(es, nc, "pem")
        self.acm = Sem(es, nc, "acm")
        self.dvm = Sem(es, nc, "dvm")
        self.F = (self.pef, self.acf, self.dvf)
        self.M = (self.pem, self.acm, self.dvm)

        self.block = es.enter_context(nc.Block())
        self.emit()
        self.finalize()
        return nc

    def emit(self):
        self.phase0()
        for s in range(NCH + 2):
            if s < NCH:
                self.phase_xload(s)
                self.phase_ln1(s)
            ffn_it = self.ffn_items(s)
            mam_it = self.mamba_items(s - 1) if 1 <= s <= NCH else []
            if s >= NCH - 1:
                for it in mam_it:
                    self.sect(*it)
                for it in ffn_it:
                    self.sect(*it)
            else:
                self.run_items(ffn_it, mam_it)
            if mam_it:
                got = (self.pem.n, self.acm.n, self.dvm.n)
                assert got == self._expect_m, (s, got, self._expect_m)
            if s >= 2:
                self.phase_out(s - 2)

    def phase0(self):
        dg0, dsw, acs, dv = self.dg0, self.dsw, self.acf, self.dvf

        def g_loads(g):
            for W in range(2):
                for ms in range(NSUP):
                    src = self.wg[W, :, ms * 256:(ms + 1) * 256] \
                        .rearrange("(kt kp) m -> kt kp m", kp=128)
                    dg0.inc(g.dma_start(self.w01s[W, ms], src), 16)
            for dj in range(NDJ):
                src = self.wl[:, dj * 128:(dj + 1) * 128] \
                    .rearrange("(kh hp) d -> kh hp d", hp=128)
                dg0.inc(g.dma_start(self.w2s[dj], src), 16)
            dg0.inc(g.dma_start(self.xproj16, self.xproj_w), 16)
            dg0.inc(g.dma_start(self.dtproj16, self.dtproj_w), 16)
            dg0.inc(g.dma_start(self.convw_s[:],
                                self.conv_w.squeeze(1).rearrange("(dc p) j -> p dc j", p=128)), 16)
            dg0.inc(g.dma_start(self.convb_s[:],
                                self.conv_b.rearrange("(dc p) -> p dc", p=128),
                                allow_slow_non_contiguous=True), 16)
            dg0.inc(g.dma_start(self.dtb_s[:],
                                self.dtproj_b.rearrange("(dc p) -> p dc", p=128),
                                allow_slow_non_contiguous=True), 16)
            dg0.inc(g.dma_start(self.Dp_s[:],
                                self.D_param.rearrange("(dc p) -> p dc", p=128),
                                allow_slow_non_contiguous=True), 16)
            dg0.inc(g.dma_start(self.A_perm[:], self.A_permin), 16)
            dg0.inc(g.dma_start(self.fconst_s[:], self.fconst), 16)
            dg0.inc(g.dma_start(self.id_s[:], self.ident), 16)
            dg0.inc(g.dma_start(self.ones_s[:], self.ones16), 16)
            dg0.inc(g.dma_start(self.re_s[:], self.r_even), 16)
            dg0.inc(g.dma_start(self.ro_s[:], self.r_odd), 16)

        self.sect("gpsimd", g_loads)
        self._dg0_all = dg0.n

        def s_loads(sy):
            sy.wait_ge(dg0.h, self._dg0_all)
            for kt in range(DCN):
                dsw.inc(sy.dma_start(self.xproj_s[:, kt, :],
                                     self.xproj16[:, kt * 128:(kt + 1) * 128],
                                     transpose=True), 16)
            for dc in range(DCN):
                dsw.inc(sy.dma_start(self.dtproj_s[:, dc, :],
                                     self.dtproj16[dc * 128:(dc + 1) * 128, :],
                                     transpose=True), 16)

        self.sect("sync", s_loads)
        self._projw_tok = dsw.n

        def s_negexp(sc):
            sc.wait_ge(dg0.h, self._dg0_all)
            acs.inc(sc.activation(self.A_perm[:], self.A_perm[:], AF.Exp))

        self.sect("scalar", s_negexp)
        acs_tok = acs.n

        def v_init(v):
            v.wait_ge(acs.h, acs_tok)
            v.tensor_scalar_mul(self.A_perm[:], self.A_perm[:], -1.0)
            v.memset(self.h_state[:], 0.0)
            v.memset(self.a_g[:].rearrange("p a t -> p (a t)"), 0.0)
            dv.inc(v.memset(self.xh2[:].rearrange("p a t -> p (a t)"), 0.0))

        self.sect("vector", v_init)
        self._init_dv = dv.n

    def phase_xload(self, s):
        ds, pe, dv = self.dsx, self.pef, self.dvf
        xb = self.xbuf[s % 3]
        ds_base = ds.n
        pe_base = pe.n
        dv_all = dv.n

        def s_load(sy):
            if s >= 3:
                sy.wait_ge(self.dsoo.h, self._out_done[s - 3])
            if s >= 1:
                sy.wait_ge(pe.h, pe_base)
                sy.wait_ge(dv.h, dv_all)
            ds.inc(sy.dma_start(self.xio[:],
                                self.x_in[s * TC: s * TC + 128, :]), 16)
            sy.wait_ge(pe.h, pe_base + 4)
            ds.inc(sy.dma_start(self.xio[:],
                                self.x_in[s * TC + 128: s * TC + 256, :]), 16)

        self.sect("sync", s_load)

        def t_tr(t):
            t.wait_ge(dv.h, dv_all)
            if s == 0:
                t.wait_ge(dv.h, self._init_dv)
            k = 0
            for th in range(2):
                t.wait_ge(ds.h, ds_base + 16 * (th + 1))
                for dcg in range(4):
                    k += 1
                    b = self.bk[4 + (th * 4 + dcg) % 2]
                    if k <= 2:
                        b.wait_free(t)
                    else:
                        t.wait_ge(dv.h, dv_all + (k - 2))
                    for j in range(4):
                        dc = dcg * 4 + j
                        ins = t.transpose(b.ps[:, j * 128:(j + 1) * 128],
                                          self.xio[:, dc * 128:(dc + 1) * 128],
                                          self.id_s[:])
                    pe.inc(ins)

        self.sect("tensor", t_tr)

        def v_evac(v):
            k = 0
            for th in range(2):
                for dcg in range(4):
                    k += 1
                    b = self.bk[4 + (th * 4 + dcg) % 2]
                    v.wait_ge(pe.h, pe_base + k)
                    ins = v.tensor_copy(
                        xb[:, dcg * 4:(dcg + 1) * 4, th * 128:(th + 1) * 128],
                        b.ps[:].rearrange("p (a t) -> p a t", a=4))
                    dv.inc(ins)
                    b.freed_by(dv)

        self.sect("vector", v_evac)

    def _ln_stats_items(self, xb, fam, banks, pe_b, acs_b, dv_b, x_tok=None,
                        with_m2=False):
        pe, acs, dv = fam
        bm, bq = banks
        sc_ = self.lnscr["F" if fam is self.F else "M"]
        items = []
        for g in range(NG):
            def s_sq(sc, g=g):
                if g == 0 and x_tok is not None:
                    sem, n = x_tok
                    sc.wait_ge(sem.h, n)
                if g >= 1:
                    sc.wait_ge(pe.h, pe_b + g)
                    sc.wait_ge(dv.h, dv_b + g)
                xv = xb[:, g, :]
                sc.activation(sc_["q"][:], xv, AF.Square)
                acs.inc(sc.activation(sc_["qh"][:], sc_["q"][:], AF.Copy))
            items.append(("scalar", s_sq))

            def v_hl(v, g=g):
                if g >= 1:
                    v.wait_ge(pe.h, pe_b + g)
                xv = xb[:, g, :]
                v.wait_ge(acs.h, acs_b + g + 1)
                v.scalar_tensor_tensor(sc_["ql"][:], sc_["qh"][:], -1.0, sc_["q"][:],
                                       ALU.mult, ALU.add)
                v.tensor_copy(sc_["xh"][:], xv)
                dv.inc(v.scalar_tensor_tensor(sc_["xl"][:], sc_["xh"][:], -1.0, xv,
                                              ALU.mult, ALU.add))
            items.append(("vector", v_hl))

            def t_mm(t, g=g):
                if g == 0:
                    bm.wait_free(t)
                    bq.wait_free(t)
                t.wait_ge(dv.h, dv_b + g + 1)
                t.matmul(bm.ps[:, 0:256], self.ones_s[:], sc_["xh"][:],
                         start=(g == 0), stop=False)
                t.matmul(bm.ps[:, 0:256], self.ones_s[:], sc_["xl"][:],
                         start=False, stop=(g == NG - 1))
                t.matmul(bq.ps[:, 0:256], self.ones_s[:], sc_["qh"][:],
                         start=(g == 0), stop=False)
                ins = t.matmul(bq.ps[:, 0:256], self.ones_s[:], sc_["ql"][:],
                               start=False, stop=(g == NG - 1))
                pe.inc(ins)
            items.append(("tensor", t_mm))

        def s_stats(sc):
            sc.wait_ge(pe.h, pe_b + NG)
            sc.activation(sc_["m"][:], bm.ps[:, 0:256], AF.Copy, scale=INVD)
            acs.inc(sc.activation(sc_["msq"][:], sc_["m"][:], AF.Square))
        items.append(("scalar", s_stats))

        def v_var(v):
            v.wait_ge(acs.h, acs_b + NG + 1)
            dv.inc(v.scalar_tensor_tensor(sc_["var"][:], bq.ps[:, 0:256], INVD,
                                          sc_["msq"][:], ALU.mult, ALU.subtract))
        items.append(("vector", v_var))

        def s_r(sc):
            sc.wait_ge(dv.h, dv_b + NG + 1)
            sc.activation(sc_["r"][:], sc_["var"][:], AF.Ln, bias=EPS)
            ins = sc.activation(sc_["r"][:], sc_["r"][:], AF.Exp, scale=-0.5)
            if with_m2:
                sc.activation(self.m2_t[:], sc_["m"][:], AF.Copy)
                ins = sc.activation(self.r2_t[:], sc_["r"][:], AF.Copy)
            acs.inc(ins)
            bm.freed_by(acs)
            bq.freed_by(acs)
        items.append(("scalar", s_r))
        return items

    def _ln_apply_items(self, xb, dst, fam, r_tok, out_slice=None, pre_wait=None,
                        out_dv=None):
        pe, acs, dv = fam
        sc_ = self.lnscr["F" if fam is self.F else "M"]

        def v_xq(v):
            v.wait_ge(acs.h, r_tok)
            if pre_wait is not None:
                sem, n = pre_wait
                v.wait_ge(sem.h, n)
            ins = None
            for g in range(NG):
                xv = xb[:, g, :]
                v.tensor_tensor(sc_["xm"][:], xv, sc_["m"][:], ALU.subtract)
                if out_slice is None:
                    o = dst[:, g, :]
                else:
                    o = dst[:, g, out_slice]
                ins = v.tensor_tensor(o, sc_["xm"][:], sc_["r"][:], ALU.mult)
            dv.inc(ins)
        return [("vector", v_xq)]

    def phase_ln1(self, s):
        pe, acs, dv = self.F
        xb = self.xbuf[s % 3]
        items = self._ln_stats_items(xb, self.F, (self.bk[4], self.bk[5]),
                                     pe_b=pe.n, acs_b=acs.n, dv_b=dv.n,
                                     x_tok=(dv, dv.n))
        r_tok = acs.n + NG + 2
        pw = (self.pef, self._w01_done.get(s - 1, 0))
        items += self._ln_apply_items(xb, self.x1q, self.F, r_tok, pre_wait=pw)
        x1q_tok = dv.n + (NG + 1) + 1
        for it in items:
            self.sect(*it)
        assert dv.n == x1q_tok, (dv.n, x1q_tok)
        self._x1q_tok = x1q_tok

    def ffn_items(self, s):
        pe, acs, dv = self.F
        dgw = [self.dgwA, self.dgwB]
        dgw2 = self.dgw2
        calls = []
        if s < NCH:
            calls.append((0, self.x1q, self.act[0], s, ("F", self._x1q_tok)))
        if s >= 2:
            calls.append((1, self.x3q[s % 2], self.act[1], s - 2,
                          ("M", self._x3q_tok[s - 2])))
        ncal = len(calls)
        if ncal == 0:
            return []

        dgw_base = [dgw[0].n, dgw[1].n]
        dgw2_base = dgw2.n
        pe_base = pe.n
        acs_base = acs.n
        dv_base = dv.n

        def gidx(i, ci):
            return i * ncal + ci

        G = NSUP * ncal
        items = []

        def g_w(g):
            g.wait_ge(pe.h, pe_base)
            if s == 0:
                g.wait_ge(self.dg0.h, self._dg0_all)
            for i in range(NSUP):
                if i >= 2:
                    g.wait_ge(pe.h, pe_base + (i - 1) * ncal)
                for W in range(2):
                    dgw[i % 2].inc(g.dma_start(
                        self.w01b[:, i % 2, W],
                        self.w01s[W, i].rearrange("kt kp m -> kp kt m")), 16)
            for dj in range(NDJ):
                for hh in range(2):
                    u = 2 * dj + hh
                    if u >= 2:
                        g.wait_ge(pe.h, pe_base + G + u - 1)
                    dgw2.inc(g.dma_start(
                        self.w2b[u % 2][:],
                        self.w2s[dj, hh * 32:(hh + 1) * 32]
                        .rearrange("kh hp d -> hp kh d")), 16)
        items.append(("gpsimd", g_w))

        entry_waits = []
        for c, _, _, _, _ in calls:
            for mt in range(2):
                b = self.bk[c * 2 + mt]
                if b.free_tok is not None:
                    entry_waits.append(b.free_tok)
                    b.free_tok = None

        for i in range(NSUP):
            for ci, (c, xq, actb, cs, rhs_tok) in enumerate(calls):
                g = gidx(i, ci)

                def t_mm(t, i=i, ci=ci, c=c, xq=xq, g=g, rhs_tok=rhs_tok):
                    if i == 0 and ci == 0:
                        for sem, n in entry_waits:
                            t.wait_ge(sem.h, n)
                    if i == 0:
                        fam, n = rhs_tok
                        t.wait_ge((self.dvf if fam == "F" else self.dvm).h, n)
                    t.wait_ge(dgw[i % 2].h, dgw_base[i % 2] + 32 * (i // 2 + 1))
                    if i >= 1:
                        t.wait_ge(acs.h, acs_base + 2 * gidx(i - 1, ci) + 1)
                    ins = None
                    for mt in range(2):
                        b = self.bk[c * 2 + mt]
                        for kt in range(DCN):
                            t.matmul(b.ps[:, 0:256],
                                     self.w01b[:, i % 2, 0, kt, mt * 128:(mt + 1) * 128],
                                     xq[:, kt, :],
                                     start=(kt == 0), stop=False)
                        for kt in range(DCN):
                            ins = t.matmul(b.ps[:, 256:512],
                                           self.w01b[:, i % 2, 1, kt, mt * 128:(mt + 1) * 128],
                                           xq[:, kt, :],
                                           start=False, stop=(kt == DCN - 1))
                    pe.inc(ins)
                items.append(("tensor", t_mm))

                def s_evx2(sc, ci=ci, c=c, g=g):
                    sc.wait_ge(pe.h, pe_base + g + 1)
                    if g >= 1:
                        sc.wait_ge(dv.h, dv_base + 2 * (g - 1) + 2)
                    for mt in range(2):
                        b = self.bk[c * 2 + mt]
                        sc.activation(self.gf16[ci][:, mt, :], b.ps[:], AF.Copy)
                    acs.inc(sc.activation(self.ch_x2[:],
                                          self.gf16[ci][:, :, 0:256], AF.Square))
                items.append(("scalar", s_evx2))

                def v_chA(v, ci=ci, g=g):
                    gv_ = self.gf16[ci][:, :, 0:256]
                    v.wait_ge(acs.h, acs_base + 2 * g + 1)
                    v.tensor_tensor(self.ch_x3[:], self.ch_x2[:], gv_, ALU.mult)
                    v.tensor_scalar_mul(self.ch_t[:], self.ch_x3[:], C1)
                    v.tensor_tensor(self.ch_t[:], self.ch_t[:], gv_, ALU.add)
                    dv.inc(v.tensor_scalar_mul(self.ch_u[:], self.ch_t[:], C2))
                items.append(("vector", v_chA))

                def s_tanh(sc, g=g):
                    sc.wait_ge(dv.h, dv_base + 2 * g + 1)
                    acs.inc(sc.activation(self.ch_t[:], self.ch_u[:], AF.Tanh))
                items.append(("scalar", s_tanh))

                def v_chB(v, i=i, ci=ci, g=g, actb=actb):
                    gv_ = self.gf16[ci][:, :, 0:256]
                    fv_ = self.gf16[ci][:, :, 256:512]
                    if i == 0:
                        v.wait_ge(pe.h, pe_base)
                    v.wait_ge(acs.h, acs_base + 2 * g + 2)
                    v.tensor_scalar(self.ch_cdf[:], self.ch_t[:], 1.0, 0.5,
                                    ALU.add, ALU.mult)
                    v.tensor_tensor(self.ch_gv[:], self.ch_cdf[:], gv_, ALU.mult)
                    v.tensor_tensor(self.ch_a16[:], self.ch_gv[:], fv_, ALU.mult)
                    dv.inc(v.tensor_copy(actb[:, 2 * i:2 * i + 2, :], self.ch_a16[:]))
                items.append(("vector", v_chB))

        chain_done = dv_base + 2 * G
        self._w01_done[s] = pe_base + G
        for c, _, _, _, _ in calls:
            for mt in range(2):
                self.bk[c * 2 + mt].free_tok = (acs, acs_base + 2 * (G - 1) + 1)

        pe2_base = pe_base + G
        acs2_base = acs_base + 2 * G
        w1_evac_done = acs_base + 2 * (G - 1) + 1
        hh0_done = dv_base + 2 * (NSUP // 2) * ncal

        for dj in range(NDJ):
            def t_w2(t, dj=dj):
                if dj <= 1:
                    t.wait_ge(acs.h, w1_evac_done)
                    if dj == 0:
                        t.wait_ge(dv.h, hh0_done)
                if dj >= 2:
                    t.wait_ge(acs.h, acs2_base + (dj - 1) * ncal)
                for hh in range(2):
                    u = 2 * dj + hh
                    t.wait_ge(dgw2.h, dgw2_base + 16 * (u + 1))
                    if dj == 0 and hh == 1:
                        t.wait_ge(dv.h, chain_done)
                    ins = None
                    for ci, (c, xq, actb, cs, _) in enumerate(calls):
                        b = self.bk[c * 2 + dj % 2]
                        for kh in range(32):
                            ins = t.matmul(b.ps[:, 0:256],
                                           self.w2b[u % 2][:, kh, :],
                                           actb[:, hh * 32 + kh, :],
                                           start=(hh == 0 and kh == 0),
                                           stop=(hh == 1 and kh == 31))
                    pe.inc(ins)
            items.append(("tensor", t_w2))

            for ci in range(ncal):
                def s_w2(sc, dj=dj, ci=ci):
                    sc.wait_ge(pe.h, pe2_base + 2 * dj + 2)
                    if dj >= 2:
                        sc.wait_ge(dv.h, chain_done + (dj - 2) * ncal + ci + 1)
                    c = calls[ci][0]
                    acs.inc(sc.activation(self.out16[2 * (dj % 2) + ci][:],
                                          self.bk[c * 2 + dj % 2].ps[:, 0:256],
                                          AF.Copy))
                items.append(("scalar", s_w2))

                def v_w2(v, dj=dj, ci=ci):
                    c, xq, actb, cs, _ = calls[ci]
                    v.wait_ge(acs.h, acs2_base + dj * ncal + ci + 1)
                    xbv = self.xbuf[cs % 3]
                    dv.inc(v.scalar_tensor_tensor(xbv[:, dj, :],
                                                  self.out16[2 * (dj % 2) + ci][:], 0.5,
                                                  xbv[:, dj, :], ALU.mult, ALU.add))
                items.append(("vector", v_w2))

        for ci, (c, xq, actb, cs, _) in enumerate(calls):
            for p in range(2):
                last_dj = NDJ - 2 + p
                self.bk[c * 2 + p].free_tok = (acs, acs2_base + last_dj * ncal + ci + 1)

        self._ffn_pe_end[s] = pe2_base + 2 * NDJ
        end_dv = chain_done + NDJ * ncal
        for ci, (c, xq, actb, cs, _) in enumerate(calls):
            self._x1done[cs] = end_dv
        if s >= 2:
            self._x2done = end_dv
        return items

    def phase_out(self, c):
        pe, dv, ds = self.pef, self.dvf, self.dsoo
        xb = self.xbuf[c % 3]
        done_tok = self._x2done
        pe_base = pe.n
        dv_base = dv.n
        ds_base = ds.n

        def t_tr(t):
            t.wait_ge(dv.h, done_tok)
            k = 0
            for th in range(2):
                for dcg in range(4):
                    k += 1
                    b = self.bk[4 + (th * 4 + dcg) % 2]
                    if k <= 2:
                        b.wait_free(t)
                    else:
                        t.wait_ge(dv.h, dv_base + (k - 2))
                    for j in range(4):
                        dc = dcg * 4 + j
                        ins = t.transpose(b.ps[:, j * 128:(j + 1) * 128],
                                          xb[:, dc, th * 128:(th + 1) * 128],
                                          self.id_s[:])
                    pe.inc(ins)

        self.sect("tensor", t_tr)

        def v_evac(v):
            k = 0
            for th in range(2):
                for dcg in range(4):
                    k += 1
                    if k == 5:
                        v.wait_ge(ds.h, ds_base + 16)
                    b = self.bk[4 + (th * 4 + dcg) % 2]
                    v.wait_ge(pe.h, pe_base + k)
                    ins = v.tensor_copy(
                        self.xio[:, dcg * 512:(dcg + 1) * 512]
                        .rearrange("p (a t) -> p a t", a=4),
                        b.ps[:].rearrange("p (a t) -> p a t", a=4))
                    dv.inc(ins)
                    b.freed_by(dv)

        self.sect("vector", v_evac)

        def s_out(sy):
            for th in range(2):
                sy.wait_ge(dv.h, dv_base + 4 * (th + 1))
                ds.inc(sy.dma_start(
                    self.out[c * TC + th * 128: c * TC + (th + 1) * 128, :],
                    self.xio[:]), 16)

        self.sect("sync", s_out)
        self._out_done[c] = ds.n

    def mamba_items(self, m):
        pe, acs, dv = self.M
        xb = self.xbuf[m % 3]
        items = []

        pe0, ac0, dv0 = pe.n, acs.n, dv.n

        x_tok = (self.dvf, self._x1done[m])
        items += self._ln_stats_items(xb, self.M, (self.bk[6], self.bk[7]),
                                      pe_b=pe0, acs_b=ac0, dv_b=dv0,
                                      x_tok=x_tok, with_m2=True)
        pe1 = pe0 + NG
        ac1 = ac0 + NG + 2
        dv1 = dv0 + NG + 1
        r_tok = ac1

        def v_halo(v):
            v.wait_ge(acs.h, r_tok)
            dv.inc(v.tensor_copy(self.xh2[:, :, 1:4],
                                 self.xh2[:, :, TC + 1:TC + 4]))
        items.append(("vector", v_halo))
        dv2 = dv1 + 1

        items += self._ln_apply_items(xb, self.xh2, self.M, r_tok,
                                      out_slice=slice(4, 4 + TC))
        xh2_tok = dv2 + 1

        acs_cv = ac1
        for dc in range(DCN):
            def v_taps(v, dc=dc):
                if dc == 0:
                    v.wait_ge(dv.h, xh2_tok)
                else:
                    v.wait_ge(acs.h, acs_cv + dc)
                v.tensor_scalar_mul(self.xcb[:], self.xh2[:, dc, 1:1 + TC],
                                    self.convw_s[:, dc, 0:1])
                for j in range(1, 4):
                    v.scalar_tensor_tensor(self.xcb[:],
                                           self.xh2[:, dc, 1 + j:1 + j + TC],
                                           self.convw_s[:, dc, j:j + 1],
                                           self.xcb[:], ALU.mult, ALU.add)
                dv.inc(v.tensor_scalar_add(self.xcb[:], self.xcb[:],
                                           self.convb_s[:, dc:dc + 1]))
            items.append(("vector", v_taps))

            def s_tanh(sc, dc=dc):
                sc.wait_ge(dv.h, xh2_tok + 2 * dc + 1)
                acs.inc(sc.activation(self.th_s[:], self.xcb[:], AF.Tanh, scale=0.5))
            items.append(("scalar", s_tanh))

            def v_silu(v, dc=dc):
                v.wait_ge(acs.h, acs_cv + dc + 1)
                v.scalar_tensor_tensor(self.u_s[:], self.th_s[:], 1.0, self.xcb[:],
                                       ALU.add, ALU.mult)
                dv.inc(v.tensor_scalar_mul(self.xconv[:, dc, :], self.u_s[:], 0.5))
            items.append(("vector", v_silu))

        dv3 = xh2_tok + 32
        ac2 = acs_cv + 16

        bdt, bbc = self.bk[6], self.bk[7]
        pe_pj = pe1

        def t_proj(t):
            t.wait_ge(dv.h, dv3)
            if m == 0:
                t.wait_ge(self.dsw.h, self._projw_tok)
            bdt.wait_free(t)
            bbc.wait_free(t)
            ins = None
            for kt in range(DCN):
                t.matmul(bdt.ps[:, 0:256], self.xproj_s[:, kt, 0:128], self.xconv[:, kt, :],
                         start=(kt == 0), stop=(kt == DCN - 1))
                ins = t.matmul(bbc.ps[0:32, 0:256], self.xproj_s[:, kt, 128:160],
                               self.xconv[:, kt, :],
                               start=(kt == 0), stop=(kt == DCN - 1))
            pe.inc(ins)
        items.append(("tensor", t_proj))

        def v_pj(v):
            v.wait_ge(pe.h, pe_pj + 1)
            v.tensor_copy(self.dt16[:], bdt.ps[:, 0:256])
            dv.inc(v.tensor_copy(self.bc16[0:32, :], bbc.ps[0:32, 0:256]))
            bdt.freed_by(dv)
            bbc.freed_by(dv)
        items.append(("vector", v_pj))
        dv_pj = dv3 + 1

        pe_d = pe_pj + 1
        acs_d = ac2
        dv_d = dv_pj
        for dc in range(DCN):
            def t_delta(t, dc=dc):
                b = self.bk[6 + dc % 2]
                if dc < 2:
                    t.wait_ge(dv.h, dv_pj)
                else:
                    t.wait_ge(dv.h, dv_d + 3 * (dc - 2) + 2)
                pe.inc(t.matmul(b.ps[:, 0:256], self.dtproj_s[:, dc, :], self.dt16[:],
                                start=True, stop=True))
            items.append(("tensor", t_delta))

            def s_z(sc, dc=dc):
                b = self.bk[6 + dc % 2]
                sc.wait_ge(pe.h, pe_d + dc + 1)
                if dc >= 1:
                    sc.wait_ge(dv.h, dv_d + 3 * (dc - 1) + 2)
                acs.inc(sc.activation(self.z_s[:], b.ps[:, 0:256], AF.Identity,
                                      bias=self.dtb_s[:, dc:dc + 1]))
            items.append(("scalar", s_z))

            def v_az(v, dc=dc):
                v.wait_ge(acs.h, acs_d + 2 * dc + 1)
                dv.inc(v.scalar_tensor_tensor(self.az_s[:], self.z_s[:], -1.0, self.z_s[:],
                                              ALU.mult, ALU.max))
            items.append(("vector", v_az))

            def s_expln(sc, dc=dc):
                sc.wait_ge(dv.h, dv_d + 3 * dc + 1)
                sc.activation(self.e_s[:], self.az_s[:], AF.Exp, scale=-1.0)
                acs.inc(sc.activation(self.e_s[:], self.e_s[:], AF.Ln, bias=1.0))
            items.append(("scalar", s_expln))

            def v_dl(v, dc=dc):
                v.wait_ge(acs.h, acs_d + 2 * dc + 2)
                dv.inc(v.scalar_tensor_tensor(self.delta[:, dc, :], self.z_s[:], 0.0,
                                              self.e_s[:], ALU.max, ALU.add))
                dv.inc(v.tensor_tensor(self.wtd[:, dc, :], self.delta[:, dc, :],
                                       self.xconv[:, dc, :], ALU.mult))
            items.append(("vector", v_dl))

        delta_tok = dv_d + 3 * DCN
        ac3 = acs_d + 2 * DCN
        pe3 = pe_d + DCN
        tok_y = [(dv, dv_d + 3 * (DCN - 2) + 2), (dv, dv_d + 3 * (DCN - 1) + 2)]

        dsbc = self.dsbc
        bc_base = dsbc.n

        def s_bc(sy):
            sy.wait_ge(dv.h, delta_tok)
            sy.wait_ge(pe.h, pe_pj + 1)
            for np_ in range(NP):
                for n2 in range(2):
                    dsbc.inc(sy.dma_start(
                        self.Bbc[n2 * 64:(n2 + 1) * 64, np_, :],
                        self.bc16[2 * np_ + n2:2 * np_ + n2 + 1, :].unsqueeze(1)
                        .broadcast_to([1, 64, TC])), 16)
                    dsbc.inc(sy.dma_start(
                        self.Cbc[n2 * 64:(n2 + 1) * 64, np_, :],
                        self.bc16[16 + 2 * np_ + n2:16 + 2 * np_ + n2 + 1, :].unsqueeze(1)
                        .broadcast_to([1, 64, TC])), 16)
        items.append(("sync", s_bc))
        bc_tok = bc_base + 16 * 32

        dsp = [self.dse, self.dso]
        dsp_base = [self.dse.n, self.dso.n]
        acs_sc = ac3
        scan_dv = delta_tok
        pe_y = pe3

        def sdv(nscan, ny):
            return scan_dv + 5 * nscan + ny

        def after_scan(dblk):
            return sdv(dblk + 1, dblk // 2)

        def after_y(dc):
            return sdv(2 * dc + 2, dc + 1)

        def s_ddn(sy):
            for dblk in range(NDBLK):
                par = dblk % 2
                if dblk >= 2:
                    sy.wait_ge(dv.h, after_scan(dblk - 2))
                    sy.wait_ge(acs.h, acs_sc + (dblk - 1))
                p0 = 64 * (dblk % 2)
                dc = dblk // 2
                for n2 in range(2):
                    dsp[par].inc(sy.dma_start(
                        self.ddn[par][n2 * 64:(n2 + 1) * 64, :],
                        self.delta[p0:p0 + 64, dc, :]), 16)
                    dsp[par].inc(sy.dma_start(
                        self.wdn[par][n2 * 64:(n2 + 1) * 64, :],
                        self.wtd[p0:p0 + 64, dc, :]), 16)
        items.append(("sync", s_ddn))

        for dblk in range(NDBLK):
            par = dblk % 2

            def s_exp(sc, dblk=dblk, par=par):
                sc.wait_ge(dsp[par].h, dsp_base[par] + 64 * (dblk // 2 + 1))
                if dblk >= 1:
                    sc.wait_ge(dv.h, after_scan(dblk - 1))
                    sc.wait_ge(pe.h, pe_y + dblk)
                ins = None
                for np_ in range(NP):
                    ins = sc.activation(self.a_g[:, np_, 1:SEPW], self.ddn[par][:],
                                        AF.Exp,
                                        scale=self.A_perm[:, dblk * 8 + np_:dblk * 8 + np_ + 1])
                acs.inc(ins)
            items.append(("scalar", s_exp))

            def v_scan(v, dblk=dblk, par=par):
                if dblk == 0:
                    v.wait_ge(dsbc.h, bc_tok)
                v.wait_ge(acs.h, acs_sc + dblk + 1)
                if dblk >= 1:
                    v.wait_ge(pe.h, pe_y + dblk)
                v.tensor_copy(self.b_g[:, :, 0:1],
                              self.h_state[:, dblk * 8:dblk * 8 + 8].unsqueeze(2))
                v.tensor_tensor(self.b_g[:, :, 1:SEPW],
                                self.wdn[par][:].unsqueeze(1).broadcast_to([128, NP, TC]),
                                self.Bbc[:], ALU.mult)
                v.tensor_tensor_scan(self.h_g[:].rearrange("p a t -> p (a t)"),
                                     self.a_g[:].rearrange("p a t -> p (a t)"),
                                     self.b_g[:].rearrange("p a t -> p (a t)"),
                                     0.0, ALU.mult, ALU.add)
                v.tensor_copy(self.h_state[:, dblk * 8:dblk * 8 + 8],
                              self.h_g[:, :, SEPW - 1:SEPW].squeeze(2))
                ins = v.tensor_tensor(self.a_g[:, :, 1:SEPW], self.h_g[:, :, 1:SEPW],
                                      self.Cbc[:], ALU.mult)
                dv.inc(ins, 5)
            items.append(("vector", v_scan))

            def t_red(t, dblk=dblk):
                t.wait_ge(dv.h, after_scan(dblk))
                dc = dblk // 2
                b = self.bk[6 + dc % 2]
                if dblk % 2 == 0:
                    if dc < 2:
                        if tok_y[dc % 2] is not None:
                            sem, n = tok_y[dc % 2]
                            t.wait_ge(sem.h, n)
                            tok_y[dc % 2] = None
                    else:
                        t.wait_ge(dv.h, after_y(dc - 2))
                R = self.re_s if dblk % 2 == 0 else self.ro_s
                ins = None
                for np_ in range(NP):
                    ins = t.matmul(b.ps[:, 0:256], R[:], self.a_g[:, np_, 1:SEPW],
                                   start=(dblk % 2 == 0 and np_ == 0),
                                   stop=(dblk % 2 == 1 and np_ == NP - 1))
                pe.inc(ins)
            items.append(("tensor", t_red))

            if dblk % 2 == 1:
                def v_y(v, dblk=dblk):
                    dc = dblk // 2
                    b = self.bk[6 + dc % 2]
                    v.wait_ge(pe.h, pe_y + 2 * dc + 2)
                    v.tensor_tensor(self.ytmp[:], xb[:, dc, :], self.m2_t[:],
                                    ALU.subtract)
                    v.scalar_tensor_tensor(self.ytmp2[:], self.ytmp[:],
                                           self.Dp_s[:, dc:dc + 1], self.r2_t[:],
                                           ALU.mult, ALU.mult)
                    v.tensor_tensor(self.ytmp[:], self.ytmp2[:], b.ps[:, 0:256],
                                    ALU.add)
                    dv.inc(v.tensor_tensor(xb[:, dc, :], xb[:, dc, :], self.ytmp[:],
                                           ALU.add))
                    b.freed_by(dv)
                items.append(("vector", v_y))

        dv4 = scan_dv + 5 * NDBLK + DCN
        pe4 = pe_y + NDBLK
        ac4 = acs_sc + NDBLK

        items += self._ln_stats_items(xb, self.M, (self.bk[6], self.bk[7]),
                                      pe_b=pe4, acs_b=ac4, dv_b=dv4)
        r3_tok = ac4 + NG + 2
        pwait = (self.pef, self._ffn_pe_end.get(m, 0))
        items += self._ln_apply_items(xb, self.x3q[m % 2], self.M, r3_tok,
                                      pre_wait=pwait)
        self._x3q_tok[m] = dv4 + NG + 1 + 1
        self._expect_m = (pe4 + NG, ac4 + NG + 2, dv4 + NG + 1 + 1)
        return items

    def finalize(self):
        sems = [self.dg0, self.dgwA, self.dgwB, self.dgw2, self.dsbc,
                self.dse, self.dso, self.dsx, self.dsoo, self.dsw]
        tgts = [(s.h, s.n) for s in sems]

        def s_fin(sy):
            for h, n in tgts:
                sy.wait_ge(h, n)

        self.sect("sync", s_fin)


def build_program():
    b = Builder()
    with b.es:
        return b.build()


_NC_CACHE = None


def kernel(L0_attn_x, L0_attn_conv_w, L0_attn_conv_b, L0_attn_xproj_w,
           L0_attn_dtproj_w, L0_attn_dtproj_b, L0_attn_A_log, L0_attn_D_param,
           L0_ffn_w_gating, L0_ffn_w_linear):
    global _NC_CACHE
    X = np.ascontiguousarray(np.asarray(L0_attn_x, np.float32))
    consts = {
        "conv_w": np.ascontiguousarray(np.asarray(L0_attn_conv_w, np.float32)),
        "conv_b": np.ascontiguousarray(np.asarray(L0_attn_conv_b, np.float32)),
        "xproj_w": np.ascontiguousarray(np.asarray(L0_attn_xproj_w, np.float32)),
        "dtproj_w": np.ascontiguousarray(np.asarray(L0_attn_dtproj_w, np.float32)),
        "dtproj_b": np.ascontiguousarray(np.asarray(L0_attn_dtproj_b, np.float32)),
        "A_perm_log": np.ascontiguousarray(
            np.asarray(L0_attn_A_log, np.float32)
            .reshape(32, 64, 8, 2).transpose(3, 1, 0, 2).reshape(128, 256)),
        "D_param": np.ascontiguousarray(np.asarray(L0_attn_D_param, np.float32)),
        "w_gating": np.ascontiguousarray(np.asarray(L0_ffn_w_gating, np.float32)),
        "w_linear": np.ascontiguousarray(np.asarray(L0_ffn_w_linear, np.float32)),
        "ident": np.eye(128, dtype=np.float32),
        "fconst": np.tile(np.array([[EPS, 1.0, -0.5, -1.0, 0.5, INVD, 0.0, 0.0]],
                                   np.float32), (128, 1)),
        "ones16": np.ones((128, 128), dtype=ml_dtypes.bfloat16),
    }
    re_ = np.zeros((128, 128), np.float32)
    ro_ = np.zeros((128, 128), np.float32)
    for p in range(128):
        re_[p, p % 64] = 1.0
        ro_[p, 64 + (p % 64)] = 1.0
    consts["r_even"] = re_.astype(ml_dtypes.bfloat16)
    consts["r_odd"] = ro_.astype(ml_dtypes.bfloat16)

    if _NC_CACHE is None:
        _NC_CACHE = build_program()
    nc = _NC_CACHE
    in_maps = [dict(consts, x=np.ascontiguousarray(X[b])) for b in range(8)]
    res = run_bass_kernel_spmd(nc, in_maps, core_ids=list(range(8)))
    return np.stack([np.asarray(res.results[b]["out"], np.float32) for b in range(8)],
                    axis=0)


# version 06
# speedup vs baseline: 1.1872x; 1.0075x over previous
import contextlib
import numpy as np
import ml_dtypes
import concourse.bass as bass
import concourse.mybir as mybir
from concourse.bass_utils import run_bass_kernel_spmd

F32 = mybir.dt.float32
BF16 = mybir.dt.bfloat16
E4 = mybir.dt.float8e4
AF = mybir.ActivationFunctionType
ALU = mybir.AluOpType

T, D, H = 2048, 2048, 8192
NS, RNK = 16, 128
TC = 256
NCH = T // TC
DCN = D // 128
HCN = H // 128
NSUP = 32
NDJ = 16
NDBLK = 32
NP = 8
SEPW = 257
EPS = 1e-5
C1 = float(np.float32(ml_dtypes.bfloat16(0.044715)))
C2 = float(np.float32(ml_dtypes.bfloat16(np.sqrt(2.0 / np.pi))))
INVD = 1.0 / D
NG = 16


class Sem:
    def __init__(self, es, nc, name):
        self.h = es.enter_context(nc.semaphore(name))
        self.n = 0

    def inc(self, inst, amt=1):
        inst.then_inc(self.h, amt)
        self.n += amt
        return inst


class Bank:
    def __init__(self, es, nc, name):
        self.ps = es.enter_context(nc.psum_tensor(name, [128, 512], F32))
        self.free_tok = None

    def freed_by(self, sem):
        self.free_tok = (sem, sem.n)

    def wait_free(self, eng):
        if self.free_tok is not None:
            sem, n = self.free_tok
            eng.wait_ge(sem.h, n)
            self.free_tok = None


class Builder:
    def __init__(self):
        self.nc = bass.Bass("TRN2", target_bir_lowering=False,
                            detect_race_conditions=False)
        self.es = contextlib.ExitStack()
        self._out_done = {}
        self._x1done = {}
        self._ffn_pe_end = {}
        self._w01_done = {}
        self._x3q_tok = {}

    def sect(self, eng, fn):
        getattr(self.block, eng)(fn)

    def _tbl(self, sc):
        sc.add_instruction(mybir.InstLoadActFuncSet(
            name=self.nc.get_next_instruction_name(),
            act_func_set_id=6, ins=[], outs=[]))

    def _overlay(self, name, shape, dt, base_handle, byte_off):
        nc = self.nc
        h = nc.virtual_sbuf_tensor(name, shape, dt)
        nc._tensor(h.name, h.shape, h.dtype, type="SB")
        mloc = nc.lookup_mloc(h)
        mloc.allocated = True
        mloc.addr = nc.lookup_mloc(base_handle).addr + byte_off
        return h

    def run_items(self, *lists):
        lists = [l for l in lists if l]
        if not lists:
            return
        main = max(lists, key=len)
        rest = [l for l in lists if l is not main]
        n = len(main)
        idxs = [0] * len(rest)
        for k, (eng, fn) in enumerate(main):
            self.sect(eng, fn)
            for li, l in enumerate(rest):
                want = (k + 1) * len(l) // n
                while idxs[li] < want:
                    e2, f2 = l[idxs[li]]
                    self.sect(e2, f2)
                    idxs[li] += 1
        for li, l in enumerate(rest):
            while idxs[li] < len(l):
                e2, f2 = l[idxs[li]]
                self.sect(e2, f2)
                idxs[li] += 1

    def build(self):
        nc, es = self.nc, self.es
        self.x_in = nc.dram_tensor("x", [T, D], F32, kind="ExternalInput").ap()
        self.conv_w = nc.dram_tensor("conv_w", [D, 1, 4], F32, kind="ExternalInput").ap()
        self.conv_b = nc.dram_tensor("conv_b", [D], F32, kind="ExternalInput").ap()
        self.xproj_w = nc.dram_tensor("xproj_w", [160, D], F32, kind="ExternalInput").ap()
        self.dtproj_w = nc.dram_tensor("dtproj_w", [D, RNK], F32, kind="ExternalInput").ap()
        self.dtproj_b = nc.dram_tensor("dtproj_b", [D], F32, kind="ExternalInput").ap()
        self.A_permin = nc.dram_tensor("A_perm_log", [128, 256], F32, kind="ExternalInput").ap()
        self.D_param = nc.dram_tensor("D_param", [D], F32, kind="ExternalInput").ap()
        self.wg = nc.dram_tensor("w_gating", [2, D, H], F32, kind="ExternalInput").ap()
        self.wl = nc.dram_tensor("w_linear", [H, D], F32, kind="ExternalInput").ap()
        self.ident = nc.dram_tensor("ident", [128, 128], F32, kind="ExternalInput").ap()
        self.ones16 = nc.dram_tensor("ones16", [128, 128], BF16, kind="ExternalInput").ap()
        self.r_even = nc.dram_tensor("r_even", [128, 128], BF16, kind="ExternalInput").ap()
        self.r_odd = nc.dram_tensor("r_odd", [128, 128], BF16, kind="ExternalInput").ap()
        self.fconst = nc.dram_tensor("fconst", [128, 8], F32, kind="ExternalInput").ap()
        self.out = nc.dram_tensor("out", [T, D], F32, kind="ExternalOutput").ap()
        self.w01s = nc.dram_tensor("w01s", [2, NSUP, 16, 128, 256], E4).ap()
        self.w2s = nc.dram_tensor("w2s", [NDJ, HCN, 128, 128], E4).ap()
        self.xproj16 = nc.dram_tensor("xproj16", [160, D], BF16).ap()
        self.dtproj16 = nc.dram_tensor("dtproj16", [D, RNK], BF16).ap()

        sb = lambda name, shape, dt: es.enter_context(nc.sbuf_tensor(name, shape, dt))
        self.xbuf = [sb(f"xbuf{i}", [128, DCN, TC], F32) for i in range(3)]
        self.xio = sb("xio", [128, 2048], F32)
        self.x1q = sb("x1q", [128, DCN, TC], E4)
        self.x3q = [sb(f"x3q{i}", [128, DCN, TC], E4) for i in range(2)]
        _gf = sb("gf16", [128, 2, 512], BF16)
        self.gf16 = [_gf, _gf]
        self.ch_x2 = sb("ch_x2", [128, 2, 256], BF16)
        self.ch_t = sb("ch_t", [128, 2, 256], BF16)
        self.ch_cdf = sb("ch_cdf", [128, 2, 256], BF16)
        self.ch_gv = sb("ch_gv", [128, 2, 256], BF16)
        self.ch_a16 = self.ch_t
        self.ch_x3 = self.ch_cdf
        self.ch_u = self.ch_gv
        self.act = [sb(f"act{c}", [128, HCN, TC], E4) for c in range(2)]
        self.w01b = sb("w01b", [128, 2, 2, 16, 256], E4)
        self.w2b = [sb("w2b0", [128, HCN // 2, 128], E4),
                    self._overlay("w2b1", [128, HCN // 2, 128], E4,
                                  self.xio, 2048)]
        self.out16 = [self._overlay(f"out16_{i}", [128, TC], BF16,
                                    self.xio, 512 * i) for i in range(4)]
        self.lnscr = {}
        for fam in ("F", "M"):
            d = dict(
                q=sb(f"q_{fam}", [128, 256], F32),
                qh=sb(f"qh_{fam}", [128, 256], BF16),
                ql=sb(f"ql_{fam}", [128, 256], BF16),
                xh=sb(f"xh_{fam}", [128, 256], BF16),
                xl=sb(f"xl_{fam}", [128, 256], BF16),
                m=sb(f"m_{fam}", [128, TC], F32),
                msq=sb(f"msq_{fam}", [128, TC], F32),
                r=sb(f"r_{fam}", [128, TC], F32),
            )
            d["var"] = d["msq"]
            d["xm"] = d["q"]
            self.lnscr[fam] = d
        self.m2_t = sb("m2_t", [128, TC], F32)
        self.r2_t = sb("r2_t", [128, TC], F32)
        self.xh2 = sb("xh2", [128, DCN, TC + 4], BF16)
        self.xcb = sb("xcb", [128, TC], F32)
        self.th_s = sb("th_s", [128, TC], F32)
        self.u_s = self.lnscr["M"]["q"]
        self.xconv = sb("xconv", [128, DCN, TC], BF16)
        self.Bbc = self.xconv[:, 0:NP, :]
        self.Cbc = self.xconv[:, NP:2 * NP, :]
        self.dt16 = sb("dt16", [128, TC], BF16)
        self.bc16 = sb("bc16", [128, TC], BF16)
        self.delta = sb("delta", [128, DCN, TC], F32)
        self.wtd = sb("wtd", [128, DCN, TC], BF16)
        self.z_s = self.xcb
        self.az_s = self.u_s
        self.e_s = self.th_s
        self.ytmp = self.xcb
        self.ytmp2 = self.th_s
        self.ddn = [sb(f"ddn{i}", [128, TC], F32) for i in range(2)]
        self.wdn = [sb(f"wdn{i}", [128, TC], BF16) for i in range(2)]
        self.a_g = sb("a_g", [128, NP, SEPW], BF16)
        self.b_g = sb("b_g", [128, NP, SEPW], BF16)
        self.h_g = self.b_g
        self.xh2_tail = sb("xh2_tail", [128, DCN, 3], BF16)
        self.y_g = [self._overlay(f"y_g{i}", [128, NP, SEPW], BF16,
                                  self.xh2, 4112 * i) for i in range(2)]
        self.h_state = sb("h_state", [128, 256], F32)
        self.A_perm = sb("A_perm", [128, 256], F32)
        self.fconst_s = sb("fconst_s", [128, 8], F32)
        self.id_s = sb("id_s", [128, 128], F32)
        self.ones_s = sb("ones_s", [128, 128], BF16)
        self.re_s = sb("re_s", [128, 128], BF16)
        self.ro_s = sb("ro_s", [128, 128], BF16)
        self.convw_s = sb("convw_s", [128, DCN, 4], F32)
        self.convb_s = sb("convb_s", [128, DCN], F32)
        self.dtb_s = sb("dtb_s", [128, DCN], F32)
        self.Dp_s = sb("Dp_s", [128, DCN], F32)
        self.xproj_s = sb("xproj_s", [128, DCN, 160], BF16)
        self.dtproj_s = sb("dtproj_s", [128, DCN, 128], BF16)

        for j, val in enumerate([EPS, 1.0, -0.5, -1.0, 0.5, INVD]):
            nc.const_aps.register(F32, val, self.fconst_s[:, j:j + 1])

        self.bk = [Bank(es, nc, f"bk{i}") for i in range(8)]

        self.dg0 = Sem(es, nc, "dg0")
        self.dgwA = Sem(es, nc, "dgwA")
        self.dgwB = Sem(es, nc, "dgwB")
        self.dgw2 = Sem(es, nc, "dgw2")
        self.dsbc = Sem(es, nc, "dsbc")
        self.dse = Sem(es, nc, "dse")
        self.dso = Sem(es, nc, "dso")
        self.dsx = Sem(es, nc, "dsx")
        self.dsoo = Sem(es, nc, "dsoo")
        self.dsw = Sem(es, nc, "dsw")
        self.pef = Sem(es, nc, "pef")
        self.acf = Sem(es, nc, "acf")
        self.dvf = Sem(es, nc, "dvf")
        self.pem = Sem(es, nc, "pem")
        self.acm = Sem(es, nc, "acm")
        self.dvm = Sem(es, nc, "dvm")
        self.F = (self.pef, self.acf, self.dvf)
        self.M = (self.pem, self.acm, self.dvm)

        self.block = es.enter_context(nc.Block())
        self.emit()
        self.finalize()
        return nc

    def emit(self):
        self.phase0()
        for s in range(NCH + 2):
            if s < NCH:
                self.phase_xload(s)
                self.phase_ln1(s)
            ffn_it = self.ffn_items(s)
            mam_it = self.mamba_items(s - 1) if 1 <= s <= NCH else []
            if s >= NCH - 1:
                for it in mam_it:
                    self.sect(*it)
                for it in ffn_it:
                    self.sect(*it)
            else:
                self.run_items(ffn_it, mam_it)
            if mam_it:
                got = (self.pem.n, self.acm.n, self.dvm.n)
                assert got == self._expect_m, (s, got, self._expect_m)
            if s >= 2:
                self.phase_out(s - 2)

    def phase0(self):
        dg0, dsw, acs, dv = self.dg0, self.dsw, self.acf, self.dvf

        def g_loads(g):
            for W in range(2):
                for ms in range(NSUP):
                    src = self.wg[W, :, ms * 256:(ms + 1) * 256] \
                        .rearrange("(kt kp) m -> kt kp m", kp=128)
                    dg0.inc(g.dma_start(self.w01s[W, ms], src), 16)
            for dj in range(NDJ):
                src = self.wl[:, dj * 128:(dj + 1) * 128] \
                    .rearrange("(kh hp) d -> kh hp d", hp=128)
                dg0.inc(g.dma_start(self.w2s[dj], src), 16)
            dg0.inc(g.dma_start(self.xproj16, self.xproj_w), 16)
            dg0.inc(g.dma_start(self.dtproj16, self.dtproj_w), 16)
            dg0.inc(g.dma_start(self.convw_s[:],
                                self.conv_w.squeeze(1).rearrange("(dc p) j -> p dc j", p=128)), 16)
            dg0.inc(g.dma_start(self.convb_s[:],
                                self.conv_b.rearrange("(dc p) -> p dc", p=128),
                                allow_slow_non_contiguous=True), 16)
            dg0.inc(g.dma_start(self.dtb_s[:],
                                self.dtproj_b.rearrange("(dc p) -> p dc", p=128),
                                allow_slow_non_contiguous=True), 16)
            dg0.inc(g.dma_start(self.Dp_s[:],
                                self.D_param.rearrange("(dc p) -> p dc", p=128),
                                allow_slow_non_contiguous=True), 16)
            dg0.inc(g.dma_start(self.A_perm[:], self.A_permin), 16)
            dg0.inc(g.dma_start(self.fconst_s[:], self.fconst), 16)
            dg0.inc(g.dma_start(self.id_s[:], self.ident), 16)
            dg0.inc(g.dma_start(self.ones_s[:], self.ones16), 16)
            dg0.inc(g.dma_start(self.re_s[:], self.r_even), 16)
            dg0.inc(g.dma_start(self.ro_s[:], self.r_odd), 16)

        self.sect("gpsimd", g_loads)
        self._dg0_all = dg0.n

        def s_loads(sy):
            sy.wait_ge(dg0.h, self._dg0_all)
            for kt in range(DCN):
                dsw.inc(sy.dma_start(self.xproj_s[:, kt, :],
                                     self.xproj16[:, kt * 128:(kt + 1) * 128],
                                     transpose=True), 16)
            for dc in range(DCN):
                dsw.inc(sy.dma_start(self.dtproj_s[:, dc, :],
                                     self.dtproj16[dc * 128:(dc + 1) * 128, :],
                                     transpose=True), 16)

        self.sect("sync", s_loads)
        self._projw_tok = dsw.n

        def s_negexp(sc):
            sc.wait_ge(dg0.h, self._dg0_all)
            acs.inc(sc.activation(self.A_perm[:], self.A_perm[:], AF.Exp))

        self.sect("scalar", s_negexp)
        acs_tok = acs.n

        def v_init(v):
            v.wait_ge(acs.h, acs_tok)
            v.tensor_scalar_mul(self.A_perm[:], self.A_perm[:], -1.0)
            v.memset(self.h_state[:], 0.0)
            v.memset(self.a_g[:].rearrange("p a t -> p (a t)"), 0.0)
            v.memset(self.xh2_tail[:].rearrange("p a t -> p (a t)"), 0.0)
            dv.inc(v.memset(self.xh2[:].rearrange("p a t -> p (a t)"), 0.0))

        self.sect("vector", v_init)
        self._init_dv = dv.n

    def phase_xload(self, s):
        ds, pe, dv = self.dsx, self.pef, self.dvf
        xb = self.xbuf[s % 3]
        ds_base = ds.n
        pe_base = pe.n
        dv_all = dv.n

        def s_load(sy):
            if s >= 3:
                sy.wait_ge(self.dsoo.h, self._out_done[s - 3])
            if s >= 1:
                sy.wait_ge(pe.h, pe_base)
                sy.wait_ge(dv.h, dv_all)
            ds.inc(sy.dma_start(self.xio[:],
                                self.x_in[s * TC: s * TC + 128, :]), 16)
            sy.wait_ge(pe.h, pe_base + 4)
            ds.inc(sy.dma_start(self.xio[:],
                                self.x_in[s * TC + 128: s * TC + 256, :]), 16)

        self.sect("sync", s_load)

        def t_tr(t):
            t.wait_ge(dv.h, dv_all)
            if s == 0:
                t.wait_ge(dv.h, self._init_dv)
            k = 0
            for th in range(2):
                t.wait_ge(ds.h, ds_base + 16 * (th + 1))
                for dcg in range(4):
                    k += 1
                    b = self.bk[4 + (th * 4 + dcg) % 2]
                    if k <= 2:
                        b.wait_free(t)
                    else:
                        t.wait_ge(dv.h, dv_all + (k - 2))
                    for j in range(4):
                        dc = dcg * 4 + j
                        ins = t.transpose(b.ps[:, j * 128:(j + 1) * 128],
                                          self.xio[:, dc * 128:(dc + 1) * 128],
                                          self.id_s[:])
                    pe.inc(ins)

        self.sect("tensor", t_tr)

        def v_evac(v):
            k = 0
            for th in range(2):
                for dcg in range(4):
                    k += 1
                    b = self.bk[4 + (th * 4 + dcg) % 2]
                    v.wait_ge(pe.h, pe_base + k)
                    ins = v.tensor_copy(
                        xb[:, dcg * 4:(dcg + 1) * 4, th * 128:(th + 1) * 128],
                        b.ps[:].rearrange("p (a t) -> p a t", a=4))
                    dv.inc(ins)
                    b.freed_by(dv)

        self.sect("vector", v_evac)

    def _ln_stats_items(self, xb, fam, banks, pe_b, acs_b, dv_b, x_tok=None,
                        with_m2=False):
        pe, acs, dv = fam
        bm, bq = banks
        sc_ = self.lnscr["F" if fam is self.F else "M"]
        items = []
        for g in range(NG):
            def s_sq(sc, g=g):
                if g == 0 and x_tok is not None:
                    sem, n = x_tok
                    sc.wait_ge(sem.h, n)
                if g >= 1:
                    sc.wait_ge(pe.h, pe_b + g)
                    sc.wait_ge(dv.h, dv_b + g)
                xv = xb[:, g, :]
                sc.activation(sc_["q"][:], xv, AF.Square)
                acs.inc(sc.activation(sc_["qh"][:], sc_["q"][:], AF.Copy))
            items.append(("scalar", s_sq))

            def v_hl(v, g=g):
                if g >= 1:
                    v.wait_ge(pe.h, pe_b + g)
                xv = xb[:, g, :]
                v.wait_ge(acs.h, acs_b + g + 1)
                v.scalar_tensor_tensor(sc_["ql"][:], sc_["qh"][:], -1.0, sc_["q"][:],
                                       ALU.mult, ALU.add)
                v.tensor_copy(sc_["xh"][:], xv)
                dv.inc(v.scalar_tensor_tensor(sc_["xl"][:], sc_["xh"][:], -1.0, xv,
                                              ALU.mult, ALU.add))
            items.append(("vector", v_hl))

            def t_mm(t, g=g):
                if g == 0:
                    bm.wait_free(t)
                    bq.wait_free(t)
                t.wait_ge(dv.h, dv_b + g + 1)
                t.matmul(bm.ps[:, 0:256], self.ones_s[:], sc_["xh"][:],
                         start=(g == 0), stop=False)
                t.matmul(bm.ps[:, 0:256], self.ones_s[:], sc_["xl"][:],
                         start=False, stop=(g == NG - 1))
                t.matmul(bq.ps[:, 0:256], self.ones_s[:], sc_["qh"][:],
                         start=(g == 0), stop=False)
                ins = t.matmul(bq.ps[:, 0:256], self.ones_s[:], sc_["ql"][:],
                               start=False, stop=(g == NG - 1))
                pe.inc(ins)
            items.append(("tensor", t_mm))

        def s_stats(sc):
            sc.wait_ge(pe.h, pe_b + NG)
            sc.activation(sc_["m"][:], bm.ps[:, 0:256], AF.Copy, scale=INVD)
            acs.inc(sc.activation(sc_["msq"][:], sc_["m"][:], AF.Square))
        items.append(("scalar", s_stats))

        def v_var(v):
            v.wait_ge(acs.h, acs_b + NG + 1)
            dv.inc(v.scalar_tensor_tensor(sc_["var"][:], bq.ps[:, 0:256], INVD,
                                          sc_["msq"][:], ALU.mult, ALU.subtract))
        items.append(("vector", v_var))

        def s_r(sc):
            sc.wait_ge(dv.h, dv_b + NG + 1)
            sc.activation(sc_["r"][:], sc_["var"][:], AF.Ln, bias=EPS)
            ins = sc.activation(sc_["r"][:], sc_["r"][:], AF.Exp, scale=-0.5)
            if with_m2:
                sc.activation(self.m2_t[:], sc_["m"][:], AF.Copy)
                ins = sc.activation(self.r2_t[:], sc_["r"][:], AF.Copy)
            acs.inc(ins)
            bm.freed_by(acs)
            bq.freed_by(acs)
        items.append(("scalar", s_r))
        return items

    def _ln_apply_items(self, xb, dst, fam, r_tok, out_slice=None, pre_wait=None,
                        out_dv=None):
        pe, acs, dv = fam
        sc_ = self.lnscr["F" if fam is self.F else "M"]

        def v_xq(v):
            v.wait_ge(acs.h, r_tok)
            if pre_wait is not None:
                sem, n = pre_wait
                v.wait_ge(sem.h, n)
            ins = None
            for g in range(NG):
                xv = xb[:, g, :]
                v.tensor_tensor(sc_["xm"][:], xv, sc_["m"][:], ALU.subtract)
                if out_slice is None:
                    o = dst[:, g, :]
                else:
                    o = dst[:, g, out_slice]
                ins = v.tensor_tensor(o, sc_["xm"][:], sc_["r"][:], ALU.mult)
            dv.inc(ins)
        return [("vector", v_xq)]

    def phase_ln1(self, s):
        pe, acs, dv = self.F
        xb = self.xbuf[s % 3]
        items = self._ln_stats_items(xb, self.F, (self.bk[4], self.bk[5]),
                                     pe_b=pe.n, acs_b=acs.n, dv_b=dv.n,
                                     x_tok=(dv, dv.n))
        r_tok = acs.n + NG + 2
        pw = (self.pef, self._w01_done.get(s - 1, 0))
        items += self._ln_apply_items(xb, self.x1q, self.F, r_tok, pre_wait=pw)
        x1q_tok = dv.n + (NG + 1) + 1
        for it in items:
            self.sect(*it)
        assert dv.n == x1q_tok, (dv.n, x1q_tok)
        self._x1q_tok = x1q_tok

    def ffn_items(self, s):
        pe, acs, dv = self.F
        dgw = [self.dgwA, self.dgwB]
        dgw2 = self.dgw2
        calls = []
        if s < NCH:
            calls.append((0, self.x1q, self.act[0], s, ("F", self._x1q_tok)))
        if s >= 2:
            calls.append((1, self.x3q[s % 2], self.act[1], s - 2,
                          ("M", self._x3q_tok[s - 2])))
        ncal = len(calls)
        if ncal == 0:
            return []

        dgw_base = [dgw[0].n, dgw[1].n]
        dgw2_base = dgw2.n
        pe_base = pe.n
        acs_base = acs.n
        dv_base = dv.n

        def gidx(i, ci):
            return i * ncal + ci

        G = NSUP * ncal
        items = []

        def g_w(g):
            g.wait_ge(pe.h, pe_base)
            if s == 0:
                g.wait_ge(self.dg0.h, self._dg0_all)
            for i in range(NSUP):
                if i >= 2:
                    g.wait_ge(pe.h, pe_base + (i - 1) * ncal)
                for W in range(2):
                    dgw[i % 2].inc(g.dma_start(
                        self.w01b[:, i % 2, W],
                        self.w01s[W, i].rearrange("kt kp m -> kp kt m")), 16)
            for dj in range(NDJ):
                for hh in range(2):
                    u = 2 * dj + hh
                    if u >= 2:
                        g.wait_ge(pe.h, pe_base + G + u - 1)
                    dgw2.inc(g.dma_start(
                        self.w2b[u % 2][:],
                        self.w2s[dj, hh * 32:(hh + 1) * 32]
                        .rearrange("kh hp d -> hp kh d")), 16)
        items.append(("gpsimd", g_w))

        entry_waits = []
        for c, _, _, _, _ in calls:
            for mt in range(2):
                b = self.bk[c * 2 + mt]
                if b.free_tok is not None:
                    entry_waits.append(b.free_tok)
                    b.free_tok = None

        for i in range(NSUP):
            for ci, (c, xq, actb, cs, rhs_tok) in enumerate(calls):
                g = gidx(i, ci)

                def t_mm(t, i=i, ci=ci, c=c, xq=xq, g=g, rhs_tok=rhs_tok):
                    if i == 0 and ci == 0:
                        for sem, n in entry_waits:
                            t.wait_ge(sem.h, n)
                    if i == 0:
                        fam, n = rhs_tok
                        t.wait_ge((self.dvf if fam == "F" else self.dvm).h, n)
                    t.wait_ge(dgw[i % 2].h, dgw_base[i % 2] + 32 * (i // 2 + 1))
                    if i >= 1:
                        t.wait_ge(acs.h, acs_base + 2 * gidx(i - 1, ci) + 1)
                    ins = None
                    for mt in range(2):
                        b = self.bk[c * 2 + mt]
                        for kt in range(DCN):
                            t.matmul(b.ps[:, 0:256],
                                     self.w01b[:, i % 2, 0, kt, mt * 128:(mt + 1) * 128],
                                     xq[:, kt, :],
                                     start=(kt == 0), stop=False)
                        for kt in range(DCN):
                            ins = t.matmul(b.ps[:, 256:512],
                                           self.w01b[:, i % 2, 1, kt, mt * 128:(mt + 1) * 128],
                                           xq[:, kt, :],
                                           start=False, stop=(kt == DCN - 1))
                    pe.inc(ins)
                items.append(("tensor", t_mm))

                def s_evx2(sc, ci=ci, c=c, g=g):
                    sc.wait_ge(pe.h, pe_base + g + 1)
                    if g >= 1:
                        sc.wait_ge(dv.h, dv_base + 2 * (g - 1) + 2)
                    for mt in range(2):
                        b = self.bk[c * 2 + mt]
                        sc.activation(self.gf16[ci][:, mt, :], b.ps[:], AF.Copy)
                    acs.inc(sc.activation(self.ch_x2[:],
                                          self.gf16[ci][:, :, 0:256], AF.Square))
                items.append(("scalar", s_evx2))

                def v_chA(v, ci=ci, g=g):
                    gv_ = self.gf16[ci][:, :, 0:256]
                    v.wait_ge(acs.h, acs_base + 2 * g + 1)
                    v.tensor_tensor(self.ch_x3[:], self.ch_x2[:], gv_, ALU.mult)
                    v.tensor_scalar_mul(self.ch_t[:], self.ch_x3[:], C1)
                    v.tensor_tensor(self.ch_t[:], self.ch_t[:], gv_, ALU.add)
                    dv.inc(v.tensor_scalar_mul(self.ch_u[:], self.ch_t[:], C2))
                items.append(("vector", v_chA))

                def s_tanh(sc, g=g):
                    sc.wait_ge(dv.h, dv_base + 2 * g + 1)
                    acs.inc(sc.activation(self.ch_t[:], self.ch_u[:], AF.Tanh))
                items.append(("scalar", s_tanh))

                def v_chB(v, i=i, ci=ci, g=g, actb=actb):
                    gv_ = self.gf16[ci][:, :, 0:256]
                    fv_ = self.gf16[ci][:, :, 256:512]
                    if i == 0:
                        v.wait_ge(pe.h, pe_base)
                    v.wait_ge(acs.h, acs_base + 2 * g + 2)
                    v.tensor_scalar(self.ch_cdf[:], self.ch_t[:], 1.0, 0.5,
                                    ALU.add, ALU.mult)
                    v.tensor_tensor(self.ch_gv[:], self.ch_cdf[:], gv_, ALU.mult)
                    v.tensor_tensor(self.ch_a16[:], self.ch_gv[:], fv_, ALU.mult)
                    dv.inc(v.tensor_copy(actb[:, 2 * i:2 * i + 2, :], self.ch_a16[:]))
                items.append(("vector", v_chB))

        chain_done = dv_base + 2 * G
        self._w01_done[s] = pe_base + G
        for c, _, _, _, _ in calls:
            for mt in range(2):
                self.bk[c * 2 + mt].free_tok = (acs, acs_base + 2 * (G - 1) + 1)

        pe2_base = pe_base + G
        acs2_base = acs_base + 2 * G
        w1_evac_done = acs_base + 2 * (G - 1) + 1
        hh0_done = dv_base + 2 * (NSUP // 2) * ncal

        for dj in range(NDJ):
            def t_w2(t, dj=dj):
                if dj <= 1:
                    t.wait_ge(acs.h, w1_evac_done)
                    if dj == 0:
                        t.wait_ge(dv.h, hh0_done)
                if dj >= 2:
                    t.wait_ge(acs.h, acs2_base + (dj - 1) * ncal)
                for hh in range(2):
                    u = 2 * dj + hh
                    t.wait_ge(dgw2.h, dgw2_base + 16 * (u + 1))
                    if dj == 0 and hh == 1:
                        t.wait_ge(dv.h, chain_done)
                    ins = None
                    for ci, (c, xq, actb, cs, _) in enumerate(calls):
                        b = self.bk[c * 2 + dj % 2]
                        for kh in range(32):
                            ins = t.matmul(b.ps[:, 0:256],
                                           self.w2b[u % 2][:, kh, :],
                                           actb[:, hh * 32 + kh, :],
                                           start=(hh == 0 and kh == 0),
                                           stop=(hh == 1 and kh == 31))
                    pe.inc(ins)
            items.append(("tensor", t_w2))

            for ci in range(ncal):
                def s_w2(sc, dj=dj, ci=ci):
                    sc.wait_ge(pe.h, pe2_base + 2 * dj + 2)
                    if dj >= 2:
                        sc.wait_ge(dv.h, chain_done + (dj - 2) * ncal + ci + 1)
                    c = calls[ci][0]
                    acs.inc(sc.activation(self.out16[2 * (dj % 2) + ci][:],
                                          self.bk[c * 2 + dj % 2].ps[:, 0:256],
                                          AF.Copy))
                items.append(("scalar", s_w2))

                def v_w2(v, dj=dj, ci=ci):
                    c, xq, actb, cs, _ = calls[ci]
                    v.wait_ge(acs.h, acs2_base + dj * ncal + ci + 1)
                    xbv = self.xbuf[cs % 3]
                    dv.inc(v.scalar_tensor_tensor(xbv[:, dj, :],
                                                  self.out16[2 * (dj % 2) + ci][:], 0.5,
                                                  xbv[:, dj, :], ALU.mult, ALU.add))
                items.append(("vector", v_w2))

        for ci, (c, xq, actb, cs, _) in enumerate(calls):
            for p in range(2):
                last_dj = NDJ - 2 + p
                self.bk[c * 2 + p].free_tok = (acs, acs2_base + last_dj * ncal + ci + 1)

        self._ffn_pe_end[s] = pe2_base + 2 * NDJ
        end_dv = chain_done + NDJ * ncal
        for ci, (c, xq, actb, cs, _) in enumerate(calls):
            self._x1done[cs] = end_dv
        if s >= 2:
            self._x2done = end_dv
        return items

    def phase_out(self, c):
        pe, dv, ds = self.pef, self.dvf, self.dsoo
        xb = self.xbuf[c % 3]
        done_tok = self._x2done
        pe_base = pe.n
        dv_base = dv.n
        ds_base = ds.n

        def t_tr(t):
            t.wait_ge(dv.h, done_tok)
            k = 0
            for th in range(2):
                for dcg in range(4):
                    k += 1
                    b = self.bk[4 + (th * 4 + dcg) % 2]
                    if k <= 2:
                        b.wait_free(t)
                    else:
                        t.wait_ge(dv.h, dv_base + (k - 2))
                    for j in range(4):
                        dc = dcg * 4 + j
                        ins = t.transpose(b.ps[:, j * 128:(j + 1) * 128],
                                          xb[:, dc, th * 128:(th + 1) * 128],
                                          self.id_s[:])
                    pe.inc(ins)

        self.sect("tensor", t_tr)

        def v_evac(v):
            k = 0
            for th in range(2):
                for dcg in range(4):
                    k += 1
                    if k == 5:
                        v.wait_ge(ds.h, ds_base + 16)
                    b = self.bk[4 + (th * 4 + dcg) % 2]
                    v.wait_ge(pe.h, pe_base + k)
                    ins = v.tensor_copy(
                        self.xio[:, dcg * 512:(dcg + 1) * 512]
                        .rearrange("p (a t) -> p a t", a=4),
                        b.ps[:].rearrange("p (a t) -> p a t", a=4))
                    dv.inc(ins)
                    b.freed_by(dv)

        self.sect("vector", v_evac)

        def s_out(sy):
            for th in range(2):
                sy.wait_ge(dv.h, dv_base + 4 * (th + 1))
                ds.inc(sy.dma_start(
                    self.out[c * TC + th * 128: c * TC + (th + 1) * 128, :],
                    self.xio[:]), 16)

        self.sect("sync", s_out)
        self._out_done[c] = ds.n

    def mamba_items(self, m):
        pe, acs, dv = self.M
        xb = self.xbuf[m % 3]
        items = []

        pe0, ac0, dv0 = pe.n, acs.n, dv.n

        x_tok = (self.dvf, self._x1done[m])
        items += self._ln_stats_items(xb, self.M, (self.bk[6], self.bk[7]),
                                      pe_b=pe0, acs_b=ac0, dv_b=dv0,
                                      x_tok=x_tok, with_m2=True)
        pe1 = pe0 + NG
        ac1 = ac0 + NG + 2
        dv1 = dv0 + NG + 1
        r_tok = ac1

        def v_halo(v):
            v.wait_ge(acs.h, r_tok)
            dv.inc(v.tensor_copy(self.xh2[:, :, 1:4], self.xh2_tail[:]))
        items.append(("vector", v_halo))
        dv2 = dv1 + 1

        items += self._ln_apply_items(xb, self.xh2, self.M, r_tok,
                                      out_slice=slice(4, 4 + TC))
        xh2_tok = dv2 + 1

        def v_tail(v):
            v.tensor_copy(self.xh2_tail[:], self.xh2[:, :, TC + 1:TC + 4])
        items.append(("vector", v_tail))

        acs_cv = ac1
        for dc in range(DCN):
            def v_taps(v, dc=dc):
                if dc == 0:
                    v.wait_ge(dv.h, xh2_tok)
                else:
                    v.wait_ge(acs.h, acs_cv + dc)
                v.tensor_scalar_mul(self.xcb[:], self.xh2[:, dc, 1:1 + TC],
                                    self.convw_s[:, dc, 0:1])
                for j in range(1, 4):
                    v.scalar_tensor_tensor(self.xcb[:],
                                           self.xh2[:, dc, 1 + j:1 + j + TC],
                                           self.convw_s[:, dc, j:j + 1],
                                           self.xcb[:], ALU.mult, ALU.add)
                dv.inc(v.tensor_scalar_add(self.xcb[:], self.xcb[:],
                                           self.convb_s[:, dc:dc + 1]))
            items.append(("vector", v_taps))

            def s_tanh(sc, dc=dc):
                sc.wait_ge(dv.h, xh2_tok + 2 * dc + 1)
                acs.inc(sc.activation(self.th_s[:], self.xcb[:], AF.Tanh, scale=0.5))
            items.append(("scalar", s_tanh))

            def v_silu(v, dc=dc):
                v.wait_ge(acs.h, acs_cv + dc + 1)
                v.scalar_tensor_tensor(self.u_s[:], self.th_s[:], 1.0, self.xcb[:],
                                       ALU.add, ALU.mult)
                dv.inc(v.tensor_scalar_mul(self.xconv[:, dc, :], self.u_s[:], 0.5))
            items.append(("vector", v_silu))

        dv3 = xh2_tok + 32
        ac2 = acs_cv + 16

        bdt, bbc = self.bk[6], self.bk[7]
        pe_pj = pe1

        def t_proj(t):
            t.wait_ge(dv.h, dv3)
            if m == 0:
                t.wait_ge(self.dsw.h, self._projw_tok)
            bdt.wait_free(t)
            bbc.wait_free(t)
            ins = None
            for kt in range(DCN):
                t.matmul(bdt.ps[:, 0:256], self.xproj_s[:, kt, 0:128], self.xconv[:, kt, :],
                         start=(kt == 0), stop=(kt == DCN - 1))
                ins = t.matmul(bbc.ps[0:32, 0:256], self.xproj_s[:, kt, 128:160],
                               self.xconv[:, kt, :],
                               start=(kt == 0), stop=(kt == DCN - 1))
            pe.inc(ins)
        items.append(("tensor", t_proj))

        def v_pj(v):
            v.wait_ge(pe.h, pe_pj + 1)
            v.tensor_copy(self.dt16[:], bdt.ps[:, 0:256])
            dv.inc(v.tensor_copy(self.bc16[0:32, :], bbc.ps[0:32, 0:256]))
            bdt.freed_by(dv)
            bbc.freed_by(dv)
        items.append(("vector", v_pj))
        dv_pj = dv3 + 1

        pe_d = pe_pj + 1
        acs_d = ac2
        dv_d = dv_pj
        for dc in range(DCN):
            def t_delta(t, dc=dc):
                b = self.bk[6 + dc % 2]
                if dc < 2:
                    t.wait_ge(dv.h, dv_pj)
                else:
                    t.wait_ge(dv.h, dv_d + 3 * (dc - 2) + 2)
                pe.inc(t.matmul(b.ps[:, 0:256], self.dtproj_s[:, dc, :], self.dt16[:],
                                start=True, stop=True))
            items.append(("tensor", t_delta))

            def s_z(sc, dc=dc):
                b = self.bk[6 + dc % 2]
                sc.wait_ge(pe.h, pe_d + dc + 1)
                if dc >= 1:
                    sc.wait_ge(dv.h, dv_d + 3 * (dc - 1) + 2)
                acs.inc(sc.activation(self.z_s[:], b.ps[:, 0:256], AF.Identity,
                                      bias=self.dtb_s[:, dc:dc + 1]))
            items.append(("scalar", s_z))

            def v_az(v, dc=dc):
                v.wait_ge(acs.h, acs_d + 2 * dc + 1)
                dv.inc(v.scalar_tensor_tensor(self.az_s[:], self.z_s[:], -1.0, self.z_s[:],
                                              ALU.mult, ALU.max))
            items.append(("vector", v_az))

            def s_expln(sc, dc=dc):
                sc.wait_ge(dv.h, dv_d + 3 * dc + 1)
                sc.activation(self.e_s[:], self.az_s[:], AF.Exp, scale=-1.0)
                acs.inc(sc.activation(self.e_s[:], self.e_s[:], AF.Ln, bias=1.0))
            items.append(("scalar", s_expln))

            def v_dl(v, dc=dc):
                v.wait_ge(acs.h, acs_d + 2 * dc + 2)
                dv.inc(v.scalar_tensor_tensor(self.delta[:, dc, :], self.z_s[:], 0.0,
                                              self.e_s[:], ALU.max, ALU.add))
                dv.inc(v.tensor_tensor(self.wtd[:, dc, :], self.delta[:, dc, :],
                                       self.xconv[:, dc, :], ALU.mult))
            items.append(("vector", v_dl))

        delta_tok = dv_d + 3 * DCN
        ac3 = acs_d + 2 * DCN
        pe3 = pe_d + DCN
        tok_y = [(dv, dv_d + 3 * (DCN - 2) + 2), (dv, dv_d + 3 * (DCN - 1) + 2)]

        dsbc = self.dsbc
        bc_base = dsbc.n

        def s_bc(sy):
            sy.wait_ge(dv.h, delta_tok)
            sy.wait_ge(pe.h, pe_pj + 1)
            for np_ in range(NP):
                for n2 in range(2):
                    dsbc.inc(sy.dma_start(
                        self.Bbc[n2 * 64:(n2 + 1) * 64, np_, :],
                        self.bc16[2 * np_ + n2:2 * np_ + n2 + 1, :].unsqueeze(1)
                        .broadcast_to([1, 64, TC])), 16)
                    dsbc.inc(sy.dma_start(
                        self.Cbc[n2 * 64:(n2 + 1) * 64, np_, :],
                        self.bc16[16 + 2 * np_ + n2:16 + 2 * np_ + n2 + 1, :].unsqueeze(1)
                        .broadcast_to([1, 64, TC])), 16)
        items.append(("sync", s_bc))
        bc_tok = bc_base + 16 * 32

        dsp = [self.dse, self.dso]
        dsp_base = [self.dse.n, self.dso.n]
        acs_sc = ac3
        scan_dv = delta_tok
        pe_y = pe3

        def sdv(nscan, ny):
            return scan_dv + 5 * nscan + ny

        def after_scan(dblk):
            return sdv(dblk + 1, dblk // 2)

        def after_y(dc):
            return sdv(2 * dc + 2, dc + 1)

        def s_ddn(sy):
            for dblk in range(NDBLK):
                par = dblk % 2
                if dblk >= 2:
                    sy.wait_ge(dv.h, after_scan(dblk - 2))
                    sy.wait_ge(acs.h, acs_sc + (dblk - 1))
                p0 = 64 * (dblk % 2)
                dc = dblk // 2
                for n2 in range(2):
                    dsp[par].inc(sy.dma_start(
                        self.ddn[par][n2 * 64:(n2 + 1) * 64, :],
                        self.delta[p0:p0 + 64, dc, :]), 16)
                    dsp[par].inc(sy.dma_start(
                        self.wdn[par][n2 * 64:(n2 + 1) * 64, :],
                        self.wtd[p0:p0 + 64, dc, :]), 16)
        items.append(("sync", s_ddn))

        for dblk in range(NDBLK):
            par = dblk % 2

            def s_exp(sc, dblk=dblk, par=par):
                sc.wait_ge(dsp[par].h, dsp_base[par] + 64 * (dblk // 2 + 1))
                if dblk >= 1:
                    sc.wait_ge(dv.h, after_scan(dblk - 1))
                ins = None
                for np_ in range(NP):
                    ins = sc.activation(self.a_g[:, np_, 1:SEPW], self.ddn[par][:],
                                        AF.Exp,
                                        scale=self.A_perm[:, dblk * 8 + np_:dblk * 8 + np_ + 1])
                acs.inc(ins)
            items.append(("scalar", s_exp))

            def v_scan(v, dblk=dblk, par=par):
                if dblk == 0:
                    v.wait_ge(dsbc.h, bc_tok)
                v.wait_ge(acs.h, acs_sc + dblk + 1)
                if dblk >= 2:
                    v.wait_ge(pe.h, pe_y + dblk - 1)
                v.tensor_copy(self.b_g[:, :, 0:1],
                              self.h_state[:, dblk * 8:dblk * 8 + 8].unsqueeze(2))
                v.tensor_tensor(self.b_g[:, :, 1:SEPW],
                                self.wdn[par][:].unsqueeze(1).broadcast_to([128, NP, TC]),
                                self.Bbc[:], ALU.mult)
                v.tensor_tensor_scan(self.h_g[:].rearrange("p a t -> p (a t)"),
                                     self.a_g[:].rearrange("p a t -> p (a t)"),
                                     self.b_g[:].rearrange("p a t -> p (a t)"),
                                     0.0, ALU.mult, ALU.add)
                v.tensor_copy(self.h_state[:, dblk * 8:dblk * 8 + 8],
                              self.h_g[:, :, SEPW - 1:SEPW].squeeze(2))
                ins = v.tensor_tensor(self.y_g[dblk % 2][:, :, 1:SEPW],
                                      self.h_g[:, :, 1:SEPW],
                                      self.Cbc[:], ALU.mult)
                dv.inc(ins, 5)
            items.append(("vector", v_scan))

            def t_red(t, dblk=dblk):
                t.wait_ge(dv.h, after_scan(dblk))
                dc = dblk // 2
                b = self.bk[6 + dc % 2]
                if dblk % 2 == 0:
                    if dc < 2:
                        if tok_y[dc % 2] is not None:
                            sem, n = tok_y[dc % 2]
                            t.wait_ge(sem.h, n)
                            tok_y[dc % 2] = None
                    else:
                        t.wait_ge(dv.h, after_y(dc - 2))
                R = self.re_s if dblk % 2 == 0 else self.ro_s
                ins = None
                for np_ in range(NP):
                    ins = t.matmul(b.ps[:, 0:256], R[:],
                                   self.y_g[dblk % 2][:, np_, 1:SEPW],
                                   start=(dblk % 2 == 0 and np_ == 0),
                                   stop=(dblk % 2 == 1 and np_ == NP - 1))
                pe.inc(ins)
            items.append(("tensor", t_red))

            if dblk % 2 == 1:
                def v_y(v, dblk=dblk):
                    dc = dblk // 2
                    b = self.bk[6 + dc % 2]
                    v.wait_ge(pe.h, pe_y + 2 * dc + 2)
                    v.tensor_tensor(self.ytmp[:], xb[:, dc, :], self.m2_t[:],
                                    ALU.subtract)
                    v.scalar_tensor_tensor(self.ytmp2[:], self.ytmp[:],
                                           self.Dp_s[:, dc:dc + 1], self.r2_t[:],
                                           ALU.mult, ALU.mult)
                    v.tensor_tensor(self.ytmp[:], self.ytmp2[:], b.ps[:, 0:256],
                                    ALU.add)
                    dv.inc(v.tensor_tensor(xb[:, dc, :], xb[:, dc, :], self.ytmp[:],
                                           ALU.add))
                    b.freed_by(dv)
                items.append(("vector", v_y))

        dv4 = scan_dv + 5 * NDBLK + DCN
        pe4 = pe_y + NDBLK
        ac4 = acs_sc + NDBLK

        items += self._ln_stats_items(xb, self.M, (self.bk[6], self.bk[7]),
                                      pe_b=pe4, acs_b=ac4, dv_b=dv4)
        r3_tok = ac4 + NG + 2
        pwait = (self.pef, self._ffn_pe_end.get(m, 0))
        items += self._ln_apply_items(xb, self.x3q[m % 2], self.M, r3_tok,
                                      pre_wait=pwait)
        self._x3q_tok[m] = dv4 + NG + 1 + 1
        self._expect_m = (pe4 + NG, ac4 + NG + 2, dv4 + NG + 1 + 1)
        return items

    def finalize(self):
        sems = [self.dg0, self.dgwA, self.dgwB, self.dgw2, self.dsbc,
                self.dse, self.dso, self.dsx, self.dsoo, self.dsw]
        tgts = [(s.h, s.n) for s in sems]

        def s_fin(sy):
            for h, n in tgts:
                sy.wait_ge(h, n)

        self.sect("sync", s_fin)


def build_program():
    b = Builder()
    with b.es:
        return b.build()


_NC_CACHE = None


def kernel(L0_attn_x, L0_attn_conv_w, L0_attn_conv_b, L0_attn_xproj_w,
           L0_attn_dtproj_w, L0_attn_dtproj_b, L0_attn_A_log, L0_attn_D_param,
           L0_ffn_w_gating, L0_ffn_w_linear):
    global _NC_CACHE
    X = np.ascontiguousarray(np.asarray(L0_attn_x, np.float32))
    consts = {
        "conv_w": np.ascontiguousarray(np.asarray(L0_attn_conv_w, np.float32)),
        "conv_b": np.ascontiguousarray(np.asarray(L0_attn_conv_b, np.float32)),
        "xproj_w": np.ascontiguousarray(np.asarray(L0_attn_xproj_w, np.float32)),
        "dtproj_w": np.ascontiguousarray(np.asarray(L0_attn_dtproj_w, np.float32)),
        "dtproj_b": np.ascontiguousarray(np.asarray(L0_attn_dtproj_b, np.float32)),
        "A_perm_log": np.ascontiguousarray(
            np.asarray(L0_attn_A_log, np.float32)
            .reshape(32, 64, 8, 2).transpose(3, 1, 0, 2).reshape(128, 256)),
        "D_param": np.ascontiguousarray(np.asarray(L0_attn_D_param, np.float32)),
        "w_gating": np.ascontiguousarray(np.asarray(L0_ffn_w_gating, np.float32)),
        "w_linear": np.ascontiguousarray(np.asarray(L0_ffn_w_linear, np.float32)),
        "ident": np.eye(128, dtype=np.float32),
        "fconst": np.tile(np.array([[EPS, 1.0, -0.5, -1.0, 0.5, INVD, 0.0, 0.0]],
                                   np.float32), (128, 1)),
        "ones16": np.ones((128, 128), dtype=ml_dtypes.bfloat16),
    }
    re_ = np.zeros((128, 128), np.float32)
    ro_ = np.zeros((128, 128), np.float32)
    for p in range(128):
        re_[p, p % 64] = 1.0
        ro_[p, 64 + (p % 64)] = 1.0
    consts["r_even"] = re_.astype(ml_dtypes.bfloat16)
    consts["r_odd"] = ro_.astype(ml_dtypes.bfloat16)

    if _NC_CACHE is None:
        _NC_CACHE = build_program()
    nc = _NC_CACHE
    in_maps = [dict(consts, x=np.ascontiguousarray(X[b])) for b in range(8)]
    res = run_bass_kernel_spmd(nc, in_maps, core_ids=list(range(8)))
    return np.stack([np.asarray(res.results[b]["out"], np.float32) for b in range(8)],
                    axis=0)


# version 07
# speedup vs baseline: 1.1935x; 1.0053x over previous
import contextlib
import numpy as np
import ml_dtypes
import concourse.bass as bass
import concourse.mybir as mybir
from concourse.bass_utils import run_bass_kernel_spmd

F32 = mybir.dt.float32
BF16 = mybir.dt.bfloat16
E4 = mybir.dt.float8e4
AF = mybir.ActivationFunctionType
ALU = mybir.AluOpType

T, D, H = 2048, 2048, 8192
NS, RNK = 16, 128
TC = 256
NCH = T // TC
DCN = D // 128
HCN = H // 128
NSUP = 32
NDJ = 16
NDBLK = 32
NP = 8
SEPW = 257
EPS = 1e-5
C1 = float(np.float32(ml_dtypes.bfloat16(0.044715)))
C2 = float(np.float32(ml_dtypes.bfloat16(np.sqrt(2.0 / np.pi))))
INVD = 1.0 / D
NG = 16


class Sem:
    def __init__(self, es, nc, name):
        self.h = es.enter_context(nc.semaphore(name))
        self.n = 0

    def inc(self, inst, amt=1):
        inst.then_inc(self.h, amt)
        self.n += amt
        return inst


class Bank:
    def __init__(self, es, nc, name):
        self.ps = es.enter_context(nc.psum_tensor(name, [128, 512], F32))
        self.free_tok = None

    def freed_by(self, sem):
        self.free_tok = (sem, sem.n)

    def wait_free(self, eng):
        if self.free_tok is not None:
            sem, n = self.free_tok
            eng.wait_ge(sem.h, n)
            self.free_tok = None


class Builder:
    def __init__(self):
        self.nc = bass.Bass("TRN2", target_bir_lowering=False,
                            detect_race_conditions=False)
        self.es = contextlib.ExitStack()
        self._out_done = {}
        self._x1done = {}
        self._ffn_pe_end = {}
        self._w01_done = {}
        self._x3q_tok = {}

    def sect(self, eng, fn):
        getattr(self.block, eng)(fn)

    def _tbl(self, sc):
        sc.add_instruction(mybir.InstLoadActFuncSet(
            name=self.nc.get_next_instruction_name(),
            act_func_set_id=6, ins=[], outs=[]))

    def _overlay(self, name, shape, dt, base_handle, byte_off):
        nc = self.nc
        h = nc.virtual_sbuf_tensor(name, shape, dt)
        nc._tensor(h.name, h.shape, h.dtype, type="SB")
        mloc = nc.lookup_mloc(h)
        mloc.allocated = True
        mloc.addr = nc.lookup_mloc(base_handle).addr + byte_off
        return h

    def run_items(self, *lists):
        lists = [l for l in lists if l]
        if not lists:
            return
        main = max(lists, key=len)
        rest = [l for l in lists if l is not main]
        n = len(main)
        idxs = [0] * len(rest)
        for k, (eng, fn) in enumerate(main):
            self.sect(eng, fn)
            for li, l in enumerate(rest):
                want = (k + 1) * len(l) // n
                while idxs[li] < want:
                    e2, f2 = l[idxs[li]]
                    self.sect(e2, f2)
                    idxs[li] += 1
        for li, l in enumerate(rest):
            while idxs[li] < len(l):
                e2, f2 = l[idxs[li]]
                self.sect(e2, f2)
                idxs[li] += 1

    def build(self):
        nc, es = self.nc, self.es
        self.x_in = nc.dram_tensor("x", [T, D], F32, kind="ExternalInput").ap()
        self.conv_w = nc.dram_tensor("conv_w", [D, 1, 4], F32, kind="ExternalInput").ap()
        self.conv_b = nc.dram_tensor("conv_b", [D], F32, kind="ExternalInput").ap()
        self.xproj_w = nc.dram_tensor("xproj_w", [160, D], F32, kind="ExternalInput").ap()
        self.dtproj_w = nc.dram_tensor("dtproj_w", [D, RNK], F32, kind="ExternalInput").ap()
        self.dtproj_b = nc.dram_tensor("dtproj_b", [D], F32, kind="ExternalInput").ap()
        self.A_permin = nc.dram_tensor("A_perm_log", [128, 256], F32, kind="ExternalInput").ap()
        self.D_param = nc.dram_tensor("D_param", [D], F32, kind="ExternalInput").ap()
        self.wg = nc.dram_tensor("w_gating", [2, D, H], F32, kind="ExternalInput").ap()
        self.wl = nc.dram_tensor("w_linear", [H, D], F32, kind="ExternalInput").ap()
        self.ident = nc.dram_tensor("ident", [128, 128], F32, kind="ExternalInput").ap()
        self.ones16 = nc.dram_tensor("ones16", [128, 128], BF16, kind="ExternalInput").ap()
        self.r_even = nc.dram_tensor("r_even", [128, 128], BF16, kind="ExternalInput").ap()
        self.r_odd = nc.dram_tensor("r_odd", [128, 128], BF16, kind="ExternalInput").ap()
        self.fconst = nc.dram_tensor("fconst", [128, 8], F32, kind="ExternalInput").ap()
        self.out = nc.dram_tensor("out", [T, D], F32, kind="ExternalOutput").ap()
        self.w01s = nc.dram_tensor("w01s", [2, NSUP, 16, 128, 256], E4).ap()
        self.w2s = nc.dram_tensor("w2s", [NDJ, HCN, 128, 128], E4).ap()
        self.xproj16 = nc.dram_tensor("xproj16", [160, D], BF16).ap()
        self.dtproj16 = nc.dram_tensor("dtproj16", [D, RNK], BF16).ap()

        sb = lambda name, shape, dt: es.enter_context(nc.sbuf_tensor(name, shape, dt))
        self.xbuf = [sb(f"xbuf{i}", [128, DCN, TC], F32) for i in range(3)]
        self.xio = sb("xio", [128, 2048], F32)
        self.x1q = sb("x1q", [128, DCN, TC], E4)
        self.x3q = [sb(f"x3q{i}", [128, DCN, TC], E4) for i in range(2)]
        _gf = sb("gf16", [128, 2, 512], BF16)
        self.gf16 = [_gf, _gf]
        self.ch_x2 = sb("ch_x2", [128, 2, 256], BF16)
        self.ch_t = sb("ch_t", [128, 2, 256], BF16)
        self.ch_cdf = sb("ch_cdf", [128, 2, 256], BF16)
        self.ch_gv = sb("ch_gv", [128, 2, 256], BF16)
        self.ch_a16 = self.ch_t
        self.ch_x3 = self.ch_cdf
        self.ch_u = self.ch_gv
        self.act = [sb(f"act{c}", [128, HCN, TC], E4) for c in range(2)]
        self.w01b = sb("w01b", [128, 2, 2, 16, 256], E4)
        self.w2b = [sb("w2b0", [128, HCN // 2, 128], E4),
                    self._overlay("w2b1", [128, HCN // 2, 128], E4,
                                  self.xio, 2048)]
        self.out16 = [self._overlay(f"out16_{i}", [128, TC], BF16,
                                    self.xio, 512 * i) for i in range(4)]
        self.lnscr = {}
        for fam in ("F", "M"):
            d = dict(
                q=sb(f"q_{fam}", [128, 256], F32),
                qh=sb(f"qh_{fam}", [128, 256], BF16),
                ql=sb(f"ql_{fam}", [128, 256], BF16),
                xh=sb(f"xh_{fam}", [128, 256], BF16),
                xl=sb(f"xl_{fam}", [128, 256], BF16),
                m=sb(f"m_{fam}", [128, TC], F32),
                msq=sb(f"msq_{fam}", [128, TC], F32),
                r=sb(f"r_{fam}", [128, TC], F32),
            )
            d["var"] = d["msq"]
            d["xm"] = d["q"]
            self.lnscr[fam] = d
        self.m2_t = sb("m2_t", [128, TC], F32)
        self.r2_t = sb("r2_t", [128, TC], F32)
        self.xh2 = sb("xh2", [128, DCN, TC + 4], BF16)
        self.xcb = sb("xcb", [128, TC], F32)
        self.th_s = sb("th_s", [128, TC], F32)
        self.u_s = self.lnscr["M"]["q"]
        self.xconv = sb("xconv", [128, DCN, TC], BF16)
        self.Bbc = self.xconv[:, 0:NP, :]
        self.Cbc = self.xconv[:, NP:2 * NP, :]
        self.dt16 = sb("dt16", [128, TC], BF16)
        self.bc16 = sb("bc16", [128, TC], BF16)
        self.delta = sb("delta", [128, DCN, TC], F32)
        self.wtd = sb("wtd", [128, DCN, TC], BF16)
        self.z_s = self.xcb
        self.az_s = self.u_s
        self.e_s = self.th_s
        self.ytmp = self.xcb
        self.ytmp2 = self.th_s
        self.ddn = [sb(f"ddn{i}", [128, TC], F32) for i in range(2)]
        self.wdn = [sb(f"wdn{i}", [128, TC], BF16) for i in range(2)]
        self.a_g = sb("a_g", [128, NP, SEPW], BF16)
        self.b_g = sb("b_g", [128, NP, SEPW], BF16)
        self.h_g = self.b_g
        self.xh2_tail = sb("xh2_tail", [128, DCN, 3], BF16)
        self.y_g = [self._overlay(f"y_g{i}", [128, NP, SEPW], BF16,
                                  self.xh2, 4112 * i) for i in range(2)]
        self.h_state = sb("h_state", [128, 256], F32)
        self.A_perm = sb("A_perm", [128, 256], F32)
        self.fconst_s = sb("fconst_s", [128, 8], F32)
        self.id_s = sb("id_s", [128, 128], F32)
        self.ones_s = sb("ones_s", [128, 128], BF16)
        self.re_s = sb("re_s", [128, 128], BF16)
        self.ro_s = sb("ro_s", [128, 128], BF16)
        self.convw_s = sb("convw_s", [128, DCN, 4], F32)
        self.convb_s = sb("convb_s", [128, DCN], F32)
        self.dtb_s = sb("dtb_s", [128, DCN], F32)
        self.Dp_s = sb("Dp_s", [128, DCN], F32)
        self.xproj_s = sb("xproj_s", [128, DCN, 160], BF16)
        self.dtproj_s = sb("dtproj_s", [128, DCN, 128], BF16)

        for j, val in enumerate([EPS, 1.0, -0.5, -1.0, 0.5, INVD]):
            nc.const_aps.register(F32, val, self.fconst_s[:, j:j + 1])

        self.bk = [Bank(es, nc, f"bk{i}") for i in range(8)]

        self.dg0 = Sem(es, nc, "dg0")
        self.dgwA = Sem(es, nc, "dgwA")
        self.dgwB = Sem(es, nc, "dgwB")
        self.dgw2 = Sem(es, nc, "dgw2")
        self.dsbc = Sem(es, nc, "dsbc")
        self.dse = Sem(es, nc, "dse")
        self.dso = Sem(es, nc, "dso")
        self.dsx = Sem(es, nc, "dsx")
        self.dsoo = Sem(es, nc, "dsoo")
        self.dsw = Sem(es, nc, "dsw")
        self.pef = Sem(es, nc, "pef")
        self.acf = Sem(es, nc, "acf")
        self.dvf = Sem(es, nc, "dvf")
        self.pem = Sem(es, nc, "pem")
        self.acm = Sem(es, nc, "acm")
        self.dvm = Sem(es, nc, "dvm")
        self.F = (self.pef, self.acf, self.dvf)
        self.M = (self.pem, self.acm, self.dvm)

        self.block = es.enter_context(nc.Block())
        self.emit()
        self.finalize()
        return nc

    def emit(self):
        self.phase0()
        for s in range(NCH + 2):
            if s < NCH:
                self.phase_xload(s)
                self.phase_ln1(s)
            ffn_it = self.ffn_items(s)
            mam_it = self.mamba_items(s - 1) if 1 <= s <= NCH else []
            if s >= NCH - 1:
                for it in mam_it:
                    self.sect(*it)
                for it in ffn_it:
                    self.sect(*it)
            else:
                self.run_items(ffn_it, mam_it)
            if mam_it:
                got = (self.pem.n, self.acm.n, self.dvm.n)
                assert got == self._expect_m, (s, got, self._expect_m)
            if s >= 2:
                self.phase_out(s - 2)

    def phase0(self):
        dg0, dsw, acs, dv = self.dg0, self.dsw, self.acf, self.dvf

        def g_loads(g):
            for W in range(2):
                for ms in range(NSUP):
                    src = self.wg[W, :, ms * 256:(ms + 1) * 256] \
                        .rearrange("(kt kp) m -> kt kp m", kp=128)
                    dg0.inc(g.dma_start(self.w01s[W, ms], src), 16)
            for dj in range(NDJ):
                src = self.wl[:, dj * 128:(dj + 1) * 128] \
                    .rearrange("(kh hp) d -> kh hp d", hp=128)
                dg0.inc(g.dma_start(self.w2s[dj], src), 16)
            dg0.inc(g.dma_start(self.xproj16, self.xproj_w), 16)
            dg0.inc(g.dma_start(self.dtproj16, self.dtproj_w), 16)
            dg0.inc(g.dma_start(self.convw_s[:],
                                self.conv_w.squeeze(1).rearrange("(dc p) j -> p dc j", p=128)), 16)
            dg0.inc(g.dma_start(self.convb_s[:],
                                self.conv_b.rearrange("(dc p) -> p dc", p=128),
                                allow_slow_non_contiguous=True), 16)
            dg0.inc(g.dma_start(self.dtb_s[:],
                                self.dtproj_b.rearrange("(dc p) -> p dc", p=128),
                                allow_slow_non_contiguous=True), 16)
            dg0.inc(g.dma_start(self.Dp_s[:],
                                self.D_param.rearrange("(dc p) -> p dc", p=128),
                                allow_slow_non_contiguous=True), 16)
            dg0.inc(g.dma_start(self.A_perm[:], self.A_permin), 16)
            dg0.inc(g.dma_start(self.fconst_s[:], self.fconst), 16)
            dg0.inc(g.dma_start(self.id_s[:], self.ident), 16)
            dg0.inc(g.dma_start(self.ones_s[:], self.ones16), 16)
            dg0.inc(g.dma_start(self.re_s[:], self.r_even), 16)
            dg0.inc(g.dma_start(self.ro_s[:], self.r_odd), 16)

        self.sect("gpsimd", g_loads)
        self._dg0_all = dg0.n

        def s_loads(sy):
            sy.wait_ge(dg0.h, self._dg0_all)
            for kt in range(DCN):
                dsw.inc(sy.dma_start(self.xproj_s[:, kt, :],
                                     self.xproj16[:, kt * 128:(kt + 1) * 128],
                                     transpose=True), 16)
            for dc in range(DCN):
                dsw.inc(sy.dma_start(self.dtproj_s[:, dc, :],
                                     self.dtproj16[dc * 128:(dc + 1) * 128, :],
                                     transpose=True), 16)

        self.sect("sync", s_loads)
        self._projw_tok = dsw.n

        def s_negexp(sc):
            sc.wait_ge(dg0.h, self._dg0_all)
            acs.inc(sc.activation(self.A_perm[:], self.A_perm[:], AF.Exp))

        self.sect("scalar", s_negexp)
        acs_tok = acs.n

        def v_init(v):
            v.wait_ge(acs.h, acs_tok)
            v.tensor_scalar_mul(self.A_perm[:], self.A_perm[:], -1.0)
            v.memset(self.h_state[:], 0.0)
            v.memset(self.a_g[:].rearrange("p a t -> p (a t)"), 0.0)
            v.memset(self.xh2_tail[:].rearrange("p a t -> p (a t)"), 0.0)
            dv.inc(v.memset(self.xh2[:].rearrange("p a t -> p (a t)"), 0.0))

        self.sect("vector", v_init)
        self._init_dv = dv.n

    def phase_xload(self, s):
        ds, pe, dv = self.dsx, self.pef, self.dvf
        xb = self.xbuf[s % 3]
        ds_base = ds.n
        pe_base = pe.n
        dv_all = dv.n

        def s_load(sy):
            if s >= 3:
                sy.wait_ge(self.dsoo.h, self._out_done[s - 3])
            if s >= 1:
                sy.wait_ge(pe.h, pe_base)
                sy.wait_ge(dv.h, dv_all)
            ds.inc(sy.dma_start(self.xio[:],
                                self.x_in[s * TC: s * TC + 128, :]), 16)
            sy.wait_ge(pe.h, pe_base + 4)
            ds.inc(sy.dma_start(self.xio[:],
                                self.x_in[s * TC + 128: s * TC + 256, :]), 16)

        self.sect("sync", s_load)

        def t_tr(t):
            t.wait_ge(dv.h, dv_all)
            if s == 0:
                t.wait_ge(dv.h, self._init_dv)
            k = 0
            for th in range(2):
                t.wait_ge(ds.h, ds_base + 16 * (th + 1))
                for dcg in range(4):
                    k += 1
                    b = self.bk[4 + (th * 4 + dcg) % 2]
                    if k <= 2:
                        b.wait_free(t)
                    else:
                        t.wait_ge(dv.h, dv_all + (k - 2))
                    for j in range(4):
                        dc = dcg * 4 + j
                        ins = t.transpose(b.ps[:, j * 128:(j + 1) * 128],
                                          self.xio[:, dc * 128:(dc + 1) * 128],
                                          self.id_s[:])
                    pe.inc(ins)

        self.sect("tensor", t_tr)

        def v_evac(v):
            k = 0
            for th in range(2):
                for dcg in range(4):
                    k += 1
                    b = self.bk[4 + (th * 4 + dcg) % 2]
                    v.wait_ge(pe.h, pe_base + k)
                    ins = v.tensor_copy(
                        xb[:, dcg * 4:(dcg + 1) * 4, th * 128:(th + 1) * 128],
                        b.ps[:].rearrange("p (a t) -> p a t", a=4))
                    dv.inc(ins)
                    b.freed_by(dv)

        self.sect("vector", v_evac)

    def _ln_stats_items(self, xb, fam, banks, pe_b, acs_b, dv_b, x_tok=None,
                        with_m2=False):
        pe, acs, dv = fam
        bm, bq = banks
        sc_ = self.lnscr["F" if fam is self.F else "M"]
        items = []
        for g in range(NG):
            def s_sq(sc, g=g):
                if g == 0 and x_tok is not None:
                    sem, n = x_tok
                    sc.wait_ge(sem.h, n)
                if g >= 1:
                    sc.wait_ge(pe.h, pe_b + g)
                    sc.wait_ge(dv.h, dv_b + g)
                xv = xb[:, g, :]
                sc.activation(sc_["q"][:], xv, AF.Square)
                acs.inc(sc.activation(sc_["qh"][:], sc_["q"][:], AF.Copy))
            items.append(("scalar", s_sq))

            def v_hl(v, g=g):
                if g >= 1:
                    v.wait_ge(pe.h, pe_b + g)
                xv = xb[:, g, :]
                v.wait_ge(acs.h, acs_b + g + 1)
                v.scalar_tensor_tensor(sc_["ql"][:], sc_["qh"][:], -1.0, sc_["q"][:],
                                       ALU.mult, ALU.add)
                v.tensor_copy(sc_["xh"][:], xv)
                dv.inc(v.scalar_tensor_tensor(sc_["xl"][:], sc_["xh"][:], -1.0, xv,
                                              ALU.mult, ALU.add))
            items.append(("vector", v_hl))

            def t_mm(t, g=g):
                if g == 0:
                    bm.wait_free(t)
                    bq.wait_free(t)
                t.wait_ge(dv.h, dv_b + g + 1)
                t.matmul(bm.ps[:, 0:256], self.ones_s[:], sc_["xh"][:],
                         start=(g == 0), stop=False)
                t.matmul(bm.ps[:, 0:256], self.ones_s[:], sc_["xl"][:],
                         start=False, stop=(g == NG - 1))
                t.matmul(bq.ps[:, 0:256], self.ones_s[:], sc_["qh"][:],
                         start=(g == 0), stop=False)
                ins = t.matmul(bq.ps[:, 0:256], self.ones_s[:], sc_["ql"][:],
                               start=False, stop=(g == NG - 1))
                pe.inc(ins)
            items.append(("tensor", t_mm))

        def s_stats(sc):
            sc.wait_ge(pe.h, pe_b + NG)
            sc.activation(sc_["m"][:], bm.ps[:, 0:256], AF.Copy, scale=INVD)
            acs.inc(sc.activation(sc_["msq"][:], sc_["m"][:], AF.Square))
        items.append(("scalar", s_stats))

        def v_var(v):
            v.wait_ge(acs.h, acs_b + NG + 1)
            dv.inc(v.scalar_tensor_tensor(sc_["var"][:], bq.ps[:, 0:256], INVD,
                                          sc_["msq"][:], ALU.mult, ALU.subtract))
        items.append(("vector", v_var))

        def s_r(sc):
            sc.wait_ge(dv.h, dv_b + NG + 1)
            sc.activation(sc_["r"][:], sc_["var"][:], AF.Ln, bias=EPS)
            ins = sc.activation(sc_["r"][:], sc_["r"][:], AF.Exp, scale=-0.5)
            if with_m2:
                sc.activation(self.m2_t[:], sc_["m"][:], AF.Copy)
                ins = sc.activation(self.r2_t[:], sc_["r"][:], AF.Copy)
            acs.inc(ins)
            bm.freed_by(acs)
            bq.freed_by(acs)
        items.append(("scalar", s_r))
        return items

    def _ln_apply_items(self, xb, dst, fam, r_tok, out_slice=None, pre_wait=None,
                        out_dv=None):
        pe, acs, dv = fam
        sc_ = self.lnscr["F" if fam is self.F else "M"]

        def v_xq(v):
            v.wait_ge(acs.h, r_tok)
            if pre_wait is not None:
                sem, n = pre_wait
                v.wait_ge(sem.h, n)
            ins = None
            for g in range(NG):
                xv = xb[:, g, :]
                v.tensor_tensor(sc_["xm"][:], xv, sc_["m"][:], ALU.subtract)
                if out_slice is None:
                    o = dst[:, g, :]
                else:
                    o = dst[:, g, out_slice]
                ins = v.tensor_tensor(o, sc_["xm"][:], sc_["r"][:], ALU.mult)
            dv.inc(ins)
        return [("vector", v_xq)]

    def phase_ln1(self, s):
        pe, acs, dv = self.F
        xb = self.xbuf[s % 3]
        items = self._ln_stats_items(xb, self.F, (self.bk[4], self.bk[5]),
                                     pe_b=pe.n, acs_b=acs.n, dv_b=dv.n,
                                     x_tok=(dv, dv.n))
        r_tok = acs.n + NG + 2
        pw = (self.pef, self._w01_done.get(s - 1, 0))
        items += self._ln_apply_items(xb, self.x1q, self.F, r_tok, pre_wait=pw)
        x1q_tok = dv.n + (NG + 1) + 1
        for it in items:
            self.sect(*it)
        assert dv.n == x1q_tok, (dv.n, x1q_tok)
        self._x1q_tok = x1q_tok

    def ffn_items(self, s):
        pe, acs, dv = self.F
        dgw = [self.dgwA, self.dgwB]
        dgw2 = self.dgw2
        calls = []
        if s < NCH:
            calls.append((0, self.x1q, self.act[0], s, ("F", self._x1q_tok)))
        if s >= 2:
            calls.append((1, self.x3q[s % 2], self.act[1], s - 2,
                          ("M", self._x3q_tok[s - 2])))
        ncal = len(calls)
        if ncal == 0:
            return []

        dgw_base = [dgw[0].n, dgw[1].n]
        dgw2_base = dgw2.n
        pe_base = pe.n
        acs_base = acs.n
        dv_base = dv.n

        def gidx(i, ci):
            return i * ncal + ci

        G = NSUP * ncal
        items = []

        def g_w(g):
            g.wait_ge(pe.h, pe_base)
            if s == 0:
                g.wait_ge(self.dg0.h, self._dg0_all)
            for i in range(NSUP):
                if i >= 2:
                    g.wait_ge(pe.h, pe_base + (i - 1) * ncal)
                for W in range(2):
                    dgw[i % 2].inc(g.dma_start(
                        self.w01b[:, i % 2, W],
                        self.w01s[W, i].rearrange("kt kp m -> kp kt m")), 16)
            for dj in range(NDJ):
                for hh in range(2):
                    u = 2 * dj + hh
                    if u >= 2:
                        g.wait_ge(pe.h, pe_base + G + u - 1)
                    dgw2.inc(g.dma_start(
                        self.w2b[u % 2][:],
                        self.w2s[dj, hh * 32:(hh + 1) * 32]
                        .rearrange("kh hp d -> hp kh d")), 16)
        items.append(("gpsimd", g_w))

        entry_waits = []
        for c, _, _, _, _ in calls:
            for mt in range(2):
                b = self.bk[c * 2 + mt]
                if b.free_tok is not None:
                    entry_waits.append(b.free_tok)
                    b.free_tok = None

        for i in range(NSUP):
            for ci, (c, xq, actb, cs, rhs_tok) in enumerate(calls):
                g = gidx(i, ci)

                def t_mm(t, i=i, ci=ci, c=c, xq=xq, g=g, rhs_tok=rhs_tok):
                    if i == 0 and ci == 0:
                        for sem, n in entry_waits:
                            t.wait_ge(sem.h, n)
                    if i == 0:
                        fam, n = rhs_tok
                        t.wait_ge((self.dvf if fam == "F" else self.dvm).h, n)
                    t.wait_ge(dgw[i % 2].h, dgw_base[i % 2] + 32 * (i // 2 + 1))
                    if i >= 1:
                        t.wait_ge(acs.h, acs_base + 2 * gidx(i - 1, ci) + 1)
                    ins = None
                    for mt in range(2):
                        b = self.bk[c * 2 + mt]
                        for kt in range(DCN):
                            t.matmul(b.ps[:, 0:256],
                                     self.w01b[:, i % 2, 0, kt, mt * 128:(mt + 1) * 128],
                                     xq[:, kt, :],
                                     start=(kt == 0), stop=False)
                        for kt in range(DCN):
                            ins = t.matmul(b.ps[:, 256:512],
                                           self.w01b[:, i % 2, 1, kt, mt * 128:(mt + 1) * 128],
                                           xq[:, kt, :],
                                           start=False, stop=(kt == DCN - 1))
                    pe.inc(ins)
                items.append(("tensor", t_mm))

                def s_evx2(sc, ci=ci, c=c, g=g):
                    sc.wait_ge(pe.h, pe_base + g + 1)
                    if g >= 1:
                        sc.wait_ge(dv.h, dv_base + 2 * (g - 1) + 2)
                    for mt in range(2):
                        b = self.bk[c * 2 + mt]
                        sc.activation(self.gf16[ci][:, mt, :], b.ps[:], AF.Copy)
                    acs.inc(sc.activation(self.ch_x2[:],
                                          self.gf16[ci][:, :, 0:256], AF.Square))
                items.append(("scalar", s_evx2))

                def v_chA(v, ci=ci, g=g):
                    gv_ = self.gf16[ci][:, :, 0:256]
                    v.wait_ge(acs.h, acs_base + 2 * g + 1)
                    v.tensor_tensor(self.ch_x3[:], self.ch_x2[:], gv_, ALU.mult)
                    v.tensor_scalar_mul(self.ch_t[:], self.ch_x3[:], C1)
                    v.tensor_tensor(self.ch_t[:], self.ch_t[:], gv_, ALU.add)
                    dv.inc(v.tensor_scalar_mul(self.ch_u[:], self.ch_t[:], C2))
                items.append(("vector", v_chA))

                def s_tanh(sc, g=g):
                    sc.wait_ge(dv.h, dv_base + 2 * g + 1)
                    acs.inc(sc.activation(self.ch_t[:], self.ch_u[:], AF.Tanh))
                items.append(("scalar", s_tanh))

                def v_chB(v, i=i, ci=ci, g=g, actb=actb):
                    gv_ = self.gf16[ci][:, :, 0:256]
                    fv_ = self.gf16[ci][:, :, 256:512]
                    if i == 0:
                        v.wait_ge(pe.h, pe_base)
                    v.wait_ge(acs.h, acs_base + 2 * g + 2)
                    v.tensor_scalar(self.ch_cdf[:], self.ch_t[:], 1.0, 0.5,
                                    ALU.add, ALU.mult)
                    v.tensor_tensor(self.ch_gv[:], self.ch_cdf[:], gv_, ALU.mult)
                    v.tensor_tensor(self.ch_a16[:], self.ch_gv[:], fv_, ALU.mult)
                    dv.inc(v.tensor_copy(actb[:, 2 * i:2 * i + 2, :], self.ch_a16[:]))
                items.append(("vector", v_chB))

        chain_done = dv_base + 2 * G
        self._w01_done[s] = pe_base + G
        for c, _, _, _, _ in calls:
            for mt in range(2):
                self.bk[c * 2 + mt].free_tok = (acs, acs_base + 2 * (G - 1) + 1)

        pe2_base = pe_base + G
        acs2_base = acs_base + 2 * G
        w1_evac_done = acs_base + 2 * (G - 1) + 1
        hh0_done = dv_base + 2 * (NSUP // 2) * ncal

        for dj in range(NDJ):
            def t_w2(t, dj=dj):
                if dj <= 1:
                    t.wait_ge(acs.h, w1_evac_done)
                    if dj == 0:
                        t.wait_ge(dv.h, hh0_done)
                if dj >= 2:
                    t.wait_ge(acs.h, acs2_base + (dj - 1) * ncal)
                for hh in range(2):
                    u = 2 * dj + hh
                    t.wait_ge(dgw2.h, dgw2_base + 16 * (u + 1))
                    if dj == 0 and hh == 1:
                        t.wait_ge(dv.h, chain_done)
                    ins = None
                    for ci, (c, xq, actb, cs, _) in enumerate(calls):
                        b = self.bk[c * 2 + dj % 2]
                        for kh in range(32):
                            ins = t.matmul(b.ps[:, 0:256],
                                           self.w2b[u % 2][:, kh, :],
                                           actb[:, hh * 32 + kh, :],
                                           start=(hh == 0 and kh == 0),
                                           stop=(hh == 1 and kh == 31))
                    pe.inc(ins)
            items.append(("tensor", t_w2))

            for ci in range(ncal):
                def s_w2(sc, dj=dj, ci=ci):
                    sc.wait_ge(pe.h, pe2_base + 2 * dj + 2)
                    if dj >= 2:
                        sc.wait_ge(dv.h, chain_done + (dj - 2) * ncal + ci + 1)
                    c = calls[ci][0]
                    acs.inc(sc.activation(self.out16[2 * (dj % 2) + ci][:],
                                          self.bk[c * 2 + dj % 2].ps[:, 0:256],
                                          AF.Copy))
                items.append(("scalar", s_w2))

                def v_w2(v, dj=dj, ci=ci):
                    c, xq, actb, cs, _ = calls[ci]
                    v.wait_ge(acs.h, acs2_base + dj * ncal + ci + 1)
                    xbv = self.xbuf[cs % 3]
                    dv.inc(v.scalar_tensor_tensor(xbv[:, dj, :],
                                                  self.out16[2 * (dj % 2) + ci][:], 0.5,
                                                  xbv[:, dj, :], ALU.mult, ALU.add))
                items.append(("vector", v_w2))

        for ci, (c, xq, actb, cs, _) in enumerate(calls):
            for p in range(2):
                last_dj = NDJ - 2 + p
                self.bk[c * 2 + p].free_tok = (acs, acs2_base + last_dj * ncal + ci + 1)

        self._ffn_pe_end[s] = pe2_base + 2 * NDJ
        end_dv = chain_done + NDJ * ncal
        for ci, (c, xq, actb, cs, _) in enumerate(calls):
            self._x1done[cs] = end_dv
        if s >= 2:
            self._x2done = end_dv
        return items

    def phase_out(self, c):
        pe, dv, ds = self.pef, self.dvf, self.dsoo
        xb = self.xbuf[c % 3]
        done_tok = self._x2done
        pe_base = pe.n
        dv_base = dv.n
        ds_base = ds.n

        def t_tr(t):
            t.wait_ge(dv.h, done_tok)
            k = 0
            for th in range(2):
                for dcg in range(4):
                    k += 1
                    b = self.bk[4 + (th * 4 + dcg) % 2]
                    if k <= 2:
                        b.wait_free(t)
                    else:
                        t.wait_ge(dv.h, dv_base + (k - 2))
                    for j in range(4):
                        dc = dcg * 4 + j
                        ins = t.transpose(b.ps[:, j * 128:(j + 1) * 128],
                                          xb[:, dc, th * 128:(th + 1) * 128],
                                          self.id_s[:])
                    pe.inc(ins)

        self.sect("tensor", t_tr)

        def v_evac(v):
            k = 0
            for th in range(2):
                for dcg in range(4):
                    k += 1
                    if k == 5:
                        v.wait_ge(ds.h, ds_base + 16)
                    b = self.bk[4 + (th * 4 + dcg) % 2]
                    v.wait_ge(pe.h, pe_base + k)
                    ins = v.tensor_copy(
                        self.xio[:, dcg * 512:(dcg + 1) * 512]
                        .rearrange("p (a t) -> p a t", a=4),
                        b.ps[:].rearrange("p (a t) -> p a t", a=4))
                    dv.inc(ins)
                    b.freed_by(dv)

        self.sect("vector", v_evac)

        def s_out(sy):
            for th in range(2):
                sy.wait_ge(dv.h, dv_base + 4 * (th + 1))
                ds.inc(sy.dma_start(
                    self.out[c * TC + th * 128: c * TC + (th + 1) * 128, :],
                    self.xio[:]), 16)

        self.sect("sync", s_out)
        self._out_done[c] = ds.n

    def mamba_items(self, m):
        pe, acs, dv = self.M
        xb = self.xbuf[m % 3]
        items = []

        pe0, ac0, dv0 = pe.n, acs.n, dv.n

        x_tok = (self.dvf, self._x1done[m])
        items += self._ln_stats_items(xb, self.M, (self.bk[6], self.bk[7]),
                                      pe_b=pe0, acs_b=ac0, dv_b=dv0,
                                      x_tok=x_tok, with_m2=True)
        pe1 = pe0 + NG
        ac1 = ac0 + NG + 2
        dv1 = dv0 + NG + 1
        r_tok = ac1

        prev_pe4 = getattr(self, "_prev_pe4", None)

        def v_halo(v):
            v.wait_ge(acs.h, r_tok)
            if prev_pe4 is not None:
                v.wait_ge(pe.h, prev_pe4)
            dv.inc(v.tensor_copy(self.xh2[:, :, 1:4], self.xh2_tail[:]))
        items.append(("vector", v_halo))
        dv2 = dv1 + 1

        items += self._ln_apply_items(xb, self.xh2, self.M, r_tok,
                                      out_slice=slice(4, 4 + TC))
        xh2_tok = dv2 + 1

        def v_tail(v):
            v.tensor_copy(self.xh2_tail[:], self.xh2[:, :, TC + 1:TC + 4])
        items.append(("vector", v_tail))

        acs_cv = ac1
        for dc in range(DCN):
            def v_taps(v, dc=dc):
                if dc == 0:
                    v.wait_ge(dv.h, xh2_tok)
                else:
                    v.wait_ge(acs.h, acs_cv + dc)
                v.tensor_scalar_mul(self.xcb[:], self.xh2[:, dc, 1:1 + TC],
                                    self.convw_s[:, dc, 0:1])
                for j in range(1, 4):
                    v.scalar_tensor_tensor(self.xcb[:],
                                           self.xh2[:, dc, 1 + j:1 + j + TC],
                                           self.convw_s[:, dc, j:j + 1],
                                           self.xcb[:], ALU.mult, ALU.add)
                dv.inc(v.tensor_scalar_add(self.xcb[:], self.xcb[:],
                                           self.convb_s[:, dc:dc + 1]))
            items.append(("vector", v_taps))

            def s_tanh(sc, dc=dc):
                sc.wait_ge(dv.h, xh2_tok + 2 * dc + 1)
                acs.inc(sc.activation(self.th_s[:], self.xcb[:], AF.Tanh, scale=0.5))
            items.append(("scalar", s_tanh))

            def v_silu(v, dc=dc):
                v.wait_ge(acs.h, acs_cv + dc + 1)
                v.scalar_tensor_tensor(self.u_s[:], self.th_s[:], 1.0, self.xcb[:],
                                       ALU.add, ALU.mult)
                dv.inc(v.tensor_scalar_mul(self.xconv[:, dc, :], self.u_s[:], 0.5))
            items.append(("vector", v_silu))

        dv3 = xh2_tok + 32
        ac2 = acs_cv + 16

        bdt, bbc = self.bk[6], self.bk[7]
        pe_pj = pe1

        def t_proj(t):
            t.wait_ge(dv.h, dv3)
            if m == 0:
                t.wait_ge(self.dsw.h, self._projw_tok)
            bdt.wait_free(t)
            bbc.wait_free(t)
            ins = None
            for kt in range(DCN):
                t.matmul(bdt.ps[:, 0:256], self.xproj_s[:, kt, 0:128], self.xconv[:, kt, :],
                         start=(kt == 0), stop=(kt == DCN - 1))
                ins = t.matmul(bbc.ps[0:32, 0:256], self.xproj_s[:, kt, 128:160],
                               self.xconv[:, kt, :],
                               start=(kt == 0), stop=(kt == DCN - 1))
            pe.inc(ins)
        items.append(("tensor", t_proj))

        def v_pj(v):
            v.wait_ge(pe.h, pe_pj + 1)
            v.tensor_copy(self.dt16[:], bdt.ps[:, 0:256])
            dv.inc(v.tensor_copy(self.bc16[0:32, :], bbc.ps[0:32, 0:256]))
            bdt.freed_by(dv)
            bbc.freed_by(dv)
        items.append(("vector", v_pj))
        dv_pj = dv3 + 1

        pe_d = pe_pj + 1
        acs_d = ac2
        dv_d = dv_pj
        for dc in range(DCN):
            def t_delta(t, dc=dc):
                b = self.bk[6 + dc % 2]
                if dc < 2:
                    t.wait_ge(dv.h, dv_pj)
                else:
                    t.wait_ge(dv.h, dv_d + 3 * (dc - 2) + 2)
                pe.inc(t.matmul(b.ps[:, 0:256], self.dtproj_s[:, dc, :], self.dt16[:],
                                start=True, stop=True))
            items.append(("tensor", t_delta))

            def s_z(sc, dc=dc):
                b = self.bk[6 + dc % 2]
                sc.wait_ge(pe.h, pe_d + dc + 1)
                if dc >= 1:
                    sc.wait_ge(dv.h, dv_d + 3 * (dc - 1) + 2)
                acs.inc(sc.activation(self.z_s[:], b.ps[:, 0:256], AF.Identity,
                                      bias=self.dtb_s[:, dc:dc + 1]))
            items.append(("scalar", s_z))

            def v_az(v, dc=dc):
                v.wait_ge(acs.h, acs_d + 2 * dc + 1)
                dv.inc(v.scalar_tensor_tensor(self.az_s[:], self.z_s[:], -1.0, self.z_s[:],
                                              ALU.mult, ALU.max))
            items.append(("vector", v_az))

            def s_expln(sc, dc=dc):
                sc.wait_ge(dv.h, dv_d + 3 * dc + 1)
                sc.activation(self.e_s[:], self.az_s[:], AF.Exp, scale=-1.0)
                acs.inc(sc.activation(self.e_s[:], self.e_s[:], AF.Ln, bias=1.0))
            items.append(("scalar", s_expln))

            def v_dl(v, dc=dc):
                v.wait_ge(acs.h, acs_d + 2 * dc + 2)
                dv.inc(v.scalar_tensor_tensor(self.delta[:, dc, :], self.z_s[:], 0.0,
                                              self.e_s[:], ALU.max, ALU.add))
                dv.inc(v.tensor_tensor(self.wtd[:, dc, :], self.delta[:, dc, :],
                                       self.xconv[:, dc, :], ALU.mult))
            items.append(("vector", v_dl))

        delta_tok = dv_d + 3 * DCN
        ac3 = acs_d + 2 * DCN
        pe3 = pe_d + DCN
        tok_y = [(dv, dv_d + 3 * (DCN - 2) + 2), (dv, dv_d + 3 * (DCN - 1) + 2)]

        dsbc = self.dsbc
        bc_base = dsbc.n

        def s_bc(sy):
            sy.wait_ge(dv.h, delta_tok)
            sy.wait_ge(pe.h, pe_pj + 1)
            for np_ in range(NP):
                for n2 in range(2):
                    dsbc.inc(sy.dma_start(
                        self.Bbc[n2 * 64:(n2 + 1) * 64, np_, :],
                        self.bc16[2 * np_ + n2:2 * np_ + n2 + 1, :].unsqueeze(1)
                        .broadcast_to([1, 64, TC])), 16)
                    dsbc.inc(sy.dma_start(
                        self.Cbc[n2 * 64:(n2 + 1) * 64, np_, :],
                        self.bc16[16 + 2 * np_ + n2:16 + 2 * np_ + n2 + 1, :].unsqueeze(1)
                        .broadcast_to([1, 64, TC])), 16)
        items.append(("sync", s_bc))
        bc_tok = bc_base + 16 * 32

        dsp = [self.dse, self.dso]
        dsp_base = [self.dse.n, self.dso.n]
        acs_sc = ac3
        scan_dv = delta_tok
        pe_y = pe3

        def sdv(nscan, ny):
            return scan_dv + 5 * nscan + ny

        def after_scan(dblk):
            return sdv(dblk + 1, dblk // 2)

        def after_y(dc):
            return sdv(2 * dc + 2, dc + 1)

        def s_ddn(sy):
            for dblk in range(NDBLK):
                par = dblk % 2
                if dblk >= 2:
                    sy.wait_ge(dv.h, after_scan(dblk - 2))
                    sy.wait_ge(acs.h, acs_sc + (dblk - 1))
                p0 = 64 * (dblk % 2)
                dc = dblk // 2
                for n2 in range(2):
                    dsp[par].inc(sy.dma_start(
                        self.ddn[par][n2 * 64:(n2 + 1) * 64, :],
                        self.delta[p0:p0 + 64, dc, :]), 16)
                    dsp[par].inc(sy.dma_start(
                        self.wdn[par][n2 * 64:(n2 + 1) * 64, :],
                        self.wtd[p0:p0 + 64, dc, :]), 16)
        items.append(("sync", s_ddn))

        for dblk in range(NDBLK):
            par = dblk % 2

            def s_exp(sc, dblk=dblk, par=par):
                sc.wait_ge(dsp[par].h, dsp_base[par] + 64 * (dblk // 2 + 1))
                if dblk >= 1:
                    sc.wait_ge(dv.h, after_scan(dblk - 1))
                ins = None
                for np_ in range(NP):
                    ins = sc.activation(self.a_g[:, np_, 1:SEPW], self.ddn[par][:],
                                        AF.Exp,
                                        scale=self.A_perm[:, dblk * 8 + np_:dblk * 8 + np_ + 1])
                acs.inc(ins)
            items.append(("scalar", s_exp))

            def v_scan(v, dblk=dblk, par=par):
                if dblk == 0:
                    v.wait_ge(dsbc.h, bc_tok)
                v.wait_ge(acs.h, acs_sc + dblk + 1)
                if dblk >= 2:
                    v.wait_ge(pe.h, pe_y + dblk - 1)
                v.tensor_copy(self.b_g[:, :, 0:1],
                              self.h_state[:, dblk * 8:dblk * 8 + 8].unsqueeze(2))
                v.tensor_tensor(self.b_g[:, :, 1:SEPW],
                                self.wdn[par][:].unsqueeze(1).broadcast_to([128, NP, TC]),
                                self.Bbc[:], ALU.mult)
                v.tensor_tensor_scan(self.h_g[:].rearrange("p a t -> p (a t)"),
                                     self.a_g[:].rearrange("p a t -> p (a t)"),
                                     self.b_g[:].rearrange("p a t -> p (a t)"),
                                     0.0, ALU.mult, ALU.add)
                v.tensor_copy(self.h_state[:, dblk * 8:dblk * 8 + 8],
                              self.h_g[:, :, SEPW - 1:SEPW].squeeze(2))
                ins = v.tensor_tensor(self.y_g[dblk % 2][:, :, 1:SEPW],
                                      self.h_g[:, :, 1:SEPW],
                                      self.Cbc[:], ALU.mult)
                dv.inc(ins, 5)
            items.append(("vector", v_scan))

            def t_red(t, dblk=dblk):
                t.wait_ge(dv.h, after_scan(dblk))
                dc = dblk // 2
                b = self.bk[6 + dc % 2]
                if dblk % 2 == 0:
                    if dc < 2:
                        if tok_y[dc % 2] is not None:
                            sem, n = tok_y[dc % 2]
                            t.wait_ge(sem.h, n)
                            tok_y[dc % 2] = None
                    else:
                        t.wait_ge(dv.h, after_y(dc - 2))
                R = self.re_s if dblk % 2 == 0 else self.ro_s
                ins = None
                for np_ in range(NP):
                    ins = t.matmul(b.ps[:, 0:256], R[:],
                                   self.y_g[dblk % 2][:, np_, 1:SEPW],
                                   start=(dblk % 2 == 0 and np_ == 0),
                                   stop=(dblk % 2 == 1 and np_ == NP - 1))
                pe.inc(ins)
            items.append(("tensor", t_red))

            if dblk % 2 == 1:
                def v_y(v, dblk=dblk):
                    dc = dblk // 2
                    b = self.bk[6 + dc % 2]
                    v.wait_ge(pe.h, pe_y + 2 * dc + 2)
                    v.tensor_tensor(self.ytmp[:], xb[:, dc, :], self.m2_t[:],
                                    ALU.subtract)
                    v.scalar_tensor_tensor(self.ytmp2[:], self.ytmp[:],
                                           self.Dp_s[:, dc:dc + 1], self.r2_t[:],
                                           ALU.mult, ALU.mult)
                    v.tensor_tensor(self.ytmp[:], self.ytmp2[:], b.ps[:, 0:256],
                                    ALU.add)
                    dv.inc(v.tensor_tensor(xb[:, dc, :], xb[:, dc, :], self.ytmp[:],
                                           ALU.add))
                    b.freed_by(dv)
                items.append(("vector", v_y))

        dv4 = scan_dv + 5 * NDBLK + DCN
        pe4 = pe_y + NDBLK
        self._prev_pe4 = pe4
        ac4 = acs_sc + NDBLK

        items += self._ln_stats_items(xb, self.M, (self.bk[6], self.bk[7]),
                                      pe_b=pe4, acs_b=ac4, dv_b=dv4)
        r3_tok = ac4 + NG + 2
        pwait = (self.pef, self._ffn_pe_end.get(m, 0))
        items += self._ln_apply_items(xb, self.x3q[m % 2], self.M, r3_tok,
                                      pre_wait=pwait)
        self._x3q_tok[m] = dv4 + NG + 1 + 1
        self._expect_m = (pe4 + NG, ac4 + NG + 2, dv4 + NG + 1 + 1)
        return items

    def finalize(self):
        sems = [self.dg0, self.dgwA, self.dgwB, self.dgw2, self.dsbc,
                self.dse, self.dso, self.dsx, self.dsoo, self.dsw]
        tgts = [(s.h, s.n) for s in sems]

        def s_fin(sy):
            for h, n in tgts:
                sy.wait_ge(h, n)

        self.sect("sync", s_fin)


def build_program():
    b = Builder()
    with b.es:
        return b.build()


_NC_CACHE = None


def kernel(L0_attn_x, L0_attn_conv_w, L0_attn_conv_b, L0_attn_xproj_w,
           L0_attn_dtproj_w, L0_attn_dtproj_b, L0_attn_A_log, L0_attn_D_param,
           L0_ffn_w_gating, L0_ffn_w_linear):
    global _NC_CACHE
    X = np.ascontiguousarray(np.asarray(L0_attn_x, np.float32))
    consts = {
        "conv_w": np.ascontiguousarray(np.asarray(L0_attn_conv_w, np.float32)),
        "conv_b": np.ascontiguousarray(np.asarray(L0_attn_conv_b, np.float32)),
        "xproj_w": np.ascontiguousarray(np.asarray(L0_attn_xproj_w, np.float32)),
        "dtproj_w": np.ascontiguousarray(np.asarray(L0_attn_dtproj_w, np.float32)),
        "dtproj_b": np.ascontiguousarray(np.asarray(L0_attn_dtproj_b, np.float32)),
        "A_perm_log": np.ascontiguousarray(
            np.asarray(L0_attn_A_log, np.float32)
            .reshape(32, 64, 8, 2).transpose(3, 1, 0, 2).reshape(128, 256)),
        "D_param": np.ascontiguousarray(np.asarray(L0_attn_D_param, np.float32)),
        "w_gating": np.ascontiguousarray(np.asarray(L0_ffn_w_gating, np.float32)),
        "w_linear": np.ascontiguousarray(np.asarray(L0_ffn_w_linear, np.float32)),
        "ident": np.eye(128, dtype=np.float32),
        "fconst": np.tile(np.array([[EPS, 1.0, -0.5, -1.0, 0.5, INVD, 0.0, 0.0]],
                                   np.float32), (128, 1)),
        "ones16": np.ones((128, 128), dtype=ml_dtypes.bfloat16),
    }
    re_ = np.zeros((128, 128), np.float32)
    ro_ = np.zeros((128, 128), np.float32)
    for p in range(128):
        re_[p, p % 64] = 1.0
        ro_[p, 64 + (p % 64)] = 1.0
    consts["r_even"] = re_.astype(ml_dtypes.bfloat16)
    consts["r_odd"] = ro_.astype(ml_dtypes.bfloat16)

    if _NC_CACHE is None:
        _NC_CACHE = build_program()
    nc = _NC_CACHE
    in_maps = [dict(consts, x=np.ascontiguousarray(X[b])) for b in range(8)]
    res = run_bass_kernel_spmd(nc, in_maps, core_ids=list(range(8)))
    return np.stack([np.asarray(res.results[b]["out"], np.float32) for b in range(8)],
                    axis=0)
